# version 17
# baseline (speedup 1.0000x reference)
import numpy as np
import concourse.bass as bass
import concourse.mybir as mybir
from concourse.bass_utils import run_bass_kernel_spmd

F32 = mybir.dt.float32
F16 = mybir.dt.float16
I8 = mybir.dt.int8
AF = mybir.ActivationFunctionType
ALU = mybir.AluOpType
AX = mybir.AxisListType

D_MODEL = 1024
D_INNER = 2048
D_STATE = 16
NHEADS = 32
HEADDIM = 64
CONV_DIM = 2080
EPROJ = 4224
NECH_A = 17
NECH_Z = 16
BIG = 6.0e4
QSHIFT = 1536.0


ENGINES = ("sync", "tensor", "vector", "scalar", "gpsimd")


class Sched:

    def __init__(self, nc):
        self.nc = nc
        self.ops = {e: [] for e in ENGINES}
        self.sems = {}
        self.cnt = {}

    def _sem(self, name):
        if name not in self.sems:
            self.sems[name] = self.nc.alloc_semaphore(name=name)
            self.cnt[name] = 0
        return self.sems[name]

    def op(self, engine, fn, waits=(), inc=None, n=1):
        tok = None
        if inc is not None:
            self._sem(inc)
            self.cnt[inc] += n
            tok = (inc, self.cnt[inc])
        for w in waits:
            if w is not None:
                self._sem(w[0])
        self.ops[engine].append((tuple(w for w in waits if w is not None),
                                 fn, tok, n))
        return tok

    def dma(self, engine, out, in_, waits=(), inc="dma"):
        return self.op(engine,
                       lambda e, o=out, i=in_: e.dma_start(out=o, in_=i),
                       waits=waits, inc=inc, n=16)

    def emit(self):
        with self.nc.Block() as block:
            def mk(engine):
                ops = self.ops[engine]
                def body(eng):
                    for waits, fn, tok, n in ops:
                        for s, v in waits:
                            eng.wait_ge(self.sems[s], v)
                        instr = fn(eng)
                        if tok is not None:
                            if instr is None:
                                raise RuntimeError("op with inc returned None")
                            instr.then_inc(self.sems[tok[0]], n)
                return body
            for e in ENGINES:
                if self.ops[e]:
                    getattr(block, e)(mk(e))

    def barrier(self, name, engines=ENGINES):
        toks = []
        for e in engines:
            toks.append(self.op(e, lambda eng: eng.nop(), inc="bar_" + name))
        for e in engines:
            self.op(e, lambda eng: eng.nop(), waits=toks)


def _r(t, *idx):
    return t[idx]


def build_consts():
    c = {}
    c["ident"] = np.eye(128, dtype=np.float32)
    sel = np.zeros((2, 128), np.float32)
    sel[0, :64] = 1.0
    sel[1, 64:] = 1.0
    c["dseg_sel"] = sel
    oh = np.zeros((63, 128), np.float32)
    for j in range(1, 64):
        oh[j - 1, j] = 1.0
        oh[j - 1, 64 + j] = 1.0
    c["dseg_mask_lhs"] = oh
    rhs_oh = np.zeros((32, 2048), np.float32)
    for h in range(32):
        rhs_oh[h, h * 64:(h + 1) * 64] = 1.0
    rhs_mask = np.zeros((63, 2048), np.float32)
    for j in range(1, 64):
        row = np.zeros(64, np.float32)
        row[:j] = -BIG
        rhs_mask[j - 1] = np.tile(row, 32)
    c["dseg_rhs_const"] = np.concatenate([rhs_oh, rhs_mask], 0)
    c["ones32_c"] = np.ones((32, 64), np.float32)
    return c


def _pad_conv(a):
    out = np.zeros((2176,) + a.shape[1:], np.float32)
    out[0:2048] = a[0:2048]
    out[2048:2064] = a[2048:2064]
    out[2080:2096] = a[2064:2080]
    return out


def host_prepare(inputs):
    x = np.ascontiguousarray(inputs["L0_attn_x"])
    W_in = np.asarray(inputs["L0_attn_W_in"])
    conv_w = np.asarray(inputs["L0_attn_conv_w"])
    conv_b = np.asarray(inputs["L0_attn_conv_b"])
    W_out = np.asarray(inputs["L0_attn_W_out"])
    w8 = np.asarray(inputs["L0_ffn_weight_int8"])

    W_inT = np.zeros((D_MODEL, EPROJ), np.float32)
    W_inT[:, 0:2048] = W_in[D_INNER:D_INNER + 2048].T
    W_inT[:, 2048:2064] = W_in[D_INNER + 2048:D_INNER + 2064].T
    W_inT[:, 2080:2096] = W_in[D_INNER + 2064:D_INNER + 2080].T
    W_inT[:, 2112:2144] = W_in[D_INNER + CONV_DIM:].T
    W_inT[:, 2176:2176 + D_INNER] = W_in[:D_INNER].T

    shard_common = {
        "w_inT": W_inT,
        "w_outT": np.ascontiguousarray(W_out.T),
        "w_ffnT": np.ascontiguousarray(w8.T),
        "conv_w": _pad_conv(conv_w),
        "conv_b": _pad_conv(conv_b.reshape(-1, 1))[:, 0],
        "dt_bias": np.asarray(inputs["L0_attn_dt_bias"], np.float32),
        "A_log": np.asarray(inputs["L0_attn_A_log"], np.float32),
        "D_vec": np.asarray(inputs["L0_attn_D"], np.float32),
        "norm_w": np.asarray(inputs["L0_attn_norm_w"], np.float32),
        "w_scale": np.asarray(inputs["L0_ffn_weight_scale"], np.float32)[:, 0],
        "f_bias": np.asarray(inputs["L0_ffn_bias"], np.float32),
        **build_consts(),
    }
    return [dict(shard_common, xT=np.ascontiguousarray(x[b].T))
            for b in range(x.shape[0])]


def build_program(T=8192, TP=512, stages=99, debug=False, n_cores=8):
    assert T % TP == 0 and TP % 128 == 0
    NT = T // TP
    NPAIR = TP // 128
    nc = bass.Bass("TRN2", target_bir_lowering=False)

    def din(name, shape, dtype=F32):
        return nc.dram_tensor(name, shape, dtype, kind="ExternalInput").ap()

    xT = din("xT", [D_MODEL, T])
    w_inT = din("w_inT", [D_MODEL, EPROJ])
    w_outT = din("w_outT", [D_INNER, D_MODEL])
    w_ffnT = din("w_ffnT", [D_MODEL, D_MODEL], I8)
    conv_w = din("conv_w", [2176, 4])
    conv_b = din("conv_b", [2176])
    dt_bias = din("dt_bias", [32])
    A_log = din("A_log", [32])
    D_vec = din("D_vec", [32])
    norm_w = din("norm_w", [D_INNER])
    w_scale = din("w_scale", [D_MODEL])
    f_bias = din("f_bias", [D_MODEL])
    ident_in = din("ident", [128, 128])
    dseg_sel = din("dseg_sel", [2, 128])
    dseg_mask_lhs = din("dseg_mask_lhs", [63, 128])
    dseg_rhs_const = din("dseg_rhs_const", [95, 2048])
    ones32_in = din("ones32_c", [32, 64])

    out = nc.dram_tensor("out", [D_MODEL, T], F32, kind="ExternalOutput").ap()
    h1_spill = nc.dram_tensor("h1_spill", [D_MODEL, T], F32).ap()
    w16_in = nc.dram_tensor("w16_in", [128, 33 * 2048], F16).ap()
    w16_out = nc.dram_tensor("w16_out", [128, 16 * 2048], F16).ap()
    cc_in = nc.dram_tensor("cc_in", [1, 1], F32).ap()
    cc_out = nc.dram_tensor("cc_out", [1, 1], F32).ap()

    dbg_names = []
    dbg = {}
    if debug:
        specs = [("dbg_xbcc", [128, NECH_A * T]), ("dbg_dt", [32, T]),
                 ("dbg_acs", [32, T]), ("dbg_ybuf", [128, 16 * T]),
                 ("dbg_h1pre", [128, 8 * T])]
        for name, shape in specs:
            dbg[name] = nc.dram_tensor(name, shape, F32,
                                       kind="ExternalOutput").ap()
            dbg_names.append(name)

    s = Sched(nc)

    sbh = {}
    def sbuf(name, shape, dtype=F32):
        sbh[name] = nc.alloc_sbuf_tensor(name, shape, dtype)
        return sbh[name]

    x16h = sbuf("x16h", [128, 8 * TP], F16)
    x16l = sbuf("x16l", [128, 8 * TP], F16)
    xstg = sbuf("xstg", [128, 2 * TP])
    wbufA = sbuf("wbufA", [128, 2048 * 2], F16)
    raw = sbuf("raw", [128, 2 * (TP + 3)])
    hist = sbuf("hist", [128, NECH_A * 3])
    xbcc = sbuf("xbcc", [128, NECH_A * TP])
    dtv = sbuf("dtv", [32, TP])
    adt = sbuf("adt", [32, TP])
    acs = sbuf("acs", [32, TP])
    acsn = sbuf("acsn", [32, TP])
    convw = sbuf("convw", [128, NECH_A * 4])
    convb = sbuf("convb", [128, NECH_A])
    dtb = sbuf("dtb", [32, 1])
    avec = sbuf("avec", [32, 1])
    ones32 = sbuf("ones32", [32, 64])
    expa_fm = sbuf("expa_fm", [32, TP])
    dt_tm = sbuf("dt_tm", [128, NPAIR * 32])
    acs_tm = sbuf("acs_tm", [128, NPAIR * 32])
    dec_tm = sbuf("dec_tm", [128, NPAIR * 32])
    alastT = sbuf("alastT", [2, NPAIR * 32])
    rep_all = sbuf("rep_all", [128, 8 * TP + 64])
    c0 = sbuf("c0", [16, TP])
    c_rep4 = sbuf("c_rep4", [128, TP])
    c_til = sbuf("c_til", [128, 8 * TP])
    _bpad = sbuf("bpad0", [128, 1024])
    _xdt = sbuf("xdt0", [128, 2048])
    _mbuf = sbuf("mbuf0", [128, 2048])
    _l97 = sbuf("l97_0", [97, 128])
    _r97 = sbuf("r97_0", [97, 2048])
    bpad, xdt, mbuf, l97, r97 = ([_bpad] * 2, [_xdt] * 2, [_mbuf] * 2,
                                 [_l97] * 2, [_r97] * 2)
    sring = [sbuf("sring0", [128, 512]), sbuf("sring1", [128, 512]),
             sbuf("sring2", [128, 512])]
    y_buf = sbuf("y_buf", [128, 16 * TP])
    wffn16 = None
    ident = sbuf("ident_sb", [128, 128])
    ident16 = sbuf("ident16", [128, 128], F16)
    y16h = None
    y16l = None
    dt_sbuf = sbuf("dt_col", [128, 16])
    dpair = sbuf("dpair", [2, 16])
    rowA = sbuf("rowA", [1, TP])
    rowB = sbuf("rowB", [1, TP])
    rowC = sbuf("rowC", [1, TP])
    rowD = sbuf("rowD", [1, TP])
    amax_col = sbuf("amax_col", [128, 1])
    red_col = sbuf("red_col", [128, 1])
    ones_col = sbuf("ones_col", [128, 1])
    ones_row = sbuf("ones_row", [1, 128])
    normw_c = sbuf("normw_c", [128, 16])
    ws_c = sbuf("ws_c", [128, 8])
    fb_c = sbuf("fb_c", [128, 8])
    xs_sb = sbuf("xs_sb", [1, 4])
    cst = sbuf("cst", [1, 4])
    xs_col = sbuf("xs_col", [128, 1])
    inv_col = sbuf("inv_col", [128, 1])
    sws_c = sbuf("sws_c", [128, 8])

    y16h = xbcc.ap().bitcast(F16)[:, 0:16 * TP]
    y16l = xbcc.ap().bitcast(F16)[:, 16 * TP:32 * TP]

    psh = {}
    def psum(name, shape, dtype=F32):
        psh[name] = nc.alloc_psum_tensor(name, shape, dtype)
        return psh[name]

    pA = [psum("pA0", [128, TP]), psum("pA1", [128, TP])]
    pT = psum("pT", [128, 512])
    pG = psum("pG", [128, 512])
    pY = [psum("pY0", [128, 512]), psum("pY1", [128, 512])]
    pSt = psum("pSt", [128, 512])
    pT2 = psum("pT2", [128, 512])

    tk_boot = []
    tk_boot.append(s.op("sync", lambda e: e.dma_start(
        out=convb[:, :], in_=conv_b.rearrange("(c p) -> p c", p=128),
        allow_slow_non_contiguous=True), inc="dboot", n=16))
    tk_boot.append(s.dma("sync", convw.ap().rearrange("p (c k) -> p c k", k=4),
                         conv_w.rearrange("(c p) k -> p c k", p=128), inc="dboot"))
    tk_boot.append(s.dma("sync", dtb[:, :], dt_bias.unsqueeze(1), inc="dboot"))
    tk_boot.append(s.dma("sync", avec[:, :], A_log.unsqueeze(1), inc="dboot"))
    tk_boot.append(s.dma("sync", ones32[:, :], ones32_in, inc="dboot"))
    tk_ident = s.dma("sync", ident[:, :], ident_in, inc="dboot")
    s.dma("sync", l97[0][0:2, :], dseg_sel, inc="dboot")
    s.dma("sync", l97[0][34:97, :], dseg_mask_lhs, inc="dboot")
    s.dma("sync", r97[0][2:97, :], dseg_rhs_const, inc="dboot")
    tk_dp = s.op("sync", lambda e: e.dma_start(
        out=dpair[:, :], in_=D_vec.rearrange("(k two) -> two k", two=2),
        allow_slow_non_contiguous=True), inc="dboot", n=16)
    s.op("sync", lambda e: e.dma_start(
        out=normw_c[:, :], in_=norm_w.rearrange("(c p) -> p c", p=128),
        allow_slow_non_contiguous=True), inc="dboot", n=16)
    s.op("sync", lambda e: e.dma_start(
        out=ws_c[:, :], in_=w_scale.rearrange("(c p) -> p c", p=128),
        allow_slow_non_contiguous=True), inc="dboot", n=16)
    s.op("sync", lambda e: e.dma_start(
        out=fb_c[:, :], in_=f_bias.rearrange("(c p) -> p c", p=128),
        allow_slow_non_contiguous=True), inc="dboot", n=16)
    tk_bootall2 = ("dboot", s.cnt["dboot"])
    t = s.op("scalar", lambda e: e.activation(avec[:, :], avec[:, :], AF.Exp,
                                              bias=0.0, scale=1.0),
             waits=[tk_bootall2], inc="se")
    t_avec = s.op("vector",
                  lambda e: e.tensor_scalar_mul(avec[:, :], avec[:, :], -1.0),
                  waits=[t], inc="ve")
    t_hist = s.op("vector", lambda e: e.memset(hist[:, :], 0.0), inc="ve")
    s.op("vector", lambda e: e.memset(xbcc[:, :], 0.0), inc="ve")
    tk_dmm = s.op("tensor", lambda e: e.matmul(
        pT[:, 0:16], l97[0][0:2, :], dpair[:, :], start=True, stop=True),
        waits=[tk_bootall2], inc="te")
    tk_dt_sb = s.op("vector", lambda e: e.tensor_copy(dt_sbuf[:, :], pT[:, 0:16]),
                    waits=[tk_dmm], inc="ve")
    for b in {id(b): b for b in bpad}.values():
        s.op("vector", lambda e, b=b: e.memset(b[:, :], 0.0), inc="ve")
    for sr in sring:
        s.op("vector", lambda e, sr=sr: e.memset(sr[:, :], 0.0), inc="ve")
    s.op("vector", lambda e: e.memset(rep_all[:, :], 0.0), inc="ve")
    s.op("vector", lambda e: e.memset(c_rep4[:, :], 0.0), inc="ve")
    s.op("vector", lambda e: e.memset(ones_col[:, :], 1.0), inc="ve")
    s.op("vector", lambda e: e.memset(ones_row[:, :], 1.0), inc="ve")
    s.op("vector", lambda e: e.memset(amax_col[:, :], 0.0), inc="ve")
    s.op("vector", lambda e: e.memset(cst[:, 0:1], 1.0e-5), inc="ve")
    s.op("vector", lambda e: e.memset(cst[:, 1:2], 1.0 / 2048.0), inc="ve")
    s.op("vector", lambda e: e.memset(cst[:, 2:3], 1.0 / 1024.0), inc="ve")
    s.op("vector", lambda e: e.memset(cst[:, 3:4], 1.0 / 127.0), inc="ve")

    s.op("vector", lambda e: e.tensor_copy(ident16[:, :], ident[:, :]),
         waits=[tk_bootall2], inc="ve")
    xprep = {"prev": None, "tok": None}
    def emit_xprep_chunk(itx, dc, xfree):
        t0x = itx * TP
        tk_xd = s.dma("sync", xstg[:, (dc % 2) * TP:(dc % 2) * TP + TP],
                      xT.rearrange("(c p) t -> p c t", p=128)
                      [:, dc, t0x:t0x + TP],
                      waits=[xprep["prev"]], inc="dx%d" % (dc % 2))
        tk_xh = s.op("vector", lambda e, dc=dc: e.tensor_copy(
            x16h[:, dc * TP:(dc + 1) * TP],
            xstg[:, (dc % 2) * TP:(dc % 2) * TP + TP]),
            waits=[tk_xd, xfree], inc="ve")
        tk_xl = s.op("vector", lambda e, dc=dc: e.tensor_tensor(
            x16l[:, dc * TP:(dc + 1) * TP],
            xstg[:, (dc % 2) * TP:(dc % 2) * TP + TP],
            x16h[:, dc * TP:(dc + 1) * TP], ALU.subtract),
            waits=[tk_xh], inc="vxs")
        xprep["prev"] = tk_xl
        xprep["tok"] = tk_xl
    for dc in range(8):
        emit_xprep_chunk(0, dc, None)
    wst = mbuf[0]
    wsp = rep_all.ap().bitcast(F16)
    wsplit_in_tok = {}
    wsplit_out_tok = {}
    _win_order = [16] + list(range(16)) + list(range(17, 33))
    _wsq = ([(1024, w_inT, w16_in, e8, wsplit_in_tok) for e8 in _win_order] +
            [(2048, w_outT, w16_out, e8, wsplit_out_tok) for e8 in range(8)])
    wsplit_state = {"n": 0, "lo": None, "st": [None, None]}

    def emit_wsplit_chunk():
        n = wsplit_state["n"]
        if n >= len(_wsq):
            return
        wid, wsrc, wdst, e8, toks = _wsq[n]
        sl = n % 2
        spo = sl * 4096
        tk_ld = s.dma("gpsimd", wst[:, 0:wid],
                      wsrc.rearrange("(c p) n -> p c n", p=128)
                      [:, :, e8 * 128:(e8 + 1) * 128],
                      waits=[wsplit_state["lo"]], inc="dwsp")
        tk_h = s.op("vector", lambda e, wid=wid, spo=spo: e.tensor_copy(
            wsp[:, spo:spo + wid], wst[:, 0:wid]),
            waits=[tk_ld, wsplit_state["st"][sl]], inc="ve")
        tk_l = s.op("vector", lambda e, wid=wid, spo=spo: e.tensor_tensor(
            wsp[:, spo + wid:spo + 2 * wid], wst[:, 0:wid],
            wsp[:, spo:spo + wid], ALU.subtract), waits=[tk_h], inc="vws")
        tk_st = s.dma("gpsimd", wdst[:, e8 * 2 * wid:(e8 + 1) * 2 * wid],
                      wsp[:, spo:spo + 2 * wid], waits=[tk_l],
                      inc="dwsp2%d" % sl)
        toks[e8] = tk_st
        wsplit_state["lo"] = tk_l
        wsplit_state["st"][sl] = tk_st
        wsplit_state["n"] = n + 1
    t_boot_done = s.op("vector", lambda e: e.nop(),
                       waits=[tk_dt_sb], inc="ve")
    for _ in range(3):
        emit_wsplit_chunk()
    t_crep_init = t_boot_done

    raw_guard = [t_hist, t_hist]
    prev_mm_done = [None, None]
    prev_wload = [None, None]

    dbg_toks = []
    tile_guard = None
    prev_dbg = None
    guard_pt = t_boot_done
    guard_expa = None
    guard_stg = t_boot_done
    guard_alast = None
    guard_c0 = None
    guard_wo = [None, None]
    guard_y16 = None
    prev_rep = None
    prev_apply = None
    tk_ctil_free = None
    tk_spill = None
    tk_amax_tile = None
    guard_repall = None
    guard_crep = t_crep_init
    guard_ctil = None
    guard_wave0 = None
    guard_scan0 = None
    guard_pa = [None, None]
    guard_bpad = [t_boot_done, t_boot_done]
    guard_xdt = [None, None]
    guard_m = [t_boot_done, t_boot_done]
    guard_l97 = [tk_bootall2, tk_bootall2]
    guard_py = [None, None]
    guard_g = None
    guard_pg = None
    guard_pst = None
    guard_rows = None
    tk_xdt_g = t_boot_done
    tk_xdt_g2 = [t_boot_done, t_boot_done]
    prepped = False

    for it in range(NT):
        t0 = it * TP
        tk_x = xprep["tok"]
        def emit_A_chunk(it, idx, e, tile_guard, prev_dbg, tk_x):
            wslot = idx % 2
            tk_w = s.dma("sync",
                         wbufA[:, wslot * 2048:(wslot + 1) * 2048],
                         w16_in[:, e * 2048:(e + 1) * 2048],
                         waits=[prev_wload[wslot],
                                wsplit_in_tok[e] if it == 0 else None],
                         inc="dw%d" % wslot)
            tk_mm = None
            nmm = 0
            for k in range(8):
                wh = wbufA[:, wslot * 2048 + k * 128:
                           wslot * 2048 + (k + 1) * 128]
                wl = wbufA[:, wslot * 2048 + 1024 + k * 128:
                           wslot * 2048 + 1024 + (k + 1) * 128]
                for lhs, rhs in ((wh, x16h), (wh, x16l), (wl, x16h)):
                    tk_mm = s.op(
                        "tensor",
                        lambda eng, k=k, idx=idx, lhs=lhs, rhs=rhs, nmm=nmm:
                        eng.matmul(
                            pA[idx % 2][:, :], lhs,
                            rhs[:, k * TP:(k + 1) * TP],
                            start=(nmm == 0), stop=(nmm == 23)),
                        waits=([tk_w, tk_x, prev_mm_done[idx % 2]]
                               if nmm == 0 else ()),
                        inc="te" if nmm == 23 else None)
                    nmm += 1
            prev_wload[wslot] = tk_mm

            rslot = idx % 2
            roff = rslot * (TP + 3)
            if e < NECH_A - 1 or True:
                nrows = 128 if e < 16 else 48
                tk_h = s.op("vector",
                            lambda eng, e=e, roff=roff, nrows=nrows:
                            eng.tensor_copy(raw[0:nrows, roff:roff + 3],
                                            hist[0:nrows, e * 3:e * 3 + 3]),
                            waits=[raw_guard[rslot], tk_mm], inc="ve")
                tk_ev = s.op("scalar",
                             lambda eng, idx=idx, roff=roff, nrows=nrows:
                             eng.copy(raw[0:nrows, roff + 3:roff + 3 + TP],
                                      pA[idx % 2][0:nrows, :]),
                             waits=[tk_mm, raw_guard[rslot], tk_h], inc="se")
                prev_mm_done[idx % 2] = tk_ev if e < 16 else None
                tk_hs = s.op("vector",
                             lambda eng, e=e, roff=roff, nrows=nrows:
                             eng.tensor_copy(hist[0:nrows, e * 3:e * 3 + 3],
                                             raw[0:nrows, roff + TP:roff + TP + 3]),
                             waits=[tk_ev], inc="ve")
                ac = (0, e * TP)
                tkt = s.op("vector",
                           lambda eng, e=e, roff=roff, nrows=nrows:
                           eng.tensor_scalar(
                               xbcc[0:nrows, e * TP:e * TP + TP],
                               raw[0:nrows, roff:roff + TP],
                               convw[0:nrows, e * 4:e * 4 + 1], None, ALU.mult),
                           waits=[tk_ev, tile_guard, prev_dbg], inc="ve")
                for k in (1, 2, 3):
                    tkt = s.op("vector",
                               lambda eng, e=e, k=k, roff=roff, nrows=nrows:
                               eng.scalar_tensor_tensor(
                                   xbcc[0:nrows, e * TP:e * TP + TP],
                                   raw[0:nrows, roff + k:roff + k + TP],
                                   convw[0:nrows, e * 4 + k:e * 4 + k + 1],
                                   xbcc[0:nrows, e * TP:e * TP + TP],
                                   ALU.mult, ALU.add),
                               waits=[tkt], inc="ve")
                raw_guard[rslot] = tkt
                tk_si = s.op("scalar",
                             lambda eng, e=e, nrows=nrows: eng.activation(
                                 xbcc[0:nrows, e * TP:e * TP + TP],
                                 xbcc[0:nrows, e * TP:e * TP + TP],
                                 AF.Silu, bias=convb[0:nrows, e:e + 1],
                                 scale=1.0),
                             waits=[tkt], inc="se")
                ech_done.append(tk_si)
                ech_map[e] = tk_si
            if e == 16:
                tk_e1 = s.op("scalar",
                             lambda eng, idx=idx: eng.activation(
                                 dtv[:, :], pA[idx % 2][64:96, :], AF.Exp,
                                 bias=dtb[:, :], scale=1.0),
                             waits=[tk_mm, tile_guard, prev_dbg], inc="se")
                prev_mm_done[idx % 2] = tk_e1
                tk_dt = s.op("scalar",
                             lambda eng: eng.activation(
                                 dtv[:, :], dtv[:, :], AF.Ln,
                                 bias=1.0, scale=1.0),
                             waits=[tk_e1], inc="se")
                tk_adt = s.op("vector",
                              lambda eng: eng.tensor_scalar(
                                  adt[:, :], dtv[:, :], avec[:, :], None,
                                  ALU.mult),
                              waits=[tk_dt, t_avec], inc="ve")
                tk_sc = []
                for cch in range(TP // 64):
                    tk_sc.append(s.op(
                        "vector",
                        lambda eng, cch=cch: eng.tensor_tensor_scan(
                            acs[:, cch * 64:(cch + 1) * 64],
                            ones32[:, :],
                            adt[:, cch * 64:(cch + 1) * 64],
                            0.0, ALU.mult, ALU.add),
                        waits=[tk_adt, tile_guard, prev_dbg] if cch == 0
                              else (), inc="ve"))
                tk_acsn = s.op("vector",
                               lambda eng: eng.tensor_scalar_mul(
                                   acsn[:, :], acs[:, :], -1.0),
                               waits=[tk_sc[-1]], inc="ve")
                ech_done.extend([tk_dt, tk_acsn])
                pp["dt"] = tk_dt
                pp["acsn"] = tk_acsn


        if not prepped:
            ech_done = []
            ech_map = {}
            pp = {}
        def emit_ssd_ab():
            tk_dt_tok = pp["dt"]
            tk_acsn_tok = pp["acsn"]
            tk_expa = s.op("scalar", lambda e: e.activation(
                expa_fm[:, :], acs[:, :], AF.Exp, bias=0.0, scale=1.0),
                waits=[tk_acsn_tok, guard_expa], inc="se")
            pp["expa"] = tk_expa
            tts = []
            for p4 in range(NPAIR):
                tts.append(s.op("tensor", lambda e, p4=p4: e.matmul(
                    pT[:, 32 * p4:32 * p4 + 32],
                    dtv[:, 128 * p4:128 * p4 + 128], ident[0:32, 0:32],
                    is_transpose=True, start=(p4 == 0), stop=False,
                    skip_group_check=True),
                    waits=[tk_dt_tok, tk_bootall2, guard_pt] if p4 == 0
                          else (),
                    inc="te"))
                tts.append(s.op("tensor", lambda e, p4=p4: e.matmul(
                    pT[:, 128 + 32 * p4:128 + 32 * p4 + 32],
                    acs[:, 128 * p4:128 * p4 + 128], ident[0:32, 0:32],
                    is_transpose=True, start=False, stop=(p4 == NPAIR - 1),
                    skip_group_check=True),
                    waits=[tk_acsn_tok] if p4 == 0 else (), inc="te"))
            tk_dttm = s.op("vector", lambda e: e.tensor_copy(
                dt_tm[:, :], pT[:, 0:32 * NPAIR]), waits=[tts[-1]], inc="ve")
            tk_acstm = s.op("vector", lambda e: e.tensor_copy(
                acs_tm[:, :], pT[:, 128:128 + 32 * NPAIR]), inc="ve")
            pp["dttm"] = tk_dttm
            s.dma("gpsimd", alastT[0:1, :], acs_tm[63:64, :],
                  waits=[tk_acstm, guard_alast], inc="dal")
            tk_alast = s.dma("gpsimd", alastT[1:2, :], acs_tm[127:128, :],
                             inc="dal")
            tk_selmm = []
            for p4 in range(NPAIR):
                tk_selmm.append(s.op("tensor", lambda e, p4=p4: e.matmul(
                    pT[:, 256 + 32 * p4:256 + 32 * p4 + 32], l97[0][0:2, :],
                    alastT[:, 32 * p4:32 * p4 + 32], start=(p4 == 0),
                    stop=(p4 == NPAIR - 1), skip_group_check=True),
                    waits=[tk_alast] if p4 == 0 else (), inc="te"))
            pp["selmm_last"] = tk_selmm[-1]
            tk_dpre = s.op("vector", lambda e: e.tensor_tensor(
                dec_tm[:, :], pT[:, 256:256 + 32 * NPAIR], acs_tm[:, :],
                ALU.subtract), waits=[tk_selmm[-1]], inc="ve")
            tk_dec = s.op("scalar", lambda e: e.activation(
                dec_tm[:, :], dec_tm[:, :], AF.Exp, bias=0.0, scale=1.0),
                waits=[tk_dpre], inc="se")
            pp["dpre"] = tk_dpre
            pp["dec"] = tk_dec

        def emit_ssd_c():
            tk_expa = pp["expa"]
            lin_toks = []
            for hq in range(4):
                dst = bass.AP(rep_all, hq * (8 * TP + 64),
                              [[8 * TP + 64, 1], [512, 8], [64, TP // 64],
                               [1, 64]])
                lin_toks.append(s.dma("gpsimd", dst,
                                      expa_fm[8 * hq:8 * hq + 8, :],
                                      waits=[tk_expa, guard_stg],
                                      inc="dlin"))
                dstl = bass.AP(rep_all, hq * (8 * TP + 64) + 8 * TP,
                               [[8 * TP + 64, 1], [8, 8], [1, TP // 64]])
                lin_toks.append(s.op("gpsimd", lambda e, dstl=dstl, hq=hq:
                                e.dma_start(out=dstl,
                                            in_=expa_fm[8 * hq:8 * hq + 8,
                                                        63:TP:64],
                                            allow_slow_non_contiguous=True),
                                waits=[guard_stg], inc="dlin", n=16))
            tk_place = s.dma("gpsimd", rep_all[32:128:32, :],
                             rep_all[1:4, :],
                             waits=[lin_toks[-1]], inc="dplace")
            tk_shuf = s.op("vector", lambda e: e.stream_shuffle(
                rep_all[:, :], rep_all[:, :], [0] * 32),
                waits=[tk_place, guard_repall], inc="ve")
            pp["place"] = tk_place
            pp["shuf"] = tk_shuf

        def emit_ssd_de():
            tk_c0 = s.dma("sync", c0[:, :],
                          xbcc[32:48, 16 * TP:17 * TP],
                          waits=[ech_map[16], guard_c0], inc="dc0")
            crep_toks = []
            for hq in range(4):
                crep_toks.append(s.dma(
                    "sync", c_rep4[32 * hq:32 * hq + 16, :],
                    xbcc[32:48, 16 * TP:17 * TP],
                    waits=[ech_map[16], guard_crep], inc="dcrep"))
            in0 = bass.AP(c_rep4, 0,
                          [[TP, 128], [0, 8], [64, TP // 64], [1, 64]])
            tk_ctil = s.op("vector", lambda e, in0=in0: e.tensor_tensor(
                c_til[:, :], in0, rep_all[:, 0:8 * TP], ALU.mult),
                waits=[pp["shuf"], crep_toks[-1], guard_ctil], inc="ve")
            pp["c0"] = tk_c0
            pp["ctil"] = tk_ctil

        for idx, e in enumerate([NECH_A - 1] + list(range(NECH_A - 1))):
            if idx == 0 and prepped:
                continue
            if it > 0 and not prepped and stages >= 2:
                if idx == 6:
                    emit_ssd_ab()
                if idx == 9:
                    emit_ssd_c()
                if idx == 11:
                    emit_ssd_de()
            if prepped and idx == 2:
                emit_ssd_de()
            if it == 0:
                emit_wsplit_chunk()
                emit_wsplit_chunk()
                if idx % 2 == 1:
                    emit_wsplit_chunk()
            emit_A_chunk(it, idx, e, tile_guard, prev_dbg, tk_x)

        if it == 0:
            while wsplit_state["n"] < len(_wsq):
                emit_wsplit_chunk()
            tk_wsplit_done = s.op("vector", lambda e: e.nop(),
                                  waits=[wsplit_state["st"][0],
                                         wsplit_state["st"][1]], inc="vws")
            guard_stg = tk_wsplit_done
            guard_m[0] = guard_m[1] = tk_wsplit_done
            if stages >= 2:
                emit_ssd_ab()
                emit_ssd_c()
                emit_ssd_de()

        if debug:
            s.dma("sync", dbg["dbg_xbcc"][:, it * NECH_A * TP:(it + 1) * NECH_A * TP],
                  xbcc[:, :], waits=ech_done, inc="ddbg")
            s.dma("sync", dbg["dbg_dt"][:, t0:t0 + TP],
                  dtv[:, :], waits=ech_done, inc="ddbg")
            tk = s.dma("sync", dbg["dbg_acs"][:, t0:t0 + TP],
                       acs[:, :], waits=ech_done, inc="ddbg")
            dbg_toks = [("ddbg", s.cnt["ddbg"])]
            prev_dbg = dbg_toks[0]
            if stages == 1:
                tile_guard = dbg_toks[0]

        if stages <= 1:
            if it + 1 < NT:
                for dc in range(8):
                    emit_xprep_chunk(it + 1, dc, ech_done[-1])
            continue

        prepped = False
        tk_dttm = pp["dttm"]
        tk_dec = pp["dec"]
        tk_dpre = pp["dpre"]
        tk_shuf = pp["shuf"]
        tk_ctil = pp["ctil"]
        tk_c0 = pp["c0"]
        tk_acsn_tok = pp["acsn"]

        wave_tok = guard_wave0
        scan_tok = guard_scan0
        for p4 in range(NPAIR):
            par = p4 % 2
            g0 = it * (TP // 64) + 2 * p4
            toff = 128 * p4

            tk_btr = s.op("tensor", lambda e, toff=toff: e.matmul(
                pSt[:, 0:16],
                xbcc[0:16, 16 * TP + toff:16 * TP + toff + 128],
                ident[0:16, 0:16], is_transpose=True, start=True, stop=True,
                skip_group_check=True),
                waits=[ech_map[16], guard_bpad[par], tk_dec, scan_tok,
                       guard_pst if p4 == 0 else None],
                inc="te")
            in_b = bass.AP(pSt, 0, [[512, 128], [0, 32], [1, 16]])
            in_d = bass.AP(dec_tm, 32 * p4, [[NPAIR * 32, 128], [1, 32], [0, 16]])
            out_b = bass.AP(bpad[par], 0, [[1024, 128], [32, 32], [1, 16]])
            tk_bpad = s.op("vector", lambda e, out_b=out_b, in_b=in_b,
                           in_d=in_d: e.tensor_tensor(
                out_b, in_b, in_d, ALU.mult), waits=[tk_btr], inc="ve")

            tk_l97 = s.dma("sync", l97[par][2:34, :],
                           acsn[:, toff:toff + 128],
                           waits=[guard_l97[par], tk_acsn_tok], inc="dl97")
            tk_r97a = s.dma("sync", r97[par][0:1, 0:2048],
                            acs[:, toff:toff + 64],
                            waits=[guard_l97[par]], inc="dr97")
            tk_r97b = s.dma("sync", r97[par][1:2, 0:2048],
                            acs[:, toff + 64:toff + 128], inc="dr97")

            tk_xdt = []
            for grp in range(4):
                pTx = pT if grp % 2 == 0 else pT2
                gt = []
                for kk in range(4):
                    kf = 4 * grp + kk
                    gt.append(s.op("tensor", lambda e, kf=kf, kk=kk, toff=toff,
                              pTx=pTx: e.matmul(
                                  pTx[:, 128 * kk:128 * kk + 128],
                                  xbcc[:, kf * TP + toff:kf * TP + toff + 128],
                                  ident[:, :], is_transpose=True,
                                  start=(kk == 0), stop=(kk == 3),
                                  skip_group_check=True),
                              waits=[ech_map[kf]] +
                                    (([guard_xdt[par], tk_xdt_g2[grp % 2]] +
                                      ([tk_dpre, tk_bpad] if grp == 0
                                       else []))
                                     if kk == 0 else []),
                              inc="te"))
                in_dt = bass.AP(dt_tm, 32 * p4 + 8 * grp,
                                [[NPAIR * 32, 128], [1, 8], [0, 64]])
                tk_g = s.op("vector", lambda e, grp=grp, par=par,
                            in_dt=in_dt, pTx=pTx: e.tensor_tensor(
                    xdt[par][:, 512 * grp:512 * grp + 512], pTx[:, :],
                    in_dt, ALU.mult), waits=[gt[-1], tk_dttm], inc="ve")
                tk_xdt.append(tk_g)
                tk_xdt_g2[grp % 2] = tk_g
                tk_xdt_g = tk_g
            tk_gmm = []
            for c01 in range(2):
                tk_gmm.append(s.op("tensor", lambda e, c01=c01, toff=toff:
                              e.matmul(
                                  pG[64 * c01:64 * c01 + 64, 0:64],
                                  xbcc[0:16, 16 * TP + toff + 64 * c01:
                                       16 * TP + toff + 64 * c01 + 64],
                                  c0[:, toff + 64 * c01:toff + 64 * c01 + 64],
                                  start=True, stop=True,
                                  skip_group_check=True),
                              waits=[tk_c0, guard_g, guard_pg]
                                    if c01 == 0 else (),
                              inc="te"))

            tk_mq = []
            for q in range(4):
                tk_ds = s.op("tensor", lambda e, q=q, par=par: e.matmul(
                    pA[q % 2][:, 0:512], l97[par][:, :],
                    r97[par][:, 512 * q:512 * q + 512],
                    start=True, stop=True, skip_group_check=True),
                    waits=[tk_l97, tk_r97b, guard_pa[q % 2],
                           guard_m[par]] if q <= 1 else [guard_pa[q % 2]],
                    inc="te")
                tk_ex = s.op("scalar", lambda e, q=q, par=par: e.activation(
                    mbuf[par][:, 512 * q:512 * q + 512], pA[q % 2][:, 0:512],
                    AF.Exp, bias=0.0, scale=1.0), waits=[tk_ds], inc="se")
                guard_pa[q % 2] = tk_ex
                in_g = bass.AP(pG, 0, [[512, 128], [0, 8], [1, 64]])
                tk_m = s.op("vector", lambda e, q=q, par=par: e.tensor_tensor(
                    mbuf[par][:, 512 * q:512 * q + 512],
                    mbuf[par][:, 512 * q:512 * q + 512], in_g, ALU.mult),
                    waits=[tk_ex, tk_gmm[-1]], inc="ve")
                tk_mq.append(tk_m)

            for c01 in range(2):
                g = g0 + c01
                cl = 2 * p4 + c01
                tk_st = None
                for h in range(32):
                    hq, hh = h // 8, h % 8
                    kf, hl = h // 2, h % 2
                    tk_st = s.op("tensor", lambda e, h=h, hq=hq, hh=hh,
                                 kf=kf, hl=hl, c01=c01, par=par: e.matmul(
                        pSt[32 * hq:32 * hq + 32, 64 * hh:64 * hh + 64],
                        bpad[par][64 * c01:64 * c01 + 64, 32 * h:32 * h + 32],
                        xdt[par][64 * c01:64 * c01 + 64,
                                 128 * kf + 64 * hl:128 * kf + 64 * hl + 64],
                        start=(h % 8 == 0), stop=(h % 8 == 7),
                        tile_position=(64 * c01, 32 * hq),
                        skip_group_check=True),
                        waits=[tk_bpad, tk_xdt[3], scan_tok, wave_tok]
                              if h == 0 else (),
                        inc="te")
                lam_ap = bass.AP(rep_all, 8 * TP + 8 * 0 + cl,
                                 [[8 * TP + 64, 128], [8, 8], [0, 64]])
                tk_s1 = s.op("vector", lambda e, g=g, lam_ap=lam_ap:
                             e.tensor_tensor(
                                 sring[(g + 1) % 3][:, :], sring[g % 3][:, :],
                                 lam_ap, ALU.mult),
                             waits=[tk_shuf, wave_tok, scan_tok], inc="ve")
                scan_tok = s.op("vector", lambda e, g=g: e.tensor_tensor(
                    sring[(g + 1) % 3][:, :], sring[(g + 1) % 3][:, :],
                    pSt[:, :], ALU.add), waits=[tk_s1, tk_st], inc="ve")

            for w in range(4):
                tk_wv = None
                first = True
                for c01 in range(2):
                    g = g0 + c01
                    for hh4 in range(4):
                        for hl in range(2):
                            h = 8 * w + 2 * hh4 + hl
                            kf = 4 * w + hh4
                            ypart = (64 * hl, 128 * hh4 + 64 * c01)
                            wts = [guard_py[w % 2], tk_mq[w], tk_ctil,
                                   scan_tok] if first else ()
                            first_mm = (c01 == 0 and hh4 == 0)
                            first = False
                            s.op("tensor", lambda e, w=w, c01=c01, hh4=hh4,
                                 hl=hl, h=h, kf=kf, par=par, ypart=ypart,
                                 first_mm=first_mm: e.matmul(
                                     pY[w % 2][ypart[0]:ypart[0] + 64,
                                               ypart[1]:ypart[1] + 64],
                                     xdt[par][64 * c01:64 * c01 + 64,
                                              128 * kf + 64 * hl:
                                              128 * kf + 64 * hl + 64],
                                     mbuf[par][64 * c01:64 * c01 + 64,
                                               64 * h:64 * h + 64],
                                     start=first_mm, stop=False,
                                     tile_position=(64 * c01, ypart[0]),
                                     skip_group_check=True),
                                 waits=wts, inc=None)
                        for hl in range(2):
                            h = 8 * w + 2 * hh4 + hl
                            hq, hhi = h // 8, h % 8
                            ypart = (64 * hl, 128 * hh4 + 64 * c01)
                            tk_wv = s.op("tensor", lambda e, w=w, c01=c01,
                                         hh4=hh4, h=h, hq=hq, hhi=hhi, p4=p4,
                                         ypart=ypart, g=g: e.matmul(
                                pY[w % 2][ypart[0]:ypart[0] + 64,
                                          ypart[1]:ypart[1] + 64],
                                sring[g % 3][32 * hq:32 * hq + 16,
                                             64 * hhi:64 * hhi + 64],
                                c_til[32 * hq:32 * hq + 16,
                                      512 * hhi + 64 * (2 * p4 + c01):
                                      512 * hhi + 64 * (2 * p4 + c01) + 64],
                                start=False,
                                stop=(c01 == 1 and hh4 == 3),
                                tile_position=(32 * hq, ypart[0]),
                                skip_group_check=True),
                                inc="te")
                wave_tok = tk_wv
                dst = bass.AP(y_buf, (4 * w) * TP + toff,
                              [[16 * TP, 128], [TP, 4], [64, 2], [1, 64]])
                guard_py[w % 2] = s.op("scalar", lambda e, w=w, dst=dst:
                                       e.copy(dst, pY[w % 2].ap().rearrange(
                                           "p (a b c) -> p a b c", a=4, b=2)),
                                       waits=[tk_wv], inc="se")

            guard_bpad[0] = guard_bpad[1] = tk_st
            guard_xdt[0] = guard_xdt[1] = wave_tok
            guard_m[0] = guard_m[1] = wave_tok
            guard_l97[0] = guard_l97[1] = tk_mq[-1]
            guard_g = tk_mq[-1]

        guard_alast = pp["selmm_last"]
        guard_c0 = tk_gmm[-1]
        guard_pt = tk_xdt_g2[0]
        guard_expa = pp["place"]
        guard_stg = pp["place"]
        guard_repall = scan_tok
        guard_crep = tk_ctil
        guard_ctil = wave_tok
        guard_wave0 = wave_tok
        guard_scan0 = scan_tok
        tile_ssd = [wave_tok, scan_tok, guard_py[0], guard_py[1]]

        if debug and stages == 2:
            s.dma("sync", dbg["dbg_ybuf"][:, it * 16 * TP:(it + 1) * 16 * TP],
                  y_buf[:, :], waits=tile_ssd, inc="ddbg")
            tile_guard = ("ddbg", s.cnt["ddbg"])
            if it + 1 < NT:
                for dc in range(8):
                    emit_xprep_chunk(it + 1, dc, tile_guard)
            continue

        tk_gate = []
        tk_y2s = []
        tk_sq = None
        def emit_ysq_mm(k):
            return s.op("tensor", lambda e, k=k: e.matmul(
                pG[0:1, 0:TP], ones_col[:, :],
                mbuf[0][:, 512 * (k % 4):512 * (k % 4) + TP],
                start=(k == 0), stop=(k == NECH_Z - 1),
                skip_group_check=True),
                waits=[tk_y2s[k]] + ([guard_pg] if k == 0 else []), inc="te")
        for k in range(NECH_Z):
            ez = NECH_A + k
            wslot = k % 2
            tk_w = s.dma("sync",
                         wbufA[:, wslot * 2048:(wslot + 1) * 2048],
                         w16_in[:, ez * 2048:(ez + 1) * 2048],
                         waits=[prev_wload[wslot],
                                wsplit_in_tok[ez] if it == 0 else None],
                         inc="dw%d" % wslot)
            tk_mm = None
            nmm = 0
            for kc in range(8):
                wh = wbufA[:, wslot * 2048 + kc * 128:
                           wslot * 2048 + (kc + 1) * 128]
                wl = wbufA[:, wslot * 2048 + 1024 + kc * 128:
                           wslot * 2048 + 1024 + (kc + 1) * 128]
                for lhs, rhs in ((wh, x16h), (wh, x16l), (wl, x16h)):
                    tk_mm = s.op(
                        "tensor",
                        lambda e, kc=kc, k=k, lhs=lhs, rhs=rhs, nmm=nmm:
                        e.matmul(
                            pA[k % 2][:, :], lhs,
                            rhs[:, kc * TP:(kc + 1) * TP],
                            start=(nmm == 0), stop=(nmm == 23)),
                        waits=([tk_w, prev_mm_done[k % 2]] +
                               (tile_ssd if k == 0 else []))
                              if nmm == 0 else (),
                        inc="te" if nmm == 23 else None)
                    nmm += 1
            prev_wload[wslot] = tk_mm
            tk_sz = s.op("scalar", lambda e, k=k: e.activation(
                raw[:, 515 * (k % 2):515 * (k % 2) + TP], pA[k % 2][:, :],
                AF.Silu, bias=0.0, scale=1.0),
                waits=[tk_mm] + ([tile_ssd[2], tile_ssd[3]] if k < 2 else []),
                inc="se")
            prev_mm_done[k % 2] = tk_sz
            tk_d = s.op("vector", lambda e, k=k: e.scalar_tensor_tensor(
                y_buf[:, k * TP:(k + 1) * TP],
                xbcc[:, k * TP:(k + 1) * TP],
                dt_sbuf[:, k:k + 1],
                y_buf[:, k * TP:(k + 1) * TP], ALU.mult, ALU.add),
                waits=[tk_sz, tile_guard], inc="ve")
            tk_g2 = s.op("vector", lambda e, k=k: e.tensor_tensor(
                y_buf[:, k * TP:(k + 1) * TP],
                y_buf[:, k * TP:(k + 1) * TP],
                raw[:, 515 * (k % 2):515 * (k % 2) + TP], ALU.mult),
                waits=[tk_d], inc="ve")
            tk_gate.append(tk_g2)
            raw_guard[0] = raw_guard[1] = tk_g2
            tk_y2 = s.op("scalar", lambda e, k=k: e.activation(
                mbuf[0][:, 512 * (k % 4):512 * (k % 4) + TP],
                y_buf[:, k * TP:(k + 1) * TP], AF.Square, bias=0.0, scale=1.0),
                waits=[tk_g2, tk_sq], inc="se")
            tk_y2s.append(tk_y2)
            if k >= 2:
                tk_sq = emit_ysq_mm(k - 2)
        for k in (NECH_Z - 2, NECH_Z - 1):
            tk_sq = emit_ysq_mm(k)
        tk_rr0 = s.op("scalar", lambda e: e.activation(
            rowA[:, :], pG[0:1, 0:TP], AF.Sqrt,
            bias=cst[:, 0:1], scale=cst[:, 1:2]),
            waits=[tk_sq, guard_rows], inc="se")
        tk_rr = s.op("vector", lambda e: e.reciprocal(rowA[:, :], rowA[:, :]),
                     waits=[tk_rr0], inc="ve")
        tk_rrep = s.op("tensor", lambda e: e.matmul(
            pY[0][:, 0:TP], ones_row[:, :], rowA[:, :],
            start=True, stop=True, skip_group_check=True),
            waits=[tk_rr, wave_tok, guard_py[0]], inc="te")
        tk_yn = []
        for k in range(NECH_Z):
            tk_n = s.op("vector", lambda e, k=k: e.scalar_tensor_tensor(
                y_buf[:, k * TP:(k + 1) * TP],
                y_buf[:, k * TP:(k + 1) * TP],
                normw_c[:, k:k + 1],
                pY[0][:, 0:TP], ALU.mult, ALU.mult),
                waits=[tk_rrep, tk_gate[k]], inc="ve")
            tk_h16 = s.op("vector", lambda e, k=k: e.tensor_copy(
                y16h[:, k * TP:(k + 1) * TP], y_buf[:, k * TP:(k + 1) * TP]),
                waits=[tk_n, guard_y16, prev_dbg], inc="ve")
            tk_yn.append(s.op("vector", lambda e, k=k: e.tensor_tensor(
                y16l[:, k * TP:(k + 1) * TP], y_buf[:, k * TP:(k + 1) * TP],
                y16h[:, k * TP:(k + 1) * TP], ALU.subtract),
                waits=[tk_h16], inc="ve"))
        tk_ev_h1 = []
        tk_q2s = []
        tk_sum = None
        tk_sq2 = None
        def emit_h1_stats(j):
            a = s.op("tensor", lambda e, j=j: e.matmul(
                pG[0:1, 0:TP], ones_col[:, :], c_til[:, 512 * j:512 * j + TP],
                start=(j == 0), stop=(j == 7), skip_group_check=True),
                waits=[tk_ev_h1[j]], inc="te")
            b = s.op("tensor", lambda e, j=j: e.matmul(
                pSt[0:1, 0:TP], ones_col[:, :],
                mbuf[0][:, 512 * (j % 4):512 * (j % 4) + TP],
                start=(j == 0), stop=(j == 7), skip_group_check=True),
                waits=[tk_q2s[j]] +
                      ([scan_tok, guard_pst] if j == 0 else []), inc="te")
            return a, b
        for j in range(8):
            wos = j % 2
            wov = rep_all.ap().bitcast(F16)
            tk_wo = s.dma("sync",
                          wov[:, 4096 * wos:4096 * wos + 4096],
                          w16_out[:, j * 4096:(j + 1) * 4096],
                          waits=[guard_wo[wos],
                                 wsplit_out_tok[j] if it == 0 else None],
                          inc="dwo%d" % wos)
            tk_mm = None
            for k in range(16):
                wh = wov[:, 4096 * wos + k * 128:4096 * wos + (k + 1) * 128]
                wl = wov[:, 4096 * wos + 2048 + k * 128:
                         4096 * wos + 2048 + (k + 1) * 128]
                for li, (lhs, rhs) in enumerate(
                        ((wh, y16h), (wh, y16l), (wl, y16h))):
                    s.op("tensor", lambda e, k=k, j=j, lhs=lhs, rhs=rhs,
                         li=li: e.matmul(
                        pA[j % 2][:, :], lhs,
                        rhs[:, k * TP:(k + 1) * TP],
                        start=(k == 0 and li == 0), stop=False,
                        skip_group_check=True),
                        waits=[tk_wo, tk_yn[k], prev_mm_done[j % 2]]
                              if k == 0 and li == 0 else
                              ([tk_yn[k]] if j == 0 and li == 0 else ()),
                        inc=None)
            s.op("tensor", lambda e, j=j: e.matmul(
                pA[j % 2][:, :], ident16[:, :],
                x16h[:, j * TP:(j + 1) * TP],
                start=False, stop=False, skip_group_check=True), inc=None)
            tk_mm = s.op("tensor", lambda e, j=j: e.matmul(
                pA[j % 2][:, :], ident16[:, :],
                x16l[:, j * TP:(j + 1) * TP],
                start=False, stop=True, skip_group_check=True),
                inc="te")
            guard_wo[wos] = tk_mm
            if it + 1 < NT:
                emit_xprep_chunk(it + 1, j, tk_mm)
            tk_ev = s.op("scalar", lambda e, j=j: e.copy(
                c_til[:, 512 * j:512 * j + TP], pA[j % 2][:, :]),
                waits=[tk_mm], inc="se")
            prev_mm_done[j % 2] = tk_ev
            tk_ev_h1.append(tk_ev)
            tk_q2 = s.op("scalar", lambda e, j=j: e.activation(
                mbuf[0][:, 512 * (j % 4):512 * (j % 4) + TP],
                c_til[:, 512 * j:512 * j + TP], AF.Square, bias=0.0,
                scale=1.0),
                waits=[tk_ev, tk_sq2 if j > 0 else tk_sq], inc="se")
            tk_q2s.append(tk_q2)
            if j >= 1:
                tk_sum, tk_sq2 = emit_h1_stats(j - 1)
        if debug:
            s.dma("sync", dbg["dbg_h1pre"][:, it * 8 * TP:(it + 1) * 8 * TP],
                  c_til[:, 0:8 * TP], waits=[tk_ev_h1[-1], tk_sq2], inc="ddbg")
        tk_sum, tk_sq2 = emit_h1_stats(7)
        if stages >= 3 and it + 1 < NT:
            ech_done = []
            ech_map = {}
            pp = {}
            emit_A_chunk(it + 1, 0, NECH_A - 1, guard_wo[1], prev_dbg,
                         xprep["tok"])
            emit_ssd_ab()
            emit_ssd_c()
            prepped = True
        tk_mu = s.op("scalar", lambda e: e.activation(
            rowC[:, :], pG[0:1, 0:TP], AF.Copy, bias=0.0,
            scale=cst[:, 2:3]), waits=[tk_sum, guard_rows], inc="se")
        tk_ex2 = s.op("scalar", lambda e: e.activation(
            rowD[:, :], pSt[0:1, 0:TP], AF.Copy, bias=0.0,
            scale=cst[:, 2:3]), waits=[tk_sq2], inc="se")
        tk_mu2 = s.op("vector", lambda e: e.tensor_tensor(
            rowB[:, :], rowC[:, :], rowC[:, :], ALU.mult),
            waits=[tk_mu], inc="ve")
        tk_var = s.op("vector", lambda e: e.tensor_tensor(
            rowD[:, :], rowD[:, :], rowB[:, :], ALU.subtract),
            waits=[tk_ex2, tk_mu2], inc="ve")
        tk_r10 = s.op("scalar", lambda e: e.activation(
            rowA[:, :], rowD[:, :], AF.Sqrt, bias=cst[:, 0:1], scale=1.0),
            waits=[tk_var, tk_rrep, guard_rows], inc="se")
        tk_r1 = s.op("vector", lambda e: e.reciprocal(rowA[:, :], rowA[:, :]),
                     waits=[tk_r10], inc="ve")
        tk_mr0 = s.op("vector", lambda e: e.tensor_tensor(
            rowB[:, :], rowC[:, :], rowA[:, :], ALU.mult),
            waits=[tk_r1], inc="ve")
        tk_mr = s.op("vector", lambda e: e.tensor_scalar_mul(
            rowB[:, :], rowB[:, :], -1.0), waits=[tk_mr0], inc="ve")
        tk_rep1 = s.op("tensor", lambda e: e.matmul(
            pY[0][:, 0:TP], ones_row[:, :], rowA[:, :],
            start=True, stop=True, skip_group_check=True),
            waits=[tk_r1, tk_yn[-1], guard_py[0]], inc="te")
        tk_rep2 = s.op("tensor", lambda e: e.matmul(
            pY[1][:, 0:TP], ones_row[:, :], rowB[:, :],
            start=True, stop=True, skip_group_check=True),
            waits=[tk_mr, guard_py[1]], inc="te")
        guard_rows = tk_mr
        tk_am = None
        for j in range(8):
            tk_a1 = s.op("vector", lambda e, j=j: e.tensor_tensor(
                c_til[:, 512 * j:512 * j + TP], c_til[:, 512 * j:512 * j + TP],
                pY[0][:, 0:TP], ALU.mult),
                waits=[tk_rep1, tk_sq2,
                       ("ddbg", s.cnt["ddbg"]) if debug else None], inc="ve")
            tk_a2 = s.op("vector", lambda e, j=j: e.tensor_tensor(
                c_til[:, 512 * j:512 * j + TP], c_til[:, 512 * j:512 * j + TP],
                pY[1][:, 0:TP], ALU.add), waits=[tk_rep2, tk_a1], inc="ve")
            tk_rm = s.op("vector", lambda e, j=j: e.reduce_max(
                red_col[:, :], c_til[:, 512 * j:512 * j + TP], AX.X,
                apply_absolute_value=True), waits=[tk_a2], inc="ve")
            tk_am = s.op("vector", lambda e: e.tensor_max(
                amax_col[:, :], amax_col[:, :], red_col[:, :]),
                waits=[tk_rm], inc="ve")
            tk_ap_last = tk_a2
        tk_spill = s.dma("gpsimd",
                         h1_spill.rearrange("(c p) t -> p c t", p=128)
                         [:, :, t0:t0 + TP],
                         c_til.ap().rearrange("p (c t) -> p c t", t=TP),
                         waits=[tk_am], inc="dh1")
        tk_amax_tile = tk_am
        guard_y16 = guard_wo[1]
        guard_pg = tk_mu
        guard_pst = tk_ex2
        guard_py[0] = guard_py[1] = tk_ap_last
        guard_ctil = tk_spill
        tk_ctil_free = tk_spill
        tile_guard = guard_wo[1]

    if stages >= 4:
        tk_tr = s.op("tensor", lambda e: e.matmul(
            pT[0:1, 0:128], amax_col[:, :], ident[:, :], is_transpose=True,
            start=True, stop=True, skip_group_check=True),
            waits=[tk_amax_tile, tk_xdt[3]], inc="te")
        tk_m2 = s.op("vector", lambda e: e.reduce_max(
            xs_sb[0:1, 0:1], pT[0:1, 0:128], AX.X), waits=[tk_tr], inc="ve")
        tk_xs0 = s.op("scalar", lambda e: e.activation(
            xs_sb[0:1, 1:2], xs_sb[0:1, 0:1], AF.Copy, bias=0.0,
            scale=cst[:, 3:4]), waits=[tk_m2], inc="se")
        tk_xs = s.op("vector", lambda e: e.tensor_scalar_max(
            xs_sb[0:1, 1:2], xs_sb[0:1, 1:2], 1.0e-5), waits=[tk_xs0],
            inc="ve")
        tk_ccin = s.dma("sync", cc_in, xs_sb[0:1, 1:2], waits=[tk_xs],
                        inc="dci")
        tk_cc = s.op("gpsimd", lambda e: e.collective_compute(
            "AllReduce", ALU.max, replica_groups=[list(range(n_cores))],
            ins=[cc_in.opt()], outs=[cc_out.opt()]), waits=[tk_ccin],
            inc="ge")
        tk_ccout = s.dma("sync", xs_sb[0:1, 1:2], cc_out, waits=[tk_cc],
                         inc="dco")
        tk_inv = s.op("vector", lambda e: e.reciprocal(
            xs_sb[0:1, 2:3], xs_sb[0:1, 1:2]), waits=[tk_ccout], inc="ve")
        tk_rx = s.op("tensor", lambda e: e.matmul(
            pG[:, 0:1], ones_row[:, :], xs_sb[0:1, 1:2], start=True,
            stop=False, skip_group_check=True), waits=[tk_inv, tile_guard],
            inc="te")
        tk_ri = s.op("tensor", lambda e: e.matmul(
            pG[:, 1:2], ones_row[:, :], xs_sb[0:1, 2:3], start=False,
            stop=True, skip_group_check=True), inc="te")
        tk_xc = s.op("vector", lambda e: e.tensor_copy(
            xs_col[:, :], pG[:, 0:1]), waits=[tk_ri], inc="ve")
        tk_ic = s.op("vector", lambda e: e.tensor_copy(
            inv_col[:, :], pG[:, 1:2]), inc="ve")
        tk_sws = s.op("vector", lambda e: e.tensor_scalar(
            sws_c[:, :], ws_c[:, :], xs_col[:, 0:1], None, ALU.mult),
            waits=[tk_xc], inc="ve")
        wf8v = xbcc.ap().bitcast(I8)[:, 0:8192]
        wf16v = rep_all.ap().bitcast(F16)[:, 0:8192]
        tk_wf8 = s.dma("sync", wf8v,
                       w_ffnT.rearrange("(c p) n -> p c n", p=128),
                       waits=[tile_guard, prev_dbg], inc="dwf")
        tk_wf16 = s.op("vector", lambda e, wf16v=wf16v, wf8v=wf8v:
                       e.tensor_copy(wf16v, wf8v),
                       waits=[tk_wf8, guard_wo[0], guard_wo[1]], inc="ve")

        prev_rep = tile_guard
        prev_apply = tile_guard
        xbf16 = xbcc.ap().bitcast(F16)
        h1v_s = [y_buf[:, 0:8 * TP], y_buf[:, 8 * TP:16 * TP]]
        q16_s = [xbf16[:, 8 * TP:16 * TP], xbf16[:, 16 * TP:24 * TP]]
        h1_used = [tile_guard, tile_guard]
        ffn_last = [tile_guard, tile_guard]
        out_slice_prev = [None] * 8
        row_guard = guard_rows
        repA_guard = tile_guard
        p2_load = {}
        q_tok = {}

        def emit_p2_load(itx):
            par = itx % 2
            p2_load[itx] = s.dma(
                "sync", h1v_s[par].rearrange("p (c t) -> p c t", t=TP),
                h1_spill.rearrange("(c p) t -> p c t", p=128)
                [:, :, itx * TP:itx * TP + TP],
                waits=[("dh1", s.cnt["dh1"]), h1_used[par]],
                inc="dl1%d" % par)

        def emit_p2_quant_chunk(itx, c):
            par = itx % 2
            tk_u = s.op("scalar", lambda e, par=par, c=c: e.activation(
                q16_s[par][:, c * TP:(c + 1) * TP],
                h1v_s[par][:, c * TP:(c + 1) * TP], AF.Copy,
                bias=QSHIFT, scale=inv_col[:, 0:1]),
                waits=[p2_load[itx], tk_ic, ffn_last[par]], inc="se")
            tk_c2 = s.op("vector", lambda e, par=par, c=c: e.tensor_scalar(
                q16_s[par][:, c * TP:(c + 1) * TP],
                q16_s[par][:, c * TP:(c + 1) * TP],
                -QSHIFT, -128.0, ALU.add, ALU.max), waits=[tk_u], inc="ve")
            q_tok[(itx, c)] = s.op(
                "vector", lambda e, par=par, c=c: e.tensor_scalar_min(
                    q16_s[par][:, c * TP:(c + 1) * TP],
                    q16_s[par][:, c * TP:(c + 1) * TP], 127.0),
                waits=[tk_c2], inc="ve")

        emit_p2_load(0)
        for c in range(8):
            emit_p2_quant_chunk(0, c)
        prev_row_mu = tk_mu
        prev_row_ex2 = tk_ex2
        for it2 in range(NT):
            t0 = it2 * TP
            par = it2 % 2
            if it2 + 1 < NT:
                emit_p2_load(it2 + 1)
            tk_sum2 = None
            tk_sqq2 = None
            tk_h2 = []
            tk_q2bs = []
            stat_a_tok = [None] * 8

            def emit_ffn_stats(j2, it2=it2):
                a = s.op("tensor", lambda e, j2=j2: e.matmul(
                    pG[0:1, 0:TP], ones_col[:, :],
                    c_til[:, 512 * j2:512 * j2 + TP],
                    start=(j2 == 0), stop=(j2 == 7), skip_group_check=True),
                    waits=[tk_h2[j2]] +
                          ([prev_row_mu] if j2 == 0 else []), inc="te")
                b = s.op("tensor", lambda e, j2=j2: e.matmul(
                    pSt[0:1, 0:TP], ones_col[:, :],
                    mbuf[0][:, 512 * (j2 % 4):512 * (j2 % 4) + TP],
                    start=(j2 == 0), stop=(j2 == 7), skip_group_check=True),
                    waits=[tk_q2bs[j2]] +
                          ([prev_row_ex2] if j2 == 0 else []), inc="te")
                stat_a_tok[j2] = a
                return a, b
            for j2 in range(8):
                tk_mm = None
                for kc in range(8):
                    wts = []
                    if kc == 0:
                        wts.append(prev_mm_done[j2 % 2])
                        if it2 == 0 and j2 == 0:
                            wts += [tk_wf16, tk_sws]
                    if j2 == 0:
                        wts.append(q_tok[(it2, kc)])
                    tk_mm = s.op("tensor",
                                 lambda e, kc=kc, j2=j2, par=par: e.matmul(
                        pA[j2 % 2][:, :],
                        wf16v[:, kc * 1024 + j2 * 128:
                              kc * 1024 + j2 * 128 + 128],
                        q16_s[par][:, kc * TP:(kc + 1) * TP],
                        start=(kc == 0), stop=(kc == 7),
                        skip_group_check=True),
                        waits=wts, inc="te" if kc == 7 else None)
                tk_ev2 = s.op("scalar", lambda e, j2=j2: e.activation(
                    c_til[:, 512 * j2:512 * j2 + TP], pA[j2 % 2][:, :],
                    AF.Copy, bias=0.0,
                    scale=sws_c[:, j2:j2 + 1]),
                    waits=[tk_mm, out_slice_prev[j2]], inc="se")
                prev_mm_done[j2 % 2] = tk_ev2
                tk_r2 = s.op("vector", lambda e, j2=j2, par=par:
                             e.scalar_tensor_tensor(
                    c_til[:, 512 * j2:512 * j2 + TP],
                    c_til[:, 512 * j2:512 * j2 + TP],
                    fb_c[:, j2:j2 + 1],
                    h1v_s[par][:, j2 * TP:(j2 + 1) * TP], ALU.add, ALU.add),
                    waits=[tk_ev2], inc="ve")
                tk_h2.append(tk_r2)
                tk_q2b = s.op("scalar", lambda e, j2=j2: e.activation(
                    mbuf[0][:, 512 * (j2 % 4):512 * (j2 % 4) + TP],
                    c_til[:, 512 * j2:512 * j2 + TP], AF.Square, bias=0.0,
                    scale=1.0), waits=[tk_r2, tk_sqq2], inc="se")
                tk_q2bs.append(tk_q2b)
                if j2 >= 1:
                    tk_sum2, tk_sqq2 = emit_ffn_stats(j2 - 1)
                if it2 + 1 < NT and j2 >= 2:
                    emit_p2_quant_chunk(it2 + 1, j2 - 2)
                if j2 == 7:
                    ffn_last[par] = tk_mm
            if it2 + 1 < NT:
                emit_p2_quant_chunk(it2 + 1, 6)
                emit_p2_quant_chunk(it2 + 1, 7)
            tk_sum2, tk_sqq2 = emit_ffn_stats(7)
            h1_used[par] = tk_h2[-1]
            tk_mu = s.op("scalar", lambda e: e.activation(
                rowC[:, :], pG[0:1, 0:TP], AF.Copy, bias=0.0,
                scale=cst[:, 2:3]), waits=[tk_sum2, row_guard], inc="se")
            tk_ex2 = s.op("scalar", lambda e: e.activation(
                rowD[:, :], pSt[0:1, 0:TP], AF.Copy, bias=0.0,
                scale=cst[:, 2:3]), waits=[tk_sqq2], inc="se")
            tk_mu2 = s.op("vector", lambda e: e.tensor_tensor(
                rowB[:, :], rowC[:, :], rowC[:, :], ALU.mult),
                waits=[tk_mu], inc="ve")
            tk_var = s.op("vector", lambda e: e.tensor_tensor(
                rowD[:, :], rowD[:, :], rowB[:, :], ALU.subtract),
                waits=[tk_ex2, tk_mu2], inc="ve")
            tk_r10 = s.op("scalar", lambda e: e.activation(
                rowA[:, :], rowD[:, :], AF.Sqrt, bias=cst[:, 0:1],
                scale=1.0), waits=[tk_var, repA_guard], inc="se")
            tk_r1 = s.op("vector", lambda e: e.reciprocal(
                rowA[:, :], rowA[:, :]), waits=[tk_r10], inc="ve")
            tk_mr0 = s.op("vector", lambda e: e.tensor_tensor(
                rowB[:, :], rowC[:, :], rowA[:, :], ALU.mult),
                waits=[tk_r1], inc="ve")
            tk_mr = s.op("vector", lambda e: e.tensor_scalar_mul(
                rowB[:, :], rowB[:, :], -1.0), waits=[tk_mr0], inc="ve")
            tk_rep1 = s.op("tensor", lambda e: e.matmul(
                pY[0][:, 0:TP], ones_row[:, :], rowA[:, :], start=True,
                stop=True, skip_group_check=True),
                waits=[tk_r1, prev_apply], inc="te")
            tk_rep2 = s.op("tensor", lambda e: e.matmul(
                pY[1][:, 0:TP], ones_row[:, :], rowB[:, :], start=True,
                stop=True, skip_group_check=True), waits=[tk_mr], inc="te")
            prev_row_mu = tk_mu
            prev_row_ex2 = tk_ex2
            row_guard = tk_mr
            repA_guard = tk_rep1
            tk_ap = None
            for j2 in range(8):
                tk_b1 = s.op("vector", lambda e, j2=j2: e.tensor_tensor(
                    c_til[:, 512 * j2:512 * j2 + TP],
                    c_til[:, 512 * j2:512 * j2 + TP], pY[0][:, 0:TP],
                    ALU.mult),
                    waits=[tk_rep1, stat_a_tok[j2], tk_q2bs[j2]], inc="ve")
                tk_ap = s.op("vector", lambda e, j2=j2: e.tensor_tensor(
                    c_til[:, 512 * j2:512 * j2 + TP],
                    c_til[:, 512 * j2:512 * j2 + TP], pY[1][:, 0:TP],
                    ALU.add), waits=[tk_rep2, tk_b1], inc="ve")
                out_slice_prev[j2] = s.dma(
                    "gpsimd",
                    out.rearrange("(c p) t -> p c t", p=128)
                    [:, j2:j2 + 1, t0:t0 + TP],
                    c_til.ap().rearrange("p (c t) -> p c t", t=TP)
                    [:, j2:j2 + 1, :],
                    waits=[tk_ap], inc="dout%d" % j2)
            prev_apply = tk_ap
        s.op("sync", lambda e: e.nop(),
             waits=[("dout%d" % j2, s.cnt["dout%d" % j2]) for j2 in range(8)])

    if debug:
        s.op("sync", lambda e: e.nop(), waits=[("ddbg", s.cnt["ddbg"])])
    if stages <= 1 and not debug:
        tk = s.dma("sync", out[0:128, 0:TP], xbcc[:, 0:TP], waits=ech_done,
                   inc="dout")
        s.op("sync", lambda e: e.nop(), waits=[tk])

    s.emit()
    return nc, ["out"] + dbg_names


def kernel(**inputs):
    shards = host_prepare(inputs)
    nc, outs = build_program(T=8192, TP=512)
    res = run_bass_kernel_spmd(nc, shards, core_ids=list(range(8)))
    return np.stack([r["out"].T for r in res.results], axis=0)


# version 18
# speedup vs baseline: 1.0113x; 1.0113x over previous
import numpy as np
import concourse.bass as bass
import concourse.mybir as mybir
from concourse.bass_utils import run_bass_kernel_spmd

F32 = mybir.dt.float32
F16 = mybir.dt.float16
I8 = mybir.dt.int8
AF = mybir.ActivationFunctionType
ALU = mybir.AluOpType
AX = mybir.AxisListType

D_MODEL = 1024
D_INNER = 2048
D_STATE = 16
NHEADS = 32
HEADDIM = 64
CONV_DIM = 2080
EPROJ = 4224
NECH_A = 17
NECH_Z = 16
BIG = 6.0e4
QSHIFT = 1536.0


ENGINES = ("sync", "tensor", "vector", "scalar", "gpsimd")


class Sched:

    def __init__(self, nc):
        self.nc = nc
        self.ops = {e: [] for e in ENGINES}
        self.sems = {}
        self.cnt = {}

    def _sem(self, name):
        if name not in self.sems:
            self.sems[name] = self.nc.alloc_semaphore(name=name)
            self.cnt[name] = 0
        return self.sems[name]

    def op(self, engine, fn, waits=(), inc=None, n=1):
        tok = None
        if inc is not None:
            self._sem(inc)
            self.cnt[inc] += n
            tok = (inc, self.cnt[inc])
        for w in waits:
            if w is not None:
                self._sem(w[0])
        self.ops[engine].append((tuple(w for w in waits if w is not None),
                                 fn, tok, n))
        return tok

    def dma(self, engine, out, in_, waits=(), inc="dma"):
        return self.op(engine,
                       lambda e, o=out, i=in_: e.dma_start(out=o, in_=i),
                       waits=waits, inc=inc, n=16)

    def emit(self):
        with self.nc.Block() as block:
            def mk(engine):
                ops = self.ops[engine]
                def body(eng):
                    for waits, fn, tok, n in ops:
                        for s, v in waits:
                            eng.wait_ge(self.sems[s], v)
                        instr = fn(eng)
                        if tok is not None:
                            if instr is None:
                                raise RuntimeError("op with inc returned None")
                            instr.then_inc(self.sems[tok[0]], n)
                return body
            for e in ENGINES:
                if self.ops[e]:
                    getattr(block, e)(mk(e))

    def barrier(self, name, engines=ENGINES):
        toks = []
        for e in engines:
            toks.append(self.op(e, lambda eng: eng.nop(), inc="bar_" + name))
        for e in engines:
            self.op(e, lambda eng: eng.nop(), waits=toks)


def _r(t, *idx):
    return t[idx]


def build_consts():
    c = {}
    c["ident"] = np.eye(128, dtype=np.float32)
    sel = np.zeros((2, 128), np.float32)
    sel[0, :64] = 1.0
    sel[1, 64:] = 1.0
    c["dseg_sel"] = sel
    oh = np.zeros((63, 128), np.float32)
    for j in range(1, 64):
        oh[j - 1, j] = 1.0
        oh[j - 1, 64 + j] = 1.0
    c["dseg_mask_lhs"] = oh
    rhs_oh = np.zeros((32, 2048), np.float32)
    for h in range(32):
        rhs_oh[h, h * 64:(h + 1) * 64] = 1.0
    rhs_mask = np.zeros((63, 2048), np.float32)
    for j in range(1, 64):
        row = np.zeros(64, np.float32)
        row[:j] = -BIG
        rhs_mask[j - 1] = np.tile(row, 32)
    c["dseg_rhs_const"] = np.concatenate([rhs_oh, rhs_mask], 0)
    c["ones32_c"] = np.ones((32, 64), np.float32)
    return c


def _pad_conv(a):
    out = np.zeros((2176,) + a.shape[1:], np.float32)
    out[0:2048] = a[0:2048]
    out[2048:2064] = a[2048:2064]
    out[2080:2096] = a[2064:2080]
    return out


def host_prepare(inputs):
    x = np.ascontiguousarray(inputs["L0_attn_x"])
    W_in = np.asarray(inputs["L0_attn_W_in"])
    conv_w = np.asarray(inputs["L0_attn_conv_w"])
    conv_b = np.asarray(inputs["L0_attn_conv_b"])
    W_out = np.asarray(inputs["L0_attn_W_out"])
    w8 = np.asarray(inputs["L0_ffn_weight_int8"])

    W_inT = np.zeros((D_MODEL, EPROJ), np.float32)
    W_inT[:, 0:2048] = W_in[D_INNER:D_INNER + 2048].T
    W_inT[:, 2048:2064] = W_in[D_INNER + 2048:D_INNER + 2064].T
    W_inT[:, 2080:2096] = W_in[D_INNER + 2064:D_INNER + 2080].T
    W_inT[:, 2112:2144] = W_in[D_INNER + CONV_DIM:].T
    W_inT[:, 2176:2176 + D_INNER] = W_in[:D_INNER].T

    shard_common = {
        "w_inT": W_inT,
        "w_outT": np.ascontiguousarray(W_out.T),
        "w_ffnT": np.ascontiguousarray(w8.T),
        "conv_w": _pad_conv(conv_w),
        "conv_b": _pad_conv(conv_b.reshape(-1, 1))[:, 0],
        "dt_bias": np.asarray(inputs["L0_attn_dt_bias"], np.float32),
        "A_log": np.asarray(inputs["L0_attn_A_log"], np.float32),
        "D_vec": np.asarray(inputs["L0_attn_D"], np.float32),
        "norm_w": np.asarray(inputs["L0_attn_norm_w"], np.float32),
        "w_scale": np.asarray(inputs["L0_ffn_weight_scale"], np.float32)[:, 0],
        "f_bias": np.asarray(inputs["L0_ffn_bias"], np.float32),
        **build_consts(),
    }
    return [dict(shard_common, xT=np.ascontiguousarray(x[b].T))
            for b in range(x.shape[0])]


def build_program(T=8192, TP=512, stages=99, debug=False, n_cores=8):
    assert T % TP == 0 and TP % 128 == 0
    NT = T // TP
    NPAIR = TP // 128
    nc = bass.Bass("TRN2", target_bir_lowering=False)

    def din(name, shape, dtype=F32):
        return nc.dram_tensor(name, shape, dtype, kind="ExternalInput").ap()

    xT = din("xT", [D_MODEL, T])
    w_inT = din("w_inT", [D_MODEL, EPROJ])
    w_outT = din("w_outT", [D_INNER, D_MODEL])
    w_ffnT = din("w_ffnT", [D_MODEL, D_MODEL], I8)
    conv_w = din("conv_w", [2176, 4])
    conv_b = din("conv_b", [2176])
    dt_bias = din("dt_bias", [32])
    A_log = din("A_log", [32])
    D_vec = din("D_vec", [32])
    norm_w = din("norm_w", [D_INNER])
    w_scale = din("w_scale", [D_MODEL])
    f_bias = din("f_bias", [D_MODEL])
    ident_in = din("ident", [128, 128])
    dseg_sel = din("dseg_sel", [2, 128])
    dseg_mask_lhs = din("dseg_mask_lhs", [63, 128])
    dseg_rhs_const = din("dseg_rhs_const", [95, 2048])
    ones32_in = din("ones32_c", [32, 64])

    out = nc.dram_tensor("out", [D_MODEL, T], F32, kind="ExternalOutput").ap()
    h1_spill = nc.dram_tensor("h1_spill", [D_MODEL, T], F32).ap()
    w16_in = nc.dram_tensor("w16_in", [128, 33 * 2048], F16).ap()
    w16_out = nc.dram_tensor("w16_out", [128, 16 * 2048], F16).ap()
    cc_in = nc.dram_tensor("cc_in", [1, 1], F32).ap()
    cc_out = nc.dram_tensor("cc_out", [1, 1], F32).ap()

    dbg_names = []
    dbg = {}
    if debug:
        specs = [("dbg_xbcc", [128, NECH_A * T]), ("dbg_dt", [32, T]),
                 ("dbg_acs", [32, T]), ("dbg_ybuf", [128, 16 * T]),
                 ("dbg_h1pre", [128, 8 * T])]
        for name, shape in specs:
            dbg[name] = nc.dram_tensor(name, shape, F32,
                                       kind="ExternalOutput").ap()
            dbg_names.append(name)

    s = Sched(nc)

    sbh = {}
    def sbuf(name, shape, dtype=F32):
        sbh[name] = nc.alloc_sbuf_tensor(name, shape, dtype)
        return sbh[name]

    x16h = sbuf("x16h", [128, 8 * TP], F16)
    x16l = sbuf("x16l", [128, 8 * TP], F16)
    xstg = sbuf("xstg", [128, 2 * TP])
    wbufA = sbuf("wbufA", [128, 2048 * 2], F16)
    raw = sbuf("raw", [128, 2 * (TP + 3)])
    hist = sbuf("hist", [128, NECH_A * 3])
    xbcc = sbuf("xbcc", [128, NECH_A * TP])
    dtv = sbuf("dtv", [32, TP])
    adt = sbuf("adt", [32, TP])
    acs = sbuf("acs", [32, TP])
    acsn = sbuf("acsn", [32, TP])
    convw = sbuf("convw", [128, NECH_A * 4])
    convb = sbuf("convb", [128, NECH_A])
    dtb = sbuf("dtb", [32, 1])
    avec = sbuf("avec", [32, 1])
    ones32 = sbuf("ones32", [32, 64])
    expa_fm = sbuf("expa_fm", [32, TP])
    dt_tm = sbuf("dt_tm", [128, NPAIR * 32])
    acs_tm = sbuf("acs_tm", [128, NPAIR * 32])
    dec_tm = sbuf("dec_tm", [128, NPAIR * 32])
    alastT = sbuf("alastT", [2, NPAIR * 32])
    rep_all = sbuf("rep_all", [128, 8 * TP + 64])
    c0 = sbuf("c0", [16, TP])
    c_rep4 = sbuf("c_rep4", [128, TP])
    c_til = sbuf("c_til", [128, 8 * TP])
    _bpad = sbuf("bpad0", [128, 1024])
    _xdt = sbuf("xdt0", [128, 2048])
    _mbuf = sbuf("mbuf0", [128, 2048])
    _l97 = sbuf("l97_0", [97, 128])
    _r97 = sbuf("r97_0", [97, 2048])
    bpad, xdt, mbuf, l97, r97 = ([_bpad] * 2, [_xdt] * 2, [_mbuf] * 2,
                                 [_l97] * 2, [_r97] * 2)
    sring = [sbuf("sring0", [128, 512]), sbuf("sring1", [128, 512]),
             sbuf("sring2", [128, 512])]
    y_buf = sbuf("y_buf", [128, 16 * TP])
    wffn16 = None
    ident = sbuf("ident_sb", [128, 128])
    ident16 = sbuf("ident16", [128, 128], F16)
    y16h = None
    y16l = None
    dt_sbuf = sbuf("dt_col", [128, 16])
    dpair = sbuf("dpair", [2, 16])
    rowA = sbuf("rowA", [1, TP])
    rowB = sbuf("rowB", [1, TP])
    rowC = sbuf("rowC", [1, TP])
    rowD = sbuf("rowD", [1, TP])
    amax_col = sbuf("amax_col", [128, 1])
    red_col = sbuf("red_col", [128, 1])
    ones_col = sbuf("ones_col", [128, 1])
    ones_row = sbuf("ones_row", [1, 128])
    normw_c = sbuf("normw_c", [128, 16])
    ws_c = sbuf("ws_c", [128, 8])
    fb_c = sbuf("fb_c", [128, 8])
    xs_sb = sbuf("xs_sb", [1, 4])
    cst = sbuf("cst", [1, 4])
    xs_col = sbuf("xs_col", [128, 1])
    inv_col = sbuf("inv_col", [128, 1])
    sws_c = sbuf("sws_c", [128, 8])

    y16h = xbcc.ap().bitcast(F16)[:, 0:16 * TP]
    y16l = xbcc.ap().bitcast(F16)[:, 16 * TP:32 * TP]

    psh = {}
    def psum(name, shape, dtype=F32):
        psh[name] = nc.alloc_psum_tensor(name, shape, dtype)
        return psh[name]

    pA = [psum("pA0", [128, TP]), psum("pA1", [128, TP])]
    pT = psum("pT", [128, 512])
    pG = psum("pG", [128, 512])
    pY = [psum("pY0", [128, 512]), psum("pY1", [128, 512])]
    pSt = psum("pSt", [128, 512])
    pT2 = psum("pT2", [128, 512])

    tk_boot = []
    tk_boot.append(s.op("sync", lambda e: e.dma_start(
        out=convb[:, :], in_=conv_b.rearrange("(c p) -> p c", p=128),
        allow_slow_non_contiguous=True), inc="dboot", n=16))
    tk_boot.append(s.dma("sync", convw.ap().rearrange("p (c k) -> p c k", k=4),
                         conv_w.rearrange("(c p) k -> p c k", p=128), inc="dboot"))
    tk_boot.append(s.dma("sync", dtb[:, :], dt_bias.unsqueeze(1), inc="dboot"))
    tk_boot.append(s.dma("sync", avec[:, :], A_log.unsqueeze(1), inc="dboot"))
    tk_boot.append(s.dma("sync", ones32[:, :], ones32_in, inc="dboot"))
    tk_ident = s.dma("sync", ident[:, :], ident_in, inc="dboot")
    s.dma("sync", l97[0][0:2, :], dseg_sel, inc="dboot")
    s.dma("sync", l97[0][34:97, :], dseg_mask_lhs, inc="dboot")
    s.dma("sync", r97[0][2:97, :], dseg_rhs_const, inc="dboot")
    tk_dp = s.op("sync", lambda e: e.dma_start(
        out=dpair[:, :], in_=D_vec.rearrange("(k two) -> two k", two=2),
        allow_slow_non_contiguous=True), inc="dboot", n=16)
    s.op("sync", lambda e: e.dma_start(
        out=normw_c[:, :], in_=norm_w.rearrange("(c p) -> p c", p=128),
        allow_slow_non_contiguous=True), inc="dboot", n=16)
    s.op("sync", lambda e: e.dma_start(
        out=ws_c[:, :], in_=w_scale.rearrange("(c p) -> p c", p=128),
        allow_slow_non_contiguous=True), inc="dboot", n=16)
    s.op("sync", lambda e: e.dma_start(
        out=fb_c[:, :], in_=f_bias.rearrange("(c p) -> p c", p=128),
        allow_slow_non_contiguous=True), inc="dboot", n=16)
    tk_bootall2 = ("dboot", s.cnt["dboot"])
    t = s.op("scalar", lambda e: e.activation(avec[:, :], avec[:, :], AF.Exp,
                                              bias=0.0, scale=1.0),
             waits=[tk_bootall2], inc="se")
    t_avec = s.op("vector",
                  lambda e: e.tensor_scalar_mul(avec[:, :], avec[:, :], -1.0),
                  waits=[t], inc="ve")
    t_hist = s.op("vector", lambda e: e.memset(hist[:, :], 0.0), inc="ve")
    s.op("vector", lambda e: e.memset(xbcc[:, :], 0.0), inc="ve")
    tk_dmm = s.op("tensor", lambda e: e.matmul(
        pT[:, 0:16], l97[0][0:2, :], dpair[:, :], start=True, stop=True),
        waits=[tk_bootall2], inc="te")
    tk_dt_sb = s.op("vector", lambda e: e.tensor_copy(dt_sbuf[:, :], pT[:, 0:16]),
                    waits=[tk_dmm], inc="ve")
    for b in {id(b): b for b in bpad}.values():
        s.op("vector", lambda e, b=b: e.memset(b[:, :], 0.0), inc="ve")
    for sr in sring:
        s.op("vector", lambda e, sr=sr: e.memset(sr[:, :], 0.0), inc="ve")
    s.op("vector", lambda e: e.memset(rep_all[:, :], 0.0), inc="ve")
    s.op("vector", lambda e: e.memset(c_rep4[:, :], 0.0), inc="ve")
    s.op("vector", lambda e: e.memset(ones_col[:, :], 1.0), inc="ve")
    s.op("vector", lambda e: e.memset(ones_row[:, :], 1.0), inc="ve")
    s.op("vector", lambda e: e.memset(amax_col[:, :], 0.0), inc="ve")
    s.op("vector", lambda e: e.memset(cst[:, 0:1], 1.0e-5), inc="ve")
    s.op("vector", lambda e: e.memset(cst[:, 1:2], 1.0 / 2048.0), inc="ve")
    s.op("vector", lambda e: e.memset(cst[:, 2:3], 1.0 / 1024.0), inc="ve")
    s.op("vector", lambda e: e.memset(cst[:, 3:4], 1.0 / 127.0), inc="ve")

    s.op("vector", lambda e: e.tensor_copy(ident16[:, :], ident[:, :]),
         waits=[tk_bootall2], inc="ve")
    xprep = {"prev": None, "tok": None}
    def emit_xprep_chunk(itx, dc, xfree):
        t0x = itx * TP
        tk_xd = s.dma("sync", xstg[:, (dc % 2) * TP:(dc % 2) * TP + TP],
                      xT.rearrange("(c p) t -> p c t", p=128)
                      [:, dc, t0x:t0x + TP],
                      waits=[xprep["prev"]], inc="dx%d" % (dc % 2))
        tk_xh = s.op("vector", lambda e, dc=dc: e.tensor_copy(
            x16h[:, dc * TP:(dc + 1) * TP],
            xstg[:, (dc % 2) * TP:(dc % 2) * TP + TP]),
            waits=[tk_xd, xfree], inc="ve")
        tk_xl = s.op("vector", lambda e, dc=dc: e.tensor_tensor(
            x16l[:, dc * TP:(dc + 1) * TP],
            xstg[:, (dc % 2) * TP:(dc % 2) * TP + TP],
            x16h[:, dc * TP:(dc + 1) * TP], ALU.subtract),
            waits=[tk_xh], inc="vxs")
        xprep["prev"] = tk_xl
        xprep["tok"] = tk_xl
    for dc in range(8):
        emit_xprep_chunk(0, dc, None)
    wst = mbuf[0]
    wsp = rep_all.ap().bitcast(F16)
    wsplit_in_tok = {}
    wsplit_out_tok = {}
    _win_order = [16] + list(range(16)) + list(range(17, 33))
    _wsq = ([(1024, w_inT, w16_in, e8, wsplit_in_tok) for e8 in _win_order] +
            [(2048, w_outT, w16_out, e8, wsplit_out_tok) for e8 in range(8)])
    wsplit_state = {"n": 0, "lo": None, "st": [None, None]}

    def emit_wsplit_chunk():
        n = wsplit_state["n"]
        if n >= len(_wsq):
            return
        wid, wsrc, wdst, e8, toks = _wsq[n]
        sl = n % 2
        spo = sl * 4096
        tk_ld = s.dma("gpsimd", wst[:, 0:wid],
                      wsrc.rearrange("(c p) n -> p c n", p=128)
                      [:, :, e8 * 128:(e8 + 1) * 128],
                      waits=[wsplit_state["lo"]], inc="dwsp")
        tk_h = s.op("vector", lambda e, wid=wid, spo=spo: e.tensor_copy(
            wsp[:, spo:spo + wid], wst[:, 0:wid]),
            waits=[tk_ld, wsplit_state["st"][sl]], inc="ve")
        tk_l = s.op("vector", lambda e, wid=wid, spo=spo: e.tensor_tensor(
            wsp[:, spo + wid:spo + 2 * wid], wst[:, 0:wid],
            wsp[:, spo:spo + wid], ALU.subtract), waits=[tk_h], inc="vws")
        tk_st = s.dma("gpsimd", wdst[:, e8 * 2 * wid:(e8 + 1) * 2 * wid],
                      wsp[:, spo:spo + 2 * wid], waits=[tk_l],
                      inc="dwsp2%d" % sl)
        toks[e8] = tk_st
        wsplit_state["lo"] = tk_l
        wsplit_state["st"][sl] = tk_st
        wsplit_state["n"] = n + 1
    t_boot_done = s.op("vector", lambda e: e.nop(),
                       waits=[tk_dt_sb], inc="ve")
    for _ in range(3):
        emit_wsplit_chunk()
    t_crep_init = t_boot_done

    raw_guard = [t_hist, t_hist]
    prev_mm_done = [None, None]
    prev_wload = [None, None]

    dbg_toks = []
    tile_guard = None
    prev_dbg = None
    guard_pt = t_boot_done
    guard_expa = None
    guard_stg = t_boot_done
    guard_alast = None
    guard_c0 = None
    guard_wo = [None, None]
    guard_y16 = None
    prev_rep = None
    prev_apply = None
    tk_ctil_free = None
    tk_spill = None
    tk_amax_tile = None
    guard_repall = None
    guard_crep = t_crep_init
    guard_ctil = None
    guard_wave0 = None
    guard_scan0 = None
    guard_pa = [None, None]
    guard_bpad = [t_boot_done, t_boot_done]
    guard_xdt = [None, None]
    guard_m = [t_boot_done, t_boot_done]
    guard_l97 = [tk_bootall2, tk_bootall2]
    guard_py = [None, None]
    guard_g = None
    guard_pg = None
    guard_pst = None
    guard_rows = None
    tk_xdt_g = t_boot_done
    tk_xdt_g2 = [t_boot_done, t_boot_done]
    prepped = False

    for it in range(NT):
        t0 = it * TP
        tk_x = xprep["tok"]
        def emit_A_chunk(it, idx, e, tile_guard, prev_dbg, tk_x):
            wslot = idx % 2
            tk_w = s.dma("sync",
                         wbufA[:, wslot * 2048:(wslot + 1) * 2048],
                         w16_in[:, e * 2048:(e + 1) * 2048],
                         waits=[prev_wload[wslot],
                                wsplit_in_tok[e] if it == 0 else None],
                         inc="dw%d" % wslot)
            tk_mm = None
            nmm = 0
            for k in range(8):
                wh = wbufA[:, wslot * 2048 + k * 128:
                           wslot * 2048 + (k + 1) * 128]
                wl = wbufA[:, wslot * 2048 + 1024 + k * 128:
                           wslot * 2048 + 1024 + (k + 1) * 128]
                for lhs, rhs in ((wh, x16h), (wh, x16l), (wl, x16h)):
                    tk_mm = s.op(
                        "tensor",
                        lambda eng, k=k, idx=idx, lhs=lhs, rhs=rhs, nmm=nmm:
                        eng.matmul(
                            pA[idx % 2][:, :], lhs,
                            rhs[:, k * TP:(k + 1) * TP],
                            start=(nmm == 0), stop=(nmm == 23)),
                        waits=([tk_w, tk_x, prev_mm_done[idx % 2]]
                               if nmm == 0 else ()),
                        inc="te" if nmm == 23 else None)
                    nmm += 1
            prev_wload[wslot] = tk_mm

            rslot = idx % 2
            roff = rslot * (TP + 3)
            if e < NECH_A - 1 or True:
                nrows = 128 if e < 16 else 48
                tk_h = s.op("vector",
                            lambda eng, e=e, roff=roff, nrows=nrows:
                            eng.tensor_copy(raw[0:nrows, roff:roff + 3],
                                            hist[0:nrows, e * 3:e * 3 + 3]),
                            waits=[raw_guard[rslot], tk_mm], inc="ve")
                tk_ev = s.op("scalar",
                             lambda eng, idx=idx, roff=roff, nrows=nrows:
                             eng.copy(raw[0:nrows, roff + 3:roff + 3 + TP],
                                      pA[idx % 2][0:nrows, :]),
                             waits=[tk_mm, raw_guard[rslot], tk_h], inc="se")
                prev_mm_done[idx % 2] = tk_ev if e < 16 else None
                tk_hs = s.op("vector",
                             lambda eng, e=e, roff=roff, nrows=nrows:
                             eng.tensor_copy(hist[0:nrows, e * 3:e * 3 + 3],
                                             raw[0:nrows, roff + TP:roff + TP + 3]),
                             waits=[tk_ev], inc="ve")
                ac = (0, e * TP)
                tkt = s.op("vector",
                           lambda eng, e=e, roff=roff, nrows=nrows:
                           eng.tensor_scalar(
                               xbcc[0:nrows, e * TP:e * TP + TP],
                               raw[0:nrows, roff:roff + TP],
                               convw[0:nrows, e * 4:e * 4 + 1], None, ALU.mult),
                           waits=[tk_ev, tile_guard, prev_dbg], inc="ve")
                for k in (1, 2, 3):
                    tkt = s.op("vector",
                               lambda eng, e=e, k=k, roff=roff, nrows=nrows:
                               eng.scalar_tensor_tensor(
                                   xbcc[0:nrows, e * TP:e * TP + TP],
                                   raw[0:nrows, roff + k:roff + k + TP],
                                   convw[0:nrows, e * 4 + k:e * 4 + k + 1],
                                   xbcc[0:nrows, e * TP:e * TP + TP],
                                   ALU.mult, ALU.add),
                               waits=[tkt], inc="ve")
                raw_guard[rslot] = tkt
                tk_si = s.op("scalar",
                             lambda eng, e=e, nrows=nrows: eng.activation(
                                 xbcc[0:nrows, e * TP:e * TP + TP],
                                 xbcc[0:nrows, e * TP:e * TP + TP],
                                 AF.Silu, bias=convb[0:nrows, e:e + 1],
                                 scale=1.0),
                             waits=[tkt], inc="se")
                ech_done.append(tk_si)
                ech_map[e] = tk_si
            if e == 16:
                tk_e1 = s.op("scalar",
                             lambda eng, idx=idx: eng.activation(
                                 dtv[:, :], pA[idx % 2][64:96, :], AF.Exp,
                                 bias=dtb[:, :], scale=1.0),
                             waits=[tk_mm, tile_guard, prev_dbg], inc="se")
                prev_mm_done[idx % 2] = tk_e1
                tk_dt = s.op("scalar",
                             lambda eng: eng.activation(
                                 dtv[:, :], dtv[:, :], AF.Ln,
                                 bias=1.0, scale=1.0),
                             waits=[tk_e1], inc="se")
                tk_adt = s.op("vector",
                              lambda eng: eng.tensor_scalar(
                                  adt[:, :], dtv[:, :], avec[:, :], None,
                                  ALU.mult),
                              waits=[tk_dt, t_avec], inc="ve")
                tk_sc = []
                for cch in range(TP // 64):
                    tk_sc.append(s.op(
                        "vector",
                        lambda eng, cch=cch: eng.tensor_tensor_scan(
                            acs[:, cch * 64:(cch + 1) * 64],
                            ones32[:, :],
                            adt[:, cch * 64:(cch + 1) * 64],
                            0.0, ALU.mult, ALU.add),
                        waits=[tk_adt, tile_guard, prev_dbg] if cch == 0
                              else (), inc="ve"))
                tk_acsn = s.op("vector",
                               lambda eng: eng.tensor_scalar_mul(
                                   acsn[:, :], acs[:, :], -1.0),
                               waits=[tk_sc[-1]], inc="ve")
                ech_done.extend([tk_dt, tk_acsn])
                pp["dt"] = tk_dt
                pp["acsn"] = tk_acsn


        if not prepped:
            ech_done = []
            ech_map = {}
            pp = {}
        def emit_ssd_ab():
            tk_dt_tok = pp["dt"]
            tk_acsn_tok = pp["acsn"]
            tk_expa = s.op("scalar", lambda e: e.activation(
                expa_fm[:, :], acs[:, :], AF.Exp, bias=0.0, scale=1.0),
                waits=[tk_acsn_tok, guard_expa], inc="se")
            pp["expa"] = tk_expa
            tts = []
            for p4 in range(NPAIR):
                tts.append(s.op("tensor", lambda e, p4=p4: e.matmul(
                    pT[:, 32 * p4:32 * p4 + 32],
                    dtv[:, 128 * p4:128 * p4 + 128], ident[0:32, 0:32],
                    is_transpose=True, start=(p4 == 0), stop=False,
                    skip_group_check=True),
                    waits=[tk_dt_tok, tk_bootall2, guard_pt] if p4 == 0
                          else (),
                    inc="te"))
                tts.append(s.op("tensor", lambda e, p4=p4: e.matmul(
                    pT[:, 128 + 32 * p4:128 + 32 * p4 + 32],
                    acs[:, 128 * p4:128 * p4 + 128], ident[0:32, 0:32],
                    is_transpose=True, start=False, stop=(p4 == NPAIR - 1),
                    skip_group_check=True),
                    waits=[tk_acsn_tok] if p4 == 0 else (), inc="te"))
            tk_dttm = s.op("vector", lambda e: e.tensor_copy(
                dt_tm[:, :], pT[:, 0:32 * NPAIR]), waits=[tts[-1]], inc="ve")
            tk_acstm = s.op("vector", lambda e: e.tensor_copy(
                acs_tm[:, :], pT[:, 128:128 + 32 * NPAIR]), inc="ve")
            pp["dttm"] = tk_dttm
            s.dma("gpsimd", alastT[0:1, :], acs_tm[63:64, :],
                  waits=[tk_acstm, guard_alast], inc="dal")
            tk_alast = s.dma("gpsimd", alastT[1:2, :], acs_tm[127:128, :],
                             inc="dal")
            tk_selmm = []
            for p4 in range(NPAIR):
                tk_selmm.append(s.op("tensor", lambda e, p4=p4: e.matmul(
                    pT[:, 256 + 32 * p4:256 + 32 * p4 + 32], l97[0][0:2, :],
                    alastT[:, 32 * p4:32 * p4 + 32], start=(p4 == 0),
                    stop=(p4 == NPAIR - 1), skip_group_check=True),
                    waits=[tk_alast] if p4 == 0 else (), inc="te"))
            pp["selmm_last"] = tk_selmm[-1]
            tk_dpre = s.op("vector", lambda e: e.tensor_tensor(
                dec_tm[:, :], pT[:, 256:256 + 32 * NPAIR], acs_tm[:, :],
                ALU.subtract), waits=[tk_selmm[-1]], inc="ve")
            tk_dec = s.op("scalar", lambda e: e.activation(
                dec_tm[:, :], dec_tm[:, :], AF.Exp, bias=0.0, scale=1.0),
                waits=[tk_dpre], inc="se")
            pp["dpre"] = tk_dpre
            pp["dec"] = tk_dec

        def emit_ssd_c():
            tk_expa = pp["expa"]
            lin_toks = []
            for hq in range(4):
                dst = bass.AP(rep_all, hq * (8 * TP + 64),
                              [[8 * TP + 64, 1], [512, 8], [64, TP // 64],
                               [1, 64]])
                lin_toks.append(s.dma("gpsimd", dst,
                                      expa_fm[8 * hq:8 * hq + 8, :],
                                      waits=[tk_expa, guard_stg],
                                      inc="dlin"))
                dstl = bass.AP(rep_all, hq * (8 * TP + 64) + 8 * TP,
                               [[8 * TP + 64, 1], [8, 8], [1, TP // 64]])
                lin_toks.append(s.op("gpsimd", lambda e, dstl=dstl, hq=hq:
                                e.dma_start(out=dstl,
                                            in_=expa_fm[8 * hq:8 * hq + 8,
                                                        63:TP:64],
                                            allow_slow_non_contiguous=True),
                                waits=[guard_stg], inc="dlin", n=16))
            tk_place = s.dma("gpsimd", rep_all[32:128:32, :],
                             rep_all[1:4, :],
                             waits=[lin_toks[-1]], inc="dplace")
            tk_shuf = s.op("vector", lambda e: e.stream_shuffle(
                rep_all[:, :], rep_all[:, :], [0] * 32),
                waits=[tk_place, guard_repall], inc="ve")
            pp["place"] = tk_place
            pp["shuf"] = tk_shuf

        def emit_ssd_de():
            tk_c0 = s.dma("sync", c0[:, :],
                          xbcc[32:48, 16 * TP:17 * TP],
                          waits=[ech_map[16], guard_c0], inc="dc0")
            crep_toks = []
            for hq in range(4):
                crep_toks.append(s.dma(
                    "sync", c_rep4[32 * hq:32 * hq + 16, :],
                    xbcc[32:48, 16 * TP:17 * TP],
                    waits=[ech_map[16], guard_crep], inc="dcrep"))
            in0 = bass.AP(c_rep4, 0,
                          [[TP, 128], [0, 8], [64, TP // 64], [1, 64]])
            tk_ctil = s.op("vector", lambda e, in0=in0: e.tensor_tensor(
                c_til[:, :], in0, rep_all[:, 0:8 * TP], ALU.mult),
                waits=[pp["shuf"], crep_toks[-1], guard_ctil], inc="ve")
            pp["c0"] = tk_c0
            pp["ctil"] = tk_ctil

        for idx, e in enumerate([NECH_A - 1] + list(range(NECH_A - 1))):
            if idx == 0 and prepped:
                continue
            if it > 0 and not prepped and stages >= 2:
                if idx == 6:
                    emit_ssd_ab()
                if idx == 9:
                    emit_ssd_c()
                if idx == 11:
                    emit_ssd_de()
            if prepped and idx == 2:
                emit_ssd_de()
            if it == 0:
                emit_wsplit_chunk()
                emit_wsplit_chunk()
                if idx % 2 == 1:
                    emit_wsplit_chunk()
            emit_A_chunk(it, idx, e, tile_guard, prev_dbg, tk_x)

        if it == 0:
            while wsplit_state["n"] < len(_wsq):
                emit_wsplit_chunk()
            tk_wsplit_done = s.op("vector", lambda e: e.nop(),
                                  waits=[wsplit_state["st"][0],
                                         wsplit_state["st"][1]], inc="vws")
            guard_stg = tk_wsplit_done
            guard_m[0] = guard_m[1] = tk_wsplit_done
            if stages >= 2:
                emit_ssd_ab()
                emit_ssd_c()
                emit_ssd_de()

        if debug:
            s.dma("sync", dbg["dbg_xbcc"][:, it * NECH_A * TP:(it + 1) * NECH_A * TP],
                  xbcc[:, :], waits=ech_done, inc="ddbg")
            s.dma("sync", dbg["dbg_dt"][:, t0:t0 + TP],
                  dtv[:, :], waits=ech_done, inc="ddbg")
            tk = s.dma("sync", dbg["dbg_acs"][:, t0:t0 + TP],
                       acs[:, :], waits=ech_done, inc="ddbg")
            dbg_toks = [("ddbg", s.cnt["ddbg"])]
            prev_dbg = dbg_toks[0]
            if stages == 1:
                tile_guard = dbg_toks[0]

        if stages <= 1:
            if it + 1 < NT:
                for dc in range(8):
                    emit_xprep_chunk(it + 1, dc, ech_done[-1])
            continue

        prepped = False
        tk_dttm = pp["dttm"]
        tk_dec = pp["dec"]
        tk_dpre = pp["dpre"]
        tk_shuf = pp["shuf"]
        tk_ctil = pp["ctil"]
        tk_c0 = pp["c0"]
        tk_acsn_tok = pp["acsn"]

        wave_tok = guard_wave0
        scan_tok = guard_scan0
        for p4 in range(NPAIR):
            par = p4 % 2
            g0 = it * (TP // 64) + 2 * p4
            toff = 128 * p4

            tk_btr = s.op("tensor", lambda e, toff=toff: e.matmul(
                pSt[:, 0:16],
                xbcc[0:16, 16 * TP + toff:16 * TP + toff + 128],
                ident[0:16, 0:16], is_transpose=True, start=True, stop=True,
                skip_group_check=True),
                waits=[ech_map[16], guard_bpad[par], tk_dec, scan_tok,
                       guard_pst if p4 == 0 else None],
                inc="te")
            in_b = bass.AP(pSt, 0, [[512, 128], [0, 32], [1, 16]])
            in_d = bass.AP(dec_tm, 32 * p4, [[NPAIR * 32, 128], [1, 32], [0, 16]])
            out_b = bass.AP(bpad[par], 0, [[1024, 128], [32, 32], [1, 16]])
            tk_bpad = s.op("vector", lambda e, out_b=out_b, in_b=in_b,
                           in_d=in_d: e.tensor_tensor(
                out_b, in_b, in_d, ALU.mult), waits=[tk_btr], inc="ve")

            tk_l97 = s.dma("sync", l97[par][2:34, :],
                           acsn[:, toff:toff + 128],
                           waits=[guard_l97[par], tk_acsn_tok], inc="dl97")
            tk_r97a = s.dma("sync", r97[par][0:1, 0:2048],
                            acs[:, toff:toff + 64],
                            waits=[guard_l97[par]], inc="dr97")
            tk_r97b = s.dma("sync", r97[par][1:2, 0:2048],
                            acs[:, toff + 64:toff + 128], inc="dr97")

            tk_xdt = []
            for grp in range(4):
                pTx = pT if grp % 2 == 0 else pT2
                gt = []
                for kk in range(4):
                    kf = 4 * grp + kk
                    gt.append(s.op("tensor", lambda e, kf=kf, kk=kk, toff=toff,
                              pTx=pTx: e.matmul(
                                  pTx[:, 128 * kk:128 * kk + 128],
                                  xbcc[:, kf * TP + toff:kf * TP + toff + 128],
                                  ident[:, :], is_transpose=True,
                                  start=(kk == 0), stop=(kk == 3),
                                  skip_group_check=True),
                              waits=[ech_map[kf]] +
                                    (([guard_xdt[par], tk_xdt_g2[grp % 2]] +
                                      ([tk_dpre, tk_bpad] if grp == 0
                                       else []))
                                     if kk == 0 else []),
                              inc="te"))
                in_dt = bass.AP(dt_tm, 32 * p4 + 8 * grp,
                                [[NPAIR * 32, 128], [1, 8], [0, 64]])
                tk_g = s.op("vector", lambda e, grp=grp, par=par,
                            in_dt=in_dt, pTx=pTx: e.tensor_tensor(
                    xdt[par][:, 512 * grp:512 * grp + 512], pTx[:, :],
                    in_dt, ALU.mult), waits=[gt[-1], tk_dttm], inc="ve")
                tk_xdt.append(tk_g)
                tk_xdt_g2[grp % 2] = tk_g
                tk_xdt_g = tk_g
            tk_gmm = []
            for c01 in range(2):
                tk_gmm.append(s.op("tensor", lambda e, c01=c01, toff=toff:
                              e.matmul(
                                  pG[64 * c01:64 * c01 + 64, 0:64],
                                  xbcc[0:16, 16 * TP + toff + 64 * c01:
                                       16 * TP + toff + 64 * c01 + 64],
                                  c0[:, toff + 64 * c01:toff + 64 * c01 + 64],
                                  start=True, stop=True,
                                  skip_group_check=True),
                              waits=[tk_c0, guard_g, guard_pg]
                                    if c01 == 0 else (),
                              inc="te"))

            tk_mq = []
            for q in range(4):
                tk_ds = s.op("tensor", lambda e, q=q, par=par: e.matmul(
                    pA[q % 2][:, 0:512], l97[par][:, :],
                    r97[par][:, 512 * q:512 * q + 512],
                    start=True, stop=True, skip_group_check=True),
                    waits=[tk_l97, tk_r97b, guard_pa[q % 2],
                           guard_m[par]] if q <= 1 else [guard_pa[q % 2]],
                    inc="te")
                tk_ex = s.op("scalar", lambda e, q=q, par=par: e.activation(
                    mbuf[par][:, 512 * q:512 * q + 512], pA[q % 2][:, 0:512],
                    AF.Exp, bias=0.0, scale=1.0), waits=[tk_ds], inc="se")
                guard_pa[q % 2] = tk_ex
                in_g = bass.AP(pG, 0, [[512, 128], [0, 8], [1, 64]])
                tk_m = s.op("vector", lambda e, q=q, par=par: e.tensor_tensor(
                    mbuf[par][:, 512 * q:512 * q + 512],
                    mbuf[par][:, 512 * q:512 * q + 512], in_g, ALU.mult),
                    waits=[tk_ex, tk_gmm[-1]], inc="ve")
                tk_mq.append(tk_m)

            for c01 in range(2):
                g = g0 + c01
                cl = 2 * p4 + c01
                tk_st = None
                for h in range(32):
                    hq, hh = h // 8, h % 8
                    kf, hl = h // 2, h % 2
                    tk_st = s.op("tensor", lambda e, h=h, hq=hq, hh=hh,
                                 kf=kf, hl=hl, c01=c01, par=par: e.matmul(
                        pSt[32 * hq:32 * hq + 32, 64 * hh:64 * hh + 64],
                        bpad[par][64 * c01:64 * c01 + 64, 32 * h:32 * h + 32],
                        xdt[par][64 * c01:64 * c01 + 64,
                                 128 * kf + 64 * hl:128 * kf + 64 * hl + 64],
                        start=(h % 8 == 0), stop=(h % 8 == 7),
                        tile_position=(64 * c01, 32 * hq),
                        skip_group_check=True),
                        waits=[tk_bpad, tk_xdt[3], scan_tok, wave_tok]
                              if h == 0 else (),
                        inc="te")
                lam_ap = bass.AP(rep_all, 8 * TP + 8 * 0 + cl,
                                 [[8 * TP + 64, 128], [8, 8], [0, 64]])
                tk_s1 = s.op("vector", lambda e, g=g, lam_ap=lam_ap:
                             e.tensor_tensor(
                                 sring[(g + 1) % 3][:, :], sring[g % 3][:, :],
                                 lam_ap, ALU.mult),
                             waits=[tk_shuf, wave_tok, scan_tok], inc="ve")
                scan_tok = s.op("vector", lambda e, g=g: e.tensor_tensor(
                    sring[(g + 1) % 3][:, :], sring[(g + 1) % 3][:, :],
                    pSt[:, :], ALU.add), waits=[tk_s1, tk_st], inc="ve")

            for w in range(4):
                tk_wv = None
                first = True
                for c01 in range(2):
                    g = g0 + c01
                    for hh4 in range(4):
                        for hl in range(2):
                            h = 8 * w + 2 * hh4 + hl
                            kf = 4 * w + hh4
                            ypart = (64 * hl, 128 * hh4 + 64 * c01)
                            wts = [guard_py[w % 2], tk_mq[w], tk_ctil,
                                   scan_tok] if first else ()
                            first_mm = (c01 == 0 and hh4 == 0)
                            first = False
                            s.op("tensor", lambda e, w=w, c01=c01, hh4=hh4,
                                 hl=hl, h=h, kf=kf, par=par, ypart=ypart,
                                 first_mm=first_mm: e.matmul(
                                     pY[w % 2][ypart[0]:ypart[0] + 64,
                                               ypart[1]:ypart[1] + 64],
                                     xdt[par][64 * c01:64 * c01 + 64,
                                              128 * kf + 64 * hl:
                                              128 * kf + 64 * hl + 64],
                                     mbuf[par][64 * c01:64 * c01 + 64,
                                               64 * h:64 * h + 64],
                                     start=first_mm, stop=False,
                                     tile_position=(64 * c01, ypart[0]),
                                     skip_group_check=True),
                                 waits=wts, inc=None)
                        for hl in range(2):
                            h = 8 * w + 2 * hh4 + hl
                            hq, hhi = h // 8, h % 8
                            ypart = (64 * hl, 128 * hh4 + 64 * c01)
                            tk_wv = s.op("tensor", lambda e, w=w, c01=c01,
                                         hh4=hh4, h=h, hq=hq, hhi=hhi, p4=p4,
                                         ypart=ypart, g=g: e.matmul(
                                pY[w % 2][ypart[0]:ypart[0] + 64,
                                          ypart[1]:ypart[1] + 64],
                                sring[g % 3][32 * hq:32 * hq + 16,
                                             64 * hhi:64 * hhi + 64],
                                c_til[32 * hq:32 * hq + 16,
                                      512 * hhi + 64 * (2 * p4 + c01):
                                      512 * hhi + 64 * (2 * p4 + c01) + 64],
                                start=False,
                                stop=(c01 == 1 and hh4 == 3),
                                tile_position=(32 * hq, ypart[0]),
                                skip_group_check=True),
                                inc="te")
                wave_tok = tk_wv
                dst = bass.AP(y_buf, (4 * w) * TP + toff,
                              [[16 * TP, 128], [TP, 4], [64, 2], [1, 64]])
                guard_py[w % 2] = s.op("scalar", lambda e, w=w, dst=dst:
                                       e.copy(dst, pY[w % 2].ap().rearrange(
                                           "p (a b c) -> p a b c", a=4, b=2)),
                                       waits=[tk_wv], inc="se")

            guard_bpad[0] = guard_bpad[1] = tk_st
            guard_xdt[0] = guard_xdt[1] = wave_tok
            guard_m[0] = guard_m[1] = wave_tok
            guard_l97[0] = guard_l97[1] = tk_mq[-1]
            guard_g = tk_mq[-1]

        guard_alast = pp["selmm_last"]
        guard_c0 = tk_gmm[-1]
        guard_pt = tk_xdt_g2[0]
        guard_expa = pp["place"]
        guard_stg = pp["place"]
        guard_repall = scan_tok
        guard_crep = tk_ctil
        guard_ctil = wave_tok
        guard_wave0 = wave_tok
        guard_scan0 = scan_tok
        tile_ssd = [wave_tok, scan_tok, guard_py[0], guard_py[1]]

        if debug and stages == 2:
            s.dma("sync", dbg["dbg_ybuf"][:, it * 16 * TP:(it + 1) * 16 * TP],
                  y_buf[:, :], waits=tile_ssd, inc="ddbg")
            tile_guard = ("ddbg", s.cnt["ddbg"])
            if it + 1 < NT:
                for dc in range(8):
                    emit_xprep_chunk(it + 1, dc, tile_guard)
            continue

        tk_gate = []
        tk_y2s = []
        tk_sq = None
        def emit_ysq_mm(k):
            return s.op("tensor", lambda e, k=k: e.matmul(
                pG[0:1, 0:TP], ones_col[:, :],
                mbuf[0][:, 512 * (k % 4):512 * (k % 4) + TP],
                start=(k == 0), stop=(k == NECH_Z - 1),
                skip_group_check=True),
                waits=[tk_y2s[k]] + ([guard_pg] if k == 0 else []), inc="te")
        for k in range(NECH_Z):
            ez = NECH_A + k
            wslot = k % 2
            tk_w = s.dma("sync",
                         wbufA[:, wslot * 2048:(wslot + 1) * 2048],
                         w16_in[:, ez * 2048:(ez + 1) * 2048],
                         waits=[prev_wload[wslot],
                                wsplit_in_tok[ez] if it == 0 else None],
                         inc="dw%d" % wslot)
            tk_mm = None
            nmm = 0
            for kc in range(8):
                wh = wbufA[:, wslot * 2048 + kc * 128:
                           wslot * 2048 + (kc + 1) * 128]
                wl = wbufA[:, wslot * 2048 + 1024 + kc * 128:
                           wslot * 2048 + 1024 + (kc + 1) * 128]
                for lhs, rhs in ((wh, x16h), (wh, x16l), (wl, x16h)):
                    tk_mm = s.op(
                        "tensor",
                        lambda e, kc=kc, k=k, lhs=lhs, rhs=rhs, nmm=nmm:
                        e.matmul(
                            pA[k % 2][:, :], lhs,
                            rhs[:, kc * TP:(kc + 1) * TP],
                            start=(nmm == 0), stop=(nmm == 23)),
                        waits=([tk_w, prev_mm_done[k % 2]] +
                               (tile_ssd if k == 0 else []))
                              if nmm == 0 else (),
                        inc="te" if nmm == 23 else None)
                    nmm += 1
            prev_wload[wslot] = tk_mm
            tk_sz = s.op("scalar", lambda e, k=k: e.activation(
                raw[:, 515 * (k % 2):515 * (k % 2) + TP], pA[k % 2][:, :],
                AF.Silu, bias=0.0, scale=1.0),
                waits=[tk_mm] + ([tile_ssd[2], tile_ssd[3]] if k < 2 else []),
                inc="se")
            prev_mm_done[k % 2] = tk_sz
            tk_d = s.op("vector", lambda e, k=k: e.scalar_tensor_tensor(
                y_buf[:, k * TP:(k + 1) * TP],
                xbcc[:, k * TP:(k + 1) * TP],
                dt_sbuf[:, k:k + 1],
                y_buf[:, k * TP:(k + 1) * TP], ALU.mult, ALU.add),
                waits=[tk_sz, tile_guard], inc="ve")
            tk_g2 = s.op("vector", lambda e, k=k: e.tensor_tensor(
                y_buf[:, k * TP:(k + 1) * TP],
                y_buf[:, k * TP:(k + 1) * TP],
                raw[:, 515 * (k % 2):515 * (k % 2) + TP], ALU.mult),
                waits=[tk_d], inc="ve")
            tk_gate.append(tk_g2)
            raw_guard[0] = raw_guard[1] = tk_g2
            tk_y2 = s.op("scalar", lambda e, k=k: e.activation(
                mbuf[0][:, 512 * (k % 4):512 * (k % 4) + TP],
                y_buf[:, k * TP:(k + 1) * TP], AF.Square, bias=0.0, scale=1.0),
                waits=[tk_g2, tk_sq], inc="se")
            tk_y2s.append(tk_y2)
            if k >= 2:
                tk_sq = emit_ysq_mm(k - 2)
        for k in (NECH_Z - 2, NECH_Z - 1):
            tk_sq = emit_ysq_mm(k)
        tk_rr0 = s.op("scalar", lambda e: e.activation(
            rowA[:, :], pG[0:1, 0:TP], AF.Sqrt,
            bias=cst[:, 0:1], scale=cst[:, 1:2]),
            waits=[tk_sq, guard_rows], inc="se")
        tk_rr = s.op("vector", lambda e: e.reciprocal(rowA[:, :], rowA[:, :]),
                     waits=[tk_rr0], inc="ve")
        tk_rrep = s.op("tensor", lambda e: e.matmul(
            pY[0][:, 0:TP], ones_row[:, :], rowA[:, :],
            start=True, stop=True, skip_group_check=True),
            waits=[tk_rr, wave_tok, guard_py[0]], inc="te")
        tk_yn = []
        for k in range(NECH_Z):
            tk_n = s.op("vector", lambda e, k=k: e.scalar_tensor_tensor(
                y_buf[:, k * TP:(k + 1) * TP],
                y_buf[:, k * TP:(k + 1) * TP],
                normw_c[:, k:k + 1],
                pY[0][:, 0:TP], ALU.mult, ALU.mult),
                waits=[tk_rrep, tk_gate[k]], inc="ve")
            tk_h16 = s.op("vector", lambda e, k=k: e.tensor_copy(
                y16h[:, k * TP:(k + 1) * TP], y_buf[:, k * TP:(k + 1) * TP]),
                waits=[tk_n, guard_y16, prev_dbg], inc="ve")
            tk_yn.append(s.op("vector", lambda e, k=k: e.tensor_tensor(
                y16l[:, k * TP:(k + 1) * TP], y_buf[:, k * TP:(k + 1) * TP],
                y16h[:, k * TP:(k + 1) * TP], ALU.subtract),
                waits=[tk_h16], inc="ve"))
        tk_ev_h1 = []
        tk_q2s = []
        tk_sum = None
        tk_sq2 = None
        def emit_h1_stats(j):
            a = s.op("tensor", lambda e, j=j: e.matmul(
                pG[0:1, 0:TP], ones_col[:, :], c_til[:, 512 * j:512 * j + TP],
                start=(j == 0), stop=(j == 7), skip_group_check=True),
                waits=[tk_ev_h1[j]], inc="te")
            b = s.op("tensor", lambda e, j=j: e.matmul(
                pSt[0:1, 0:TP], ones_col[:, :],
                mbuf[0][:, 512 * (j % 4):512 * (j % 4) + TP],
                start=(j == 0), stop=(j == 7), skip_group_check=True),
                waits=[tk_q2s[j]] +
                      ([scan_tok, guard_pst] if j == 0 else []), inc="te")
            return a, b
        for j in range(8):
            wos = j % 2
            wov = rep_all.ap().bitcast(F16)
            tk_wo = s.dma("sync",
                          wov[:, 4096 * wos:4096 * wos + 4096],
                          w16_out[:, j * 4096:(j + 1) * 4096],
                          waits=[guard_wo[wos],
                                 wsplit_out_tok[j] if it == 0 else None],
                          inc="dwo%d" % wos)
            tk_mm = None
            for k in range(16):
                wh = wov[:, 4096 * wos + k * 128:4096 * wos + (k + 1) * 128]
                wl = wov[:, 4096 * wos + 2048 + k * 128:
                         4096 * wos + 2048 + (k + 1) * 128]
                for li, (lhs, rhs) in enumerate(
                        ((wh, y16h), (wh, y16l), (wl, y16h))):
                    s.op("tensor", lambda e, k=k, j=j, lhs=lhs, rhs=rhs,
                         li=li: e.matmul(
                        pA[j % 2][:, :], lhs,
                        rhs[:, k * TP:(k + 1) * TP],
                        start=(k == 0 and li == 0), stop=False,
                        skip_group_check=True),
                        waits=[tk_wo, tk_yn[k], prev_mm_done[j % 2]]
                              if k == 0 and li == 0 else
                              ([tk_yn[k]] if j == 0 and li == 0 else ()),
                        inc=None)
            s.op("tensor", lambda e, j=j: e.matmul(
                pA[j % 2][:, :], ident16[:, :],
                x16h[:, j * TP:(j + 1) * TP],
                start=False, stop=False, skip_group_check=True), inc=None)
            tk_mm = s.op("tensor", lambda e, j=j: e.matmul(
                pA[j % 2][:, :], ident16[:, :],
                x16l[:, j * TP:(j + 1) * TP],
                start=False, stop=True, skip_group_check=True),
                inc="te")
            guard_wo[wos] = tk_mm
            if it + 1 < NT:
                emit_xprep_chunk(it + 1, j, tk_mm)
            tk_ev = s.op("scalar", lambda e, j=j: e.copy(
                c_til[:, 512 * j:512 * j + TP], pA[j % 2][:, :]),
                waits=[tk_mm], inc="se")
            prev_mm_done[j % 2] = tk_ev
            tk_ev_h1.append(tk_ev)
            tk_q2 = s.op("scalar", lambda e, j=j: e.activation(
                mbuf[0][:, 512 * (j % 4):512 * (j % 4) + TP],
                c_til[:, 512 * j:512 * j + TP], AF.Square, bias=0.0,
                scale=1.0),
                waits=[tk_ev, tk_sq2 if j > 0 else tk_sq], inc="se")
            tk_q2s.append(tk_q2)
            if j >= 1:
                tk_sum, tk_sq2 = emit_h1_stats(j - 1)
        if debug:
            s.dma("sync", dbg["dbg_h1pre"][:, it * 8 * TP:(it + 1) * 8 * TP],
                  c_til[:, 0:8 * TP], waits=[tk_ev_h1[-1], tk_sq2], inc="ddbg")
        tk_sum, tk_sq2 = emit_h1_stats(7)
        tk_mu = s.op("scalar", lambda e: e.activation(
            rowC[:, :], pG[0:1, 0:TP], AF.Copy, bias=0.0,
            scale=cst[:, 2:3]), waits=[tk_sum, guard_rows], inc="se")
        tk_ex2 = s.op("scalar", lambda e: e.activation(
            rowD[:, :], pSt[0:1, 0:TP], AF.Copy, bias=0.0,
            scale=cst[:, 2:3]), waits=[tk_sq2], inc="se")
        tk_mu2 = s.op("vector", lambda e: e.tensor_tensor(
            rowB[:, :], rowC[:, :], rowC[:, :], ALU.mult),
            waits=[tk_mu], inc="ve")
        tk_var = s.op("vector", lambda e: e.tensor_tensor(
            rowD[:, :], rowD[:, :], rowB[:, :], ALU.subtract),
            waits=[tk_ex2, tk_mu2], inc="ve")
        tk_r10 = s.op("scalar", lambda e: e.activation(
            rowA[:, :], rowD[:, :], AF.Sqrt, bias=cst[:, 0:1], scale=1.0),
            waits=[tk_var, tk_rrep, guard_rows], inc="se")
        tk_r1 = s.op("vector", lambda e: e.reciprocal(rowA[:, :], rowA[:, :]),
                     waits=[tk_r10], inc="ve")
        tk_mr0 = s.op("vector", lambda e: e.tensor_tensor(
            rowB[:, :], rowC[:, :], rowA[:, :], ALU.mult),
            waits=[tk_r1], inc="ve")
        tk_mr = s.op("vector", lambda e: e.tensor_scalar_mul(
            rowB[:, :], rowB[:, :], -1.0), waits=[tk_mr0], inc="ve")
        if stages >= 3 and it + 1 < NT:
            ech_done = []
            ech_map = {}
            pp = {}
            emit_A_chunk(it + 1, 0, NECH_A - 1, guard_wo[1], prev_dbg,
                         xprep["tok"])
            emit_ssd_ab()
            emit_ssd_c()
            prepped = True
        tk_rep1 = s.op("tensor", lambda e: e.matmul(
            pY[0][:, 0:TP], ones_row[:, :], rowA[:, :],
            start=True, stop=True, skip_group_check=True),
            waits=[tk_r1, tk_yn[-1], guard_py[0]], inc="te")
        tk_rep2 = s.op("tensor", lambda e: e.matmul(
            pY[1][:, 0:TP], ones_row[:, :], rowB[:, :],
            start=True, stop=True, skip_group_check=True),
            waits=[tk_mr, guard_py[1]], inc="te")
        guard_rows = tk_mr
        tk_am = None
        for j in range(8):
            tk_a1 = s.op("vector", lambda e, j=j: e.tensor_tensor(
                c_til[:, 512 * j:512 * j + TP], c_til[:, 512 * j:512 * j + TP],
                pY[0][:, 0:TP], ALU.mult),
                waits=[tk_rep1, tk_sq2,
                       ("ddbg", s.cnt["ddbg"]) if debug else None], inc="ve")
            tk_a2 = s.op("vector", lambda e, j=j: e.tensor_tensor(
                c_til[:, 512 * j:512 * j + TP], c_til[:, 512 * j:512 * j + TP],
                pY[1][:, 0:TP], ALU.add), waits=[tk_rep2, tk_a1], inc="ve")
            tk_rm = s.op("vector", lambda e, j=j: e.reduce_max(
                red_col[:, :], c_til[:, 512 * j:512 * j + TP], AX.X,
                apply_absolute_value=True), waits=[tk_a2], inc="ve")
            tk_am = s.op("vector", lambda e: e.tensor_max(
                amax_col[:, :], amax_col[:, :], red_col[:, :]),
                waits=[tk_rm], inc="ve")
            tk_ap_last = tk_a2
        tk_spill = s.dma("gpsimd",
                         h1_spill.rearrange("(c p) t -> p c t", p=128)
                         [:, :, t0:t0 + TP],
                         c_til.ap().rearrange("p (c t) -> p c t", t=TP),
                         waits=[tk_am], inc="dh1")
        tk_amax_tile = tk_am
        guard_y16 = guard_wo[1]
        guard_pg = tk_mu
        guard_pst = tk_ex2
        guard_py[0] = guard_py[1] = tk_ap_last
        guard_ctil = tk_spill
        tk_ctil_free = tk_spill
        tile_guard = guard_wo[1]

    if stages >= 4:
        tk_tr = s.op("tensor", lambda e: e.matmul(
            pT[0:1, 0:128], amax_col[:, :], ident[:, :], is_transpose=True,
            start=True, stop=True, skip_group_check=True),
            waits=[tk_amax_tile, tk_xdt[3]], inc="te")
        tk_m2 = s.op("vector", lambda e: e.reduce_max(
            xs_sb[0:1, 0:1], pT[0:1, 0:128], AX.X), waits=[tk_tr], inc="ve")
        tk_xs0 = s.op("scalar", lambda e: e.activation(
            xs_sb[0:1, 1:2], xs_sb[0:1, 0:1], AF.Copy, bias=0.0,
            scale=cst[:, 3:4]), waits=[tk_m2], inc="se")
        tk_xs = s.op("vector", lambda e: e.tensor_scalar_max(
            xs_sb[0:1, 1:2], xs_sb[0:1, 1:2], 1.0e-5), waits=[tk_xs0],
            inc="ve")
        tk_ccin = s.dma("sync", cc_in, xs_sb[0:1, 1:2], waits=[tk_xs],
                        inc="dci")
        tk_cc = s.op("gpsimd", lambda e: e.collective_compute(
            "AllReduce", ALU.max, replica_groups=[list(range(n_cores))],
            ins=[cc_in.opt()], outs=[cc_out.opt()]), waits=[tk_ccin],
            inc="ge")
        tk_ccout = s.dma("sync", xs_sb[0:1, 1:2], cc_out, waits=[tk_cc],
                         inc="dco")
        tk_inv = s.op("vector", lambda e: e.reciprocal(
            xs_sb[0:1, 2:3], xs_sb[0:1, 1:2]), waits=[tk_ccout], inc="ve")
        tk_rx = s.op("tensor", lambda e: e.matmul(
            pG[:, 0:1], ones_row[:, :], xs_sb[0:1, 1:2], start=True,
            stop=False, skip_group_check=True), waits=[tk_inv, tile_guard],
            inc="te")
        tk_ri = s.op("tensor", lambda e: e.matmul(
            pG[:, 1:2], ones_row[:, :], xs_sb[0:1, 2:3], start=False,
            stop=True, skip_group_check=True), inc="te")
        tk_xc = s.op("vector", lambda e: e.tensor_copy(
            xs_col[:, :], pG[:, 0:1]), waits=[tk_ri], inc="ve")
        tk_ic = s.op("vector", lambda e: e.tensor_copy(
            inv_col[:, :], pG[:, 1:2]), inc="ve")
        tk_sws = s.op("vector", lambda e: e.tensor_scalar(
            sws_c[:, :], ws_c[:, :], xs_col[:, 0:1], None, ALU.mult),
            waits=[tk_xc], inc="ve")
        wf8v = xbcc.ap().bitcast(I8)[:, 0:8192]
        wf16v = rep_all.ap().bitcast(F16)[:, 0:8192]
        tk_wf8 = s.dma("sync", wf8v,
                       w_ffnT.rearrange("(c p) n -> p c n", p=128),
                       waits=[tile_guard, prev_dbg], inc="dwf")
        tk_wf16 = s.op("vector", lambda e, wf16v=wf16v, wf8v=wf8v:
                       e.tensor_copy(wf16v, wf8v),
                       waits=[tk_wf8, guard_wo[0], guard_wo[1]], inc="ve")

        prev_rep = tile_guard
        prev_apply = tile_guard
        xbf16 = xbcc.ap().bitcast(F16)
        h1v_s = [y_buf[:, 0:8 * TP], y_buf[:, 8 * TP:16 * TP]]
        q16_s = [xbf16[:, 8 * TP:16 * TP], xbf16[:, 16 * TP:24 * TP]]
        h1_used = [tile_guard, tile_guard]
        ffn_last = [tile_guard, tile_guard]
        out_slice_prev = [None] * 8
        row_guard = guard_rows
        repA_guard = tile_guard
        p2_load = {}
        q_tok = {}

        def emit_p2_load(itx):
            par = itx % 2
            p2_load[itx] = s.dma(
                "sync", h1v_s[par].rearrange("p (c t) -> p c t", t=TP),
                h1_spill.rearrange("(c p) t -> p c t", p=128)
                [:, :, itx * TP:itx * TP + TP],
                waits=[("dh1", s.cnt["dh1"]), h1_used[par]],
                inc="dl1%d" % par)

        def emit_p2_quant_chunk(itx, c):
            par = itx % 2
            tk_u = s.op("scalar", lambda e, par=par, c=c: e.activation(
                q16_s[par][:, c * TP:(c + 1) * TP],
                h1v_s[par][:, c * TP:(c + 1) * TP], AF.Copy,
                bias=QSHIFT, scale=inv_col[:, 0:1]),
                waits=[p2_load[itx], tk_ic, ffn_last[par]], inc="se")
            tk_c2 = s.op("vector", lambda e, par=par, c=c: e.tensor_scalar(
                q16_s[par][:, c * TP:(c + 1) * TP],
                q16_s[par][:, c * TP:(c + 1) * TP],
                -QSHIFT, -128.0, ALU.add, ALU.max), waits=[tk_u], inc="ve")
            q_tok[(itx, c)] = s.op(
                "vector", lambda e, par=par, c=c: e.tensor_scalar_min(
                    q16_s[par][:, c * TP:(c + 1) * TP],
                    q16_s[par][:, c * TP:(c + 1) * TP], 127.0),
                waits=[tk_c2], inc="ve")

        emit_p2_load(0)
        for c in range(8):
            emit_p2_quant_chunk(0, c)
        prev_row_mu = tk_mu
        prev_row_ex2 = tk_ex2
        for it2 in range(NT):
            t0 = it2 * TP
            par = it2 % 2
            if it2 + 1 < NT:
                emit_p2_load(it2 + 1)
            tk_sum2 = None
            tk_sqq2 = None
            tk_h2 = []
            tk_q2bs = []
            stat_a_tok = [None] * 8

            def emit_ffn_stats(j2, it2=it2):
                a = s.op("tensor", lambda e, j2=j2: e.matmul(
                    pG[0:1, 0:TP], ones_col[:, :],
                    c_til[:, 512 * j2:512 * j2 + TP],
                    start=(j2 == 0), stop=(j2 == 7), skip_group_check=True),
                    waits=[tk_h2[j2]] +
                          ([prev_row_mu] if j2 == 0 else []), inc="te")
                b = s.op("tensor", lambda e, j2=j2: e.matmul(
                    pSt[0:1, 0:TP], ones_col[:, :],
                    mbuf[0][:, 512 * (j2 % 4):512 * (j2 % 4) + TP],
                    start=(j2 == 0), stop=(j2 == 7), skip_group_check=True),
                    waits=[tk_q2bs[j2]] +
                          ([prev_row_ex2] if j2 == 0 else []), inc="te")
                stat_a_tok[j2] = a
                return a, b
            for j2 in range(8):
                tk_mm = None
                for kc in range(8):
                    wts = []
                    if kc == 0:
                        wts.append(prev_mm_done[j2 % 2])
                        if it2 == 0 and j2 == 0:
                            wts += [tk_wf16, tk_sws]
                    if j2 == 0:
                        wts.append(q_tok[(it2, kc)])
                    tk_mm = s.op("tensor",
                                 lambda e, kc=kc, j2=j2, par=par: e.matmul(
                        pA[j2 % 2][:, :],
                        wf16v[:, kc * 1024 + j2 * 128:
                              kc * 1024 + j2 * 128 + 128],
                        q16_s[par][:, kc * TP:(kc + 1) * TP],
                        start=(kc == 0), stop=(kc == 7),
                        skip_group_check=True),
                        waits=wts, inc="te" if kc == 7 else None)
                tk_ev2 = s.op("scalar", lambda e, j2=j2: e.activation(
                    c_til[:, 512 * j2:512 * j2 + TP], pA[j2 % 2][:, :],
                    AF.Copy, bias=0.0,
                    scale=sws_c[:, j2:j2 + 1]),
                    waits=[tk_mm, out_slice_prev[j2]], inc="se")
                prev_mm_done[j2 % 2] = tk_ev2
                tk_r2 = s.op("vector", lambda e, j2=j2, par=par:
                             e.scalar_tensor_tensor(
                    c_til[:, 512 * j2:512 * j2 + TP],
                    c_til[:, 512 * j2:512 * j2 + TP],
                    fb_c[:, j2:j2 + 1],
                    h1v_s[par][:, j2 * TP:(j2 + 1) * TP], ALU.add, ALU.add),
                    waits=[tk_ev2], inc="ve")
                tk_h2.append(tk_r2)
                tk_q2b = s.op("scalar", lambda e, j2=j2: e.activation(
                    mbuf[0][:, 512 * (j2 % 4):512 * (j2 % 4) + TP],
                    c_til[:, 512 * j2:512 * j2 + TP], AF.Square, bias=0.0,
                    scale=1.0), waits=[tk_r2, tk_sqq2], inc="se")
                tk_q2bs.append(tk_q2b)
                if j2 >= 1:
                    tk_sum2, tk_sqq2 = emit_ffn_stats(j2 - 1)
                if it2 + 1 < NT and j2 >= 2:
                    emit_p2_quant_chunk(it2 + 1, j2 - 2)
                if j2 == 7:
                    ffn_last[par] = tk_mm
            if it2 + 1 < NT:
                emit_p2_quant_chunk(it2 + 1, 6)
                emit_p2_quant_chunk(it2 + 1, 7)
            tk_sum2, tk_sqq2 = emit_ffn_stats(7)
            h1_used[par] = tk_h2[-1]
            tk_mu = s.op("scalar", lambda e: e.activation(
                rowC[:, :], pG[0:1, 0:TP], AF.Copy, bias=0.0,
                scale=cst[:, 2:3]), waits=[tk_sum2, row_guard], inc="se")
            tk_ex2 = s.op("scalar", lambda e: e.activation(
                rowD[:, :], pSt[0:1, 0:TP], AF.Copy, bias=0.0,
                scale=cst[:, 2:3]), waits=[tk_sqq2], inc="se")
            tk_mu2 = s.op("vector", lambda e: e.tensor_tensor(
                rowB[:, :], rowC[:, :], rowC[:, :], ALU.mult),
                waits=[tk_mu], inc="ve")
            tk_var = s.op("vector", lambda e: e.tensor_tensor(
                rowD[:, :], rowD[:, :], rowB[:, :], ALU.subtract),
                waits=[tk_ex2, tk_mu2], inc="ve")
            tk_r10 = s.op("scalar", lambda e: e.activation(
                rowA[:, :], rowD[:, :], AF.Sqrt, bias=cst[:, 0:1],
                scale=1.0), waits=[tk_var, repA_guard], inc="se")
            tk_r1 = s.op("vector", lambda e: e.reciprocal(
                rowA[:, :], rowA[:, :]), waits=[tk_r10], inc="ve")
            tk_mr0 = s.op("vector", lambda e: e.tensor_tensor(
                rowB[:, :], rowC[:, :], rowA[:, :], ALU.mult),
                waits=[tk_r1], inc="ve")
            tk_mr = s.op("vector", lambda e: e.tensor_scalar_mul(
                rowB[:, :], rowB[:, :], -1.0), waits=[tk_mr0], inc="ve")
            tk_rep1 = s.op("tensor", lambda e: e.matmul(
                pY[0][:, 0:TP], ones_row[:, :], rowA[:, :], start=True,
                stop=True, skip_group_check=True),
                waits=[tk_r1, prev_apply], inc="te")
            tk_rep2 = s.op("tensor", lambda e: e.matmul(
                pY[1][:, 0:TP], ones_row[:, :], rowB[:, :], start=True,
                stop=True, skip_group_check=True), waits=[tk_mr], inc="te")
            prev_row_mu = tk_mu
            prev_row_ex2 = tk_ex2
            row_guard = tk_mr
            repA_guard = tk_rep1
            tk_ap = None
            for j2 in range(8):
                tk_b1 = s.op("vector", lambda e, j2=j2: e.tensor_tensor(
                    c_til[:, 512 * j2:512 * j2 + TP],
                    c_til[:, 512 * j2:512 * j2 + TP], pY[0][:, 0:TP],
                    ALU.mult),
                    waits=[tk_rep1, stat_a_tok[j2], tk_q2bs[j2]], inc="ve")
                tk_ap = s.op("vector", lambda e, j2=j2: e.tensor_tensor(
                    c_til[:, 512 * j2:512 * j2 + TP],
                    c_til[:, 512 * j2:512 * j2 + TP], pY[1][:, 0:TP],
                    ALU.add), waits=[tk_rep2, tk_b1], inc="ve")
                out_slice_prev[j2] = s.dma(
                    "gpsimd",
                    out.rearrange("(c p) t -> p c t", p=128)
                    [:, j2:j2 + 1, t0:t0 + TP],
                    c_til.ap().rearrange("p (c t) -> p c t", t=TP)
                    [:, j2:j2 + 1, :],
                    waits=[tk_ap], inc="dout%d" % j2)
            prev_apply = tk_ap
        s.op("sync", lambda e: e.nop(),
             waits=[("dout%d" % j2, s.cnt["dout%d" % j2]) for j2 in range(8)])

    if debug:
        s.op("sync", lambda e: e.nop(), waits=[("ddbg", s.cnt["ddbg"])])
    if stages <= 1 and not debug:
        tk = s.dma("sync", out[0:128, 0:TP], xbcc[:, 0:TP], waits=ech_done,
                   inc="dout")
        s.op("sync", lambda e: e.nop(), waits=[tk])

    s.emit()
    return nc, ["out"] + dbg_names


def kernel(**inputs):
    shards = host_prepare(inputs)
    nc, outs = build_program(T=8192, TP=512)
    res = run_bass_kernel_spmd(nc, shards, core_ids=list(range(8)))
    return np.stack([r["out"].T for r in res.results], axis=0)


# version 19
# speedup vs baseline: 1.0722x; 1.0602x over previous
import numpy as np
import concourse.bass as bass
import concourse.mybir as mybir
from concourse.bass_utils import run_bass_kernel_spmd

F32 = mybir.dt.float32
F16 = mybir.dt.float16
I8 = mybir.dt.int8
AF = mybir.ActivationFunctionType
ALU = mybir.AluOpType
AX = mybir.AxisListType

D_MODEL = 1024
D_INNER = 2048
D_STATE = 16
NHEADS = 32
HEADDIM = 64
CONV_DIM = 2080
EPROJ = 4224
NECH_A = 17
NECH_Z = 16
BIG = 6.0e4
QSHIFT = 1536.0


ENGINES = ("sync", "tensor", "vector", "scalar", "gpsimd")


class Sched:

    def __init__(self, nc):
        self.nc = nc
        self.ops = {e: [] for e in ENGINES}
        self.sems = {}
        self.cnt = {}

    def _sem(self, name):
        if name not in self.sems:
            self.sems[name] = self.nc.alloc_semaphore(name=name)
            self.cnt[name] = 0
        return self.sems[name]

    def op(self, engine, fn, waits=(), inc=None, n=1):
        tok = None
        if inc is not None:
            self._sem(inc)
            self.cnt[inc] += n
            tok = (inc, self.cnt[inc])
        for w in waits:
            if w is not None:
                self._sem(w[0])
        self.ops[engine].append((tuple(w for w in waits if w is not None),
                                 fn, tok, n))
        return tok

    def dma(self, engine, out, in_, waits=(), inc="dma"):
        return self.op(engine,
                       lambda e, o=out, i=in_: e.dma_start(out=o, in_=i),
                       waits=waits, inc=inc, n=16)

    def emit(self):
        with self.nc.Block() as block:
            def mk(engine):
                ops = self.ops[engine]
                def body(eng):
                    for waits, fn, tok, n in ops:
                        for s, v in waits:
                            eng.wait_ge(self.sems[s], v)
                        instr = fn(eng)
                        if tok is not None:
                            if instr is None:
                                raise RuntimeError("op with inc returned None")
                            instr.then_inc(self.sems[tok[0]], n)
                return body
            for e in ENGINES:
                if self.ops[e]:
                    getattr(block, e)(mk(e))

    def barrier(self, name, engines=ENGINES):
        toks = []
        for e in engines:
            toks.append(self.op(e, lambda eng: eng.nop(), inc="bar_" + name))
        for e in engines:
            self.op(e, lambda eng: eng.nop(), waits=toks)


def _r(t, *idx):
    return t[idx]


def build_consts():
    c = {}
    c["ident"] = np.eye(128, dtype=np.float32)
    sel = np.zeros((2, 128), np.float32)
    sel[0, :64] = 1.0
    sel[1, 64:] = 1.0
    c["dseg_sel"] = sel
    oh = np.zeros((63, 128), np.float32)
    for j in range(1, 64):
        oh[j - 1, j] = 1.0
        oh[j - 1, 64 + j] = 1.0
    c["dseg_mask_lhs"] = oh
    rhs_oh = np.zeros((32, 2048), np.float32)
    for h in range(32):
        rhs_oh[h, h * 64:(h + 1) * 64] = 1.0
    rhs_mask = np.zeros((63, 2048), np.float32)
    for j in range(1, 64):
        row = np.zeros(64, np.float32)
        row[:j] = -BIG
        rhs_mask[j - 1] = np.tile(row, 32)
    c["dseg_rhs_const"] = np.concatenate([rhs_oh, rhs_mask], 0)
    c["ones32_c"] = np.ones((32, 64), np.float32)
    return c


def _pad_conv(a):
    out = np.zeros((2176,) + a.shape[1:], np.float32)
    out[0:2048] = a[0:2048]
    out[2048:2064] = a[2048:2064]
    out[2080:2096] = a[2064:2080]
    return out


def host_prepare(inputs):
    x = np.ascontiguousarray(inputs["L0_attn_x"])
    W_in = np.asarray(inputs["L0_attn_W_in"])
    conv_w = np.asarray(inputs["L0_attn_conv_w"])
    conv_b = np.asarray(inputs["L0_attn_conv_b"])
    W_out = np.asarray(inputs["L0_attn_W_out"])
    w8 = np.asarray(inputs["L0_ffn_weight_int8"])

    W_inT = np.zeros((D_MODEL, EPROJ), np.float32)
    W_inT[:, 0:2048] = W_in[D_INNER:D_INNER + 2048].T
    W_inT[:, 2048:2064] = W_in[D_INNER + 2048:D_INNER + 2064].T
    W_inT[:, 2080:2096] = W_in[D_INNER + 2064:D_INNER + 2080].T
    W_inT[:, 2112:2144] = W_in[D_INNER + CONV_DIM:].T
    W_inT[:, 2176:2176 + D_INNER] = W_in[:D_INNER].T

    shard_common = {
        "w_inT": W_inT,
        "w_outT": np.ascontiguousarray(W_out.T),
        "w_ffnT": np.ascontiguousarray(w8.T),
        "conv_w": _pad_conv(conv_w),
        "conv_b": _pad_conv(conv_b.reshape(-1, 1))[:, 0],
        "dt_bias": np.asarray(inputs["L0_attn_dt_bias"], np.float32),
        "A_log": np.asarray(inputs["L0_attn_A_log"], np.float32),
        "D_vec": np.asarray(inputs["L0_attn_D"], np.float32),
        "norm_w": np.asarray(inputs["L0_attn_norm_w"], np.float32),
        "w_scale": np.asarray(inputs["L0_ffn_weight_scale"], np.float32)[:, 0],
        "f_bias": np.asarray(inputs["L0_ffn_bias"], np.float32),
        **build_consts(),
    }
    return [dict(shard_common, xT=np.ascontiguousarray(x[b].T))
            for b in range(x.shape[0])]


def build_program(T=8192, TP=512, stages=99, debug=False, n_cores=8):
    assert T % TP == 0 and TP % 128 == 0
    NT = T // TP
    NPAIR = TP // 128
    nc = bass.Bass("TRN2", target_bir_lowering=False)

    def din(name, shape, dtype=F32):
        return nc.dram_tensor(name, shape, dtype, kind="ExternalInput").ap()

    xT = din("xT", [D_MODEL, T])
    w_inT = din("w_inT", [D_MODEL, EPROJ])
    w_outT = din("w_outT", [D_INNER, D_MODEL])
    w_ffnT = din("w_ffnT", [D_MODEL, D_MODEL], I8)
    conv_w = din("conv_w", [2176, 4])
    conv_b = din("conv_b", [2176])
    dt_bias = din("dt_bias", [32])
    A_log = din("A_log", [32])
    D_vec = din("D_vec", [32])
    norm_w = din("norm_w", [D_INNER])
    w_scale = din("w_scale", [D_MODEL])
    f_bias = din("f_bias", [D_MODEL])
    ident_in = din("ident", [128, 128])
    dseg_sel = din("dseg_sel", [2, 128])
    dseg_mask_lhs = din("dseg_mask_lhs", [63, 128])
    dseg_rhs_const = din("dseg_rhs_const", [95, 2048])
    ones32_in = din("ones32_c", [32, 64])

    out = nc.dram_tensor("out", [D_MODEL, T], F32, kind="ExternalOutput").ap()
    h1_spill = nc.dram_tensor("h1_spill", [D_MODEL, T], F32).ap()
    w16_in = nc.dram_tensor("w16_in", [128, 33 * 2048], F16).ap()
    w16_out = nc.dram_tensor("w16_out", [128, 16 * 2048], F16).ap()
    cc_in = nc.dram_tensor("cc_in", [1, 1], F32).ap()
    cc_out = nc.dram_tensor("cc_out", [1, 1], F32).ap()

    dbg_names = []
    dbg = {}
    if debug:
        specs = [("dbg_xbcc", [128, NECH_A * T]), ("dbg_dt", [32, T]),
                 ("dbg_acs", [32, T]), ("dbg_ybuf", [128, 16 * T]),
                 ("dbg_h1pre", [128, 8 * T])]
        for name, shape in specs:
            dbg[name] = nc.dram_tensor(name, shape, F32,
                                       kind="ExternalOutput").ap()
            dbg_names.append(name)

    s = Sched(nc)

    sbh = {}
    def sbuf(name, shape, dtype=F32):
        sbh[name] = nc.alloc_sbuf_tensor(name, shape, dtype)
        return sbh[name]

    x16h = sbuf("x16h", [128, 8 * TP], F16)
    x16l = sbuf("x16l", [128, 8 * TP], F16)
    xstg = sbuf("xstg", [128, 2 * TP])
    wbufA = sbuf("wbufA", [128, 2048 * 2], F16)
    raw = sbuf("raw", [128, 2 * (TP + 3)])
    hist = sbuf("hist", [128, NECH_A * 3])
    xbcc = sbuf("xbcc", [128, NECH_A * TP])
    dtv = sbuf("dtv", [32, TP])
    adt = sbuf("adt", [32, TP])
    acs = sbuf("acs", [32, TP])
    acsn = sbuf("acsn", [32, TP])
    convw = sbuf("convw", [128, NECH_A * 4])
    convb = sbuf("convb", [128, NECH_A])
    dtb = sbuf("dtb", [32, 1])
    avec = sbuf("avec", [32, 1])
    ones32 = sbuf("ones32", [32, 64])
    expa_fm = sbuf("expa_fm", [32, TP])
    dt_tm = sbuf("dt_tm", [128, NPAIR * 32])
    acs_tm = sbuf("acs_tm", [128, NPAIR * 32])
    dec_tm = sbuf("dec_tm", [128, NPAIR * 32])
    alastT = sbuf("alastT", [2, NPAIR * 32])
    rep_all = sbuf("rep_all", [128, 8 * TP + 64])
    c0 = sbuf("c0", [16, TP])
    c_rep4 = sbuf("c_rep4", [128, TP])
    c_til = sbuf("c_til", [128, 8 * TP])
    _bpad = sbuf("bpad0", [128, 1024])
    _xdt = sbuf("xdt0", [128, 2048])
    _mbuf = sbuf("mbuf0", [128, 2048])
    _l97 = sbuf("l97_0", [97, 128])
    _r97 = sbuf("r97_0", [97, 2048])
    bpad, xdt, mbuf, l97, r97 = ([_bpad] * 2, [_xdt] * 2, [_mbuf] * 2,
                                 [_l97] * 2, [_r97] * 2)
    sring = [sbuf("sring0", [128, 512]), sbuf("sring1", [128, 512]),
             sbuf("sring2", [128, 512])]
    y_buf = sbuf("y_buf", [128, 16 * TP])
    wffn16 = None
    ident = sbuf("ident_sb", [128, 128])
    ident16 = sbuf("ident16", [128, 128], F16)
    y16h = None
    y16l = None
    dt_sbuf = sbuf("dt_col", [128, 16])
    dpair = sbuf("dpair", [2, 16])
    rowA = sbuf("rowA", [1, TP])
    rowB = sbuf("rowB", [1, TP])
    rowC = sbuf("rowC", [1, TP])
    rowD = sbuf("rowD", [1, TP])
    amax_col = sbuf("amax_col", [128, 1])
    red_col = sbuf("red_col", [128, 1])
    ones_col = sbuf("ones_col", [128, 1])
    ones_row = sbuf("ones_row", [1, 128])
    normw_c = sbuf("normw_c", [128, 16])
    ws_c = sbuf("ws_c", [128, 8])
    fb_c = sbuf("fb_c", [128, 8])
    xs_sb = sbuf("xs_sb", [1, 4])
    cst = sbuf("cst", [1, 4])
    xs_col = sbuf("xs_col", [128, 1])
    inv_col = sbuf("inv_col", [128, 1])
    sws_c = sbuf("sws_c", [128, 8])

    y16h = xbcc.ap().bitcast(F16)[:, 0:16 * TP]
    y16l = xbcc.ap().bitcast(F16)[:, 16 * TP:32 * TP]

    psh = {}
    def psum(name, shape, dtype=F32):
        psh[name] = nc.alloc_psum_tensor(name, shape, dtype)
        return psh[name]

    pA = [psum("pA0", [128, TP]), psum("pA1", [128, TP])]
    pT = psum("pT", [128, 512])
    pG = psum("pG", [128, 512])
    pY = [psum("pY0", [128, 512]), psum("pY1", [128, 512])]
    pSt = psum("pSt", [128, 512])
    pT2 = psum("pT2", [128, 512])

    tk_boot = []
    tk_boot.append(s.op("sync", lambda e: e.dma_start(
        out=convb[:, :], in_=conv_b.rearrange("(c p) -> p c", p=128),
        allow_slow_non_contiguous=True), inc="dboot", n=16))
    tk_boot.append(s.dma("sync", convw.ap().rearrange("p (c k) -> p c k", k=4),
                         conv_w.rearrange("(c p) k -> p c k", p=128), inc="dboot"))
    tk_boot.append(s.dma("sync", dtb[:, :], dt_bias.unsqueeze(1), inc="dboot"))
    tk_boot.append(s.dma("sync", avec[:, :], A_log.unsqueeze(1), inc="dboot"))
    tk_boot.append(s.dma("sync", ones32[:, :], ones32_in, inc="dboot"))
    tk_ident = s.dma("sync", ident[:, :], ident_in, inc="dboot")
    s.dma("sync", l97[0][0:2, :], dseg_sel, inc="dboot")
    s.dma("sync", l97[0][34:97, :], dseg_mask_lhs, inc="dboot")
    s.dma("sync", r97[0][2:97, :], dseg_rhs_const, inc="dboot")
    tk_dp = s.op("sync", lambda e: e.dma_start(
        out=dpair[:, :], in_=D_vec.rearrange("(k two) -> two k", two=2),
        allow_slow_non_contiguous=True), inc="dboot", n=16)
    s.op("sync", lambda e: e.dma_start(
        out=normw_c[:, :], in_=norm_w.rearrange("(c p) -> p c", p=128),
        allow_slow_non_contiguous=True), inc="dboot", n=16)
    s.op("sync", lambda e: e.dma_start(
        out=ws_c[:, :], in_=w_scale.rearrange("(c p) -> p c", p=128),
        allow_slow_non_contiguous=True), inc="dboot", n=16)
    s.op("sync", lambda e: e.dma_start(
        out=fb_c[:, :], in_=f_bias.rearrange("(c p) -> p c", p=128),
        allow_slow_non_contiguous=True), inc="dboot", n=16)
    tk_bootall2 = ("dboot", s.cnt["dboot"])
    t = s.op("scalar", lambda e: e.activation(avec[:, :], avec[:, :], AF.Exp,
                                              bias=0.0, scale=1.0),
             waits=[tk_bootall2], inc="se")
    t_avec = s.op("vector",
                  lambda e: e.tensor_scalar_mul(avec[:, :], avec[:, :], -1.0),
                  waits=[t], inc="ve")
    t_hist = s.op("vector", lambda e: e.memset(hist[:, :], 0.0), inc="ve")
    s.op("vector", lambda e: e.memset(xbcc[:, :], 0.0), inc="ve")
    tk_dmm = s.op("tensor", lambda e: e.matmul(
        pT[:, 0:16], l97[0][0:2, :], dpair[:, :], start=True, stop=True),
        waits=[tk_bootall2], inc="te")
    tk_dt_sb = s.op("vector", lambda e: e.tensor_copy(dt_sbuf[:, :], pT[:, 0:16]),
                    waits=[tk_dmm], inc="ve")
    for b in {id(b): b for b in bpad}.values():
        s.op("vector", lambda e, b=b: e.memset(b[:, :], 0.0), inc="ve")
    for sr in sring:
        s.op("vector", lambda e, sr=sr: e.memset(sr[:, :], 0.0), inc="ve")
    s.op("vector", lambda e: e.memset(rep_all[:, :], 0.0), inc="ve")
    s.op("vector", lambda e: e.memset(c_rep4[:, :], 0.0), inc="ve")
    s.op("vector", lambda e: e.memset(ones_col[:, :], 1.0), inc="ve")
    s.op("vector", lambda e: e.memset(ones_row[:, :], 1.0), inc="ve")
    s.op("vector", lambda e: e.memset(amax_col[:, :], 0.0), inc="ve")
    s.op("vector", lambda e: e.memset(cst[:, 0:1], 1.0e-5), inc="ve")
    s.op("vector", lambda e: e.memset(cst[:, 1:2], 1.0 / 2048.0), inc="ve")
    s.op("vector", lambda e: e.memset(cst[:, 2:3], 1.0 / 1024.0), inc="ve")
    s.op("vector", lambda e: e.memset(cst[:, 3:4], 1.0 / 127.0), inc="ve")

    s.op("vector", lambda e: e.tensor_copy(ident16[:, :], ident[:, :]),
         waits=[tk_bootall2], inc="ve")
    xprep = {"prev": None, "tok": None}
    def emit_xprep_chunk(itx, dc, xfree):
        t0x = itx * TP
        tk_xd = s.dma("sync", xstg[:, (dc % 2) * TP:(dc % 2) * TP + TP],
                      xT.rearrange("(c p) t -> p c t", p=128)
                      [:, dc, t0x:t0x + TP],
                      waits=[xprep["prev"]], inc="dx%d" % (dc % 2))
        tk_xh = s.op("vector", lambda e, dc=dc: e.tensor_copy(
            x16h[:, dc * TP:(dc + 1) * TP],
            xstg[:, (dc % 2) * TP:(dc % 2) * TP + TP]),
            waits=[tk_xd, xfree], inc="ve")
        tk_xl = s.op("vector", lambda e, dc=dc: e.tensor_tensor(
            x16l[:, dc * TP:(dc + 1) * TP],
            xstg[:, (dc % 2) * TP:(dc % 2) * TP + TP],
            x16h[:, dc * TP:(dc + 1) * TP], ALU.subtract),
            waits=[tk_xh], inc="vxs")
        xprep["prev"] = tk_xl
        xprep["tok"] = tk_xl
    for dc in range(8):
        emit_xprep_chunk(0, dc, None)
    wst = mbuf[0]
    wsp = rep_all.ap().bitcast(F16)
    wsplit_in_tok = []
    wsplit_out_tok = []
    _wsq = ([(1024, w_inT, w16_in, e8, wsplit_in_tok) for e8 in range(33)] +
            [(2048, w_outT, w16_out, e8, wsplit_out_tok) for e8 in range(8)])
    wsplit_state = {"n": 0, "lo": None, "st": [None, None]}

    def emit_wsplit_chunk():
        n = wsplit_state["n"]
        if n >= len(_wsq):
            return
        wid, wsrc, wdst, e8, toks = _wsq[n]
        sl = n % 2
        spo = sl * 4096
        tk_ld = s.dma("gpsimd", wst[:, 0:wid],
                      wsrc.rearrange("(c p) n -> p c n", p=128)
                      [:, :, e8 * 128:(e8 + 1) * 128],
                      waits=[wsplit_state["lo"]], inc="dwsp")
        tk_h = s.op("vector", lambda e, wid=wid, spo=spo: e.tensor_copy(
            wsp[:, spo:spo + wid], wst[:, 0:wid]),
            waits=[tk_ld, wsplit_state["st"][sl]], inc="ve")
        tk_l = s.op("vector", lambda e, wid=wid, spo=spo: e.tensor_tensor(
            wsp[:, spo + wid:spo + 2 * wid], wst[:, 0:wid],
            wsp[:, spo:spo + wid], ALU.subtract), waits=[tk_h], inc="vws")
        tk_st = s.dma("gpsimd", wdst[:, e8 * 2 * wid:(e8 + 1) * 2 * wid],
                      wsp[:, spo:spo + 2 * wid], waits=[tk_l],
                      inc="dwsp2%d" % sl)
        toks.append(tk_st)
        wsplit_state["lo"] = tk_l
        wsplit_state["st"][sl] = tk_st
        wsplit_state["n"] = n + 1
    t_boot_done = s.op("vector", lambda e: e.nop(),
                       waits=[tk_dt_sb], inc="ve")
    for _ in range(3):
        emit_wsplit_chunk()
    t_crep_init = t_boot_done

    raw_guard = [t_hist, t_hist]
    prev_mm_done = [None, None]
    prev_wload = [None, None]

    dbg_toks = []
    tile_guard = None
    prev_dbg = None
    guard_pt = t_boot_done
    guard_expa = None
    guard_stg = t_boot_done
    guard_alast = None
    guard_c0 = None
    guard_wo = [None, None]
    guard_y16 = None
    prev_rep = None
    prev_apply = None
    tk_ctil_free = None
    tk_spill = None
    tk_amax_tile = None
    guard_repall = None
    guard_crep = t_crep_init
    guard_ctil = None
    guard_wave0 = None
    guard_scan0 = None
    guard_pa = [None, None]
    guard_bpad = [t_boot_done, t_boot_done]
    guard_xdt = [None, None]
    guard_m = [t_boot_done, t_boot_done]
    guard_l97 = [tk_bootall2, tk_bootall2]
    guard_py = [None, None]
    guard_g = None
    guard_pg = None
    guard_pst = None
    guard_rows = None
    tk_xdt_g = t_boot_done
    tk_xdt_g2 = [t_boot_done, t_boot_done]

    for it in range(NT):
        t0 = it * TP
        tk_x = xprep["tok"]

        ech_done = []
        for e in range(NECH_A):
            if it == 0:
                emit_wsplit_chunk()
                emit_wsplit_chunk()
                if e % 2 == 1:
                    emit_wsplit_chunk()
            wslot = e % 2
            tk_w = s.dma("sync",
                         wbufA[:, wslot * 2048:(wslot + 1) * 2048],
                         w16_in[:, e * 2048:(e + 1) * 2048],
                         waits=[prev_wload[wslot],
                                wsplit_in_tok[e] if it == 0 else None],
                         inc="dw%d" % wslot)
            tk_mm = None
            nmm = 0
            for k in range(8):
                wh = wbufA[:, wslot * 2048 + k * 128:
                           wslot * 2048 + (k + 1) * 128]
                wl = wbufA[:, wslot * 2048 + 1024 + k * 128:
                           wslot * 2048 + 1024 + (k + 1) * 128]
                for lhs, rhs in ((wh, x16h), (wh, x16l), (wl, x16h)):
                    tk_mm = s.op(
                        "tensor",
                        lambda eng, k=k, e=e, lhs=lhs, rhs=rhs, nmm=nmm:
                        eng.matmul(
                            pA[e % 2][:, :], lhs,
                            rhs[:, k * TP:(k + 1) * TP],
                            start=(nmm == 0), stop=(nmm == 23)),
                        waits=([tk_w, tk_x, prev_mm_done[e % 2]]
                               if nmm == 0 else ()),
                        inc="te" if nmm == 23 else None)
                    nmm += 1
            prev_wload[wslot] = tk_mm

            rslot = e % 2
            roff = rslot * (TP + 3)
            if e < NECH_A - 1 or True:
                nrows = 128 if e < 16 else 48
                tk_h = s.op("vector",
                            lambda eng, e=e, roff=roff, nrows=nrows:
                            eng.tensor_copy(raw[0:nrows, roff:roff + 3],
                                            hist[0:nrows, e * 3:e * 3 + 3]),
                            waits=[raw_guard[rslot], tk_mm], inc="ve")
                tk_ev = s.op("scalar",
                             lambda eng, e=e, roff=roff, nrows=nrows:
                             eng.copy(raw[0:nrows, roff + 3:roff + 3 + TP],
                                      pA[e % 2][0:nrows, :]),
                             waits=[tk_mm, raw_guard[rslot], tk_h], inc="se")
                prev_mm_done[e % 2] = tk_ev if e < 16 else None
                tk_hs = s.op("vector",
                             lambda eng, e=e, roff=roff, nrows=nrows:
                             eng.tensor_copy(hist[0:nrows, e * 3:e * 3 + 3],
                                             raw[0:nrows, roff + TP:roff + TP + 3]),
                             waits=[tk_ev], inc="ve")
                ac = (0, e * TP)
                tkt = s.op("vector",
                           lambda eng, e=e, roff=roff, nrows=nrows:
                           eng.tensor_scalar(
                               xbcc[0:nrows, e * TP:e * TP + TP],
                               raw[0:nrows, roff:roff + TP],
                               convw[0:nrows, e * 4:e * 4 + 1], None, ALU.mult),
                           waits=[tk_ev, tile_guard, prev_dbg], inc="ve")
                for k in (1, 2, 3):
                    tkt = s.op("vector",
                               lambda eng, e=e, k=k, roff=roff, nrows=nrows:
                               eng.scalar_tensor_tensor(
                                   xbcc[0:nrows, e * TP:e * TP + TP],
                                   raw[0:nrows, roff + k:roff + k + TP],
                                   convw[0:nrows, e * 4 + k:e * 4 + k + 1],
                                   xbcc[0:nrows, e * TP:e * TP + TP],
                                   ALU.mult, ALU.add),
                               waits=[tkt], inc="ve")
                raw_guard[rslot] = tkt
                tk_si = s.op("scalar",
                             lambda eng, e=e, nrows=nrows: eng.activation(
                                 xbcc[0:nrows, e * TP:e * TP + TP],
                                 xbcc[0:nrows, e * TP:e * TP + TP],
                                 AF.Silu, bias=convb[0:nrows, e:e + 1],
                                 scale=1.0),
                             waits=[tkt], inc="se")
                ech_done.append(tk_si)
            if e == 16:
                tk_e1 = s.op("scalar",
                             lambda eng: eng.activation(
                                 dtv[:, :], pA[16 % 2][64:96, :], AF.Exp,
                                 bias=dtb[:, :], scale=1.0),
                             waits=[tk_mm, tile_guard, prev_dbg], inc="se")
                prev_mm_done[e % 2] = tk_e1
                tk_dt = s.op("scalar",
                             lambda eng: eng.activation(
                                 dtv[:, :], dtv[:, :], AF.Ln,
                                 bias=1.0, scale=1.0),
                             waits=[tk_e1], inc="se")
                tk_adt = s.op("vector",
                              lambda eng: eng.tensor_scalar(
                                  adt[:, :], dtv[:, :], avec[:, :], None,
                                  ALU.mult),
                              waits=[tk_dt, t_avec], inc="ve")
                tk_sc = []
                for cch in range(TP // 64):
                    tk_sc.append(s.op(
                        "vector",
                        lambda eng, cch=cch: eng.tensor_tensor_scan(
                            acs[:, cch * 64:(cch + 1) * 64],
                            ones32[:, :],
                            adt[:, cch * 64:(cch + 1) * 64],
                            0.0, ALU.mult, ALU.add),
                        waits=[tk_adt, tile_guard, prev_dbg] if cch == 0
                              else (), inc="ve"))
                tk_acsn = s.op("vector",
                               lambda eng: eng.tensor_scalar_mul(
                                   acsn[:, :], acs[:, :], -1.0),
                               waits=[tk_sc[-1]], inc="ve")
                ech_done += [tk_dt, tk_acsn]

        if it == 0:
            while wsplit_state["n"] < len(_wsq):
                emit_wsplit_chunk()
            tk_wsplit_done = s.op("vector", lambda e: e.nop(),
                                  waits=[wsplit_state["st"][0],
                                         wsplit_state["st"][1]], inc="vws")
            guard_stg = tk_wsplit_done
            guard_m[0] = guard_m[1] = tk_wsplit_done

        if debug:
            s.dma("sync", dbg["dbg_xbcc"][:, it * NECH_A * TP:(it + 1) * NECH_A * TP],
                  xbcc[:, :], waits=ech_done, inc="ddbg")
            s.dma("sync", dbg["dbg_dt"][:, t0:t0 + TP],
                  dtv[:, :], waits=ech_done, inc="ddbg")
            tk = s.dma("sync", dbg["dbg_acs"][:, t0:t0 + TP],
                       acs[:, :], waits=ech_done, inc="ddbg")
            dbg_toks = [("ddbg", s.cnt["ddbg"])]
            prev_dbg = dbg_toks[0]
            if stages == 1:
                tile_guard = dbg_toks[0]

        if stages <= 1:
            if it + 1 < NT:
                for dc in range(8):
                    emit_xprep_chunk(it + 1, dc, ech_done[-1])
            continue

        tk_dt_tok = ech_done[-2]
        tk_acsn_tok = ech_done[-1]

        tk_expa = s.op("scalar", lambda e: e.activation(
            expa_fm[:, :], acs[:, :], AF.Exp, bias=0.0, scale=1.0),
            waits=[tk_acsn_tok, guard_expa], inc="se")

        tts = []
        for p4 in range(NPAIR):
            tts.append(s.op("tensor", lambda e, p4=p4: e.matmul(
                pT[:, 32 * p4:32 * p4 + 32],
                dtv[:, 128 * p4:128 * p4 + 128], ident[0:32, 0:32],
                is_transpose=True, start=(p4 == 0), stop=False,
                skip_group_check=True),
                waits=[tk_dt_tok, tk_bootall2, guard_pt] if p4 == 0 else (),
                inc="te"))
            tts.append(s.op("tensor", lambda e, p4=p4: e.matmul(
                pT[:, 128 + 32 * p4:128 + 32 * p4 + 32],
                acs[:, 128 * p4:128 * p4 + 128], ident[0:32, 0:32],
                is_transpose=True, start=False, stop=(p4 == NPAIR - 1),
                skip_group_check=True),
                waits=[tk_acsn_tok] if p4 == 0 else (), inc="te"))
        tk_dttm = s.op("vector", lambda e: e.tensor_copy(
            dt_tm[:, :], pT[:, 0:32 * NPAIR]), waits=[tts[-1]], inc="ve")
        tk_acstm = s.op("vector", lambda e: e.tensor_copy(
            acs_tm[:, :], pT[:, 128:128 + 32 * NPAIR]), inc="ve")
        s.dma("sync", alastT[0:1, :], acs_tm[63:64, :],
              waits=[tk_acstm, guard_alast], inc="dal")
        tk_alast = s.dma("sync", alastT[1:2, :], acs_tm[127:128, :],
                         inc="dal")
        tk_selmm = []
        for p4 in range(NPAIR):
            tk_selmm.append(s.op("tensor", lambda e, p4=p4: e.matmul(
                pT[:, 256 + 32 * p4:256 + 32 * p4 + 32], l97[0][0:2, :],
                alastT[:, 32 * p4:32 * p4 + 32], start=(p4 == 0),
                stop=(p4 == NPAIR - 1), skip_group_check=True),
                waits=[tk_alast] if p4 == 0 else (), inc="te"))
        tk_dpre = s.op("vector", lambda e: e.tensor_tensor(
            dec_tm[:, :], pT[:, 256:256 + 32 * NPAIR], acs_tm[:, :],
            ALU.subtract), waits=[tk_selmm[-1]], inc="ve")
        tk_dec = s.op("scalar", lambda e: e.activation(
            dec_tm[:, :], dec_tm[:, :], AF.Exp, bias=0.0, scale=1.0),
            waits=[tk_dpre], inc="se")

        lin_toks = []
        for hq in range(4):
            dst = bass.AP(rep_all, hq * (8 * TP + 64),
                          [[8 * TP + 64, 1], [512, 8], [64, TP // 64], [1, 64]])
            lin_toks.append(s.dma("sync", dst, expa_fm[8 * hq:8 * hq + 8, :],
                                  waits=[tk_expa, guard_stg], inc="dlin"))
            dstl = bass.AP(rep_all, hq * (8 * TP + 64) + 8 * TP,
                           [[8 * TP + 64, 1], [8, 8], [1, TP // 64]])
            lin_toks.append(s.op("sync", lambda e, dstl=dstl, hq=hq:
                            e.dma_start(out=dstl,
                                        in_=expa_fm[8 * hq:8 * hq + 8,
                                                    63:TP:64],
                                        allow_slow_non_contiguous=True),
                            waits=[guard_stg], inc="dlin", n=16))
        tk_place = s.dma("sync", rep_all[32:128:32, :], rep_all[1:4, :],
                         waits=[lin_toks[-1]], inc="dplace")
        tk_shuf = s.op("vector", lambda e: e.stream_shuffle(
            rep_all[:, :], rep_all[:, :], [0] * 32),
            waits=[tk_place, guard_repall], inc="ve")

        tk_c0 = s.dma("sync", c0[:, :],
                      xbcc[32:48, 16 * TP:17 * TP],
                      waits=[ech_done[16], guard_c0], inc="dc0")
        crep_toks = []
        for hq in range(4):
            crep_toks.append(s.dma(
                "sync", c_rep4[32 * hq:32 * hq + 16, :],
                xbcc[32:48, 16 * TP:17 * TP],
                waits=[ech_done[16], guard_crep], inc="dcrep"))

        in0 = bass.AP(c_rep4, 0, [[TP, 128], [0, 8], [64, TP // 64], [1, 64]])
        tk_ctil = s.op("vector", lambda e: e.tensor_tensor(
            c_til[:, :], in0, rep_all[:, 0:8 * TP], ALU.mult),
            waits=[tk_shuf, crep_toks[-1], guard_ctil], inc="ve")

        wave_tok = guard_wave0
        scan_tok = guard_scan0
        for p4 in range(NPAIR):
            par = p4 % 2
            g0 = it * (TP // 64) + 2 * p4
            toff = 128 * p4

            tk_btr = s.op("tensor", lambda e, toff=toff: e.matmul(
                pSt[:, 0:16],
                xbcc[0:16, 16 * TP + toff:16 * TP + toff + 128],
                ident[0:16, 0:16], is_transpose=True, start=True, stop=True,
                skip_group_check=True),
                waits=[ech_done[16], guard_bpad[par], tk_dec, scan_tok,
                       guard_pst if p4 == 0 else None],
                inc="te")
            in_b = bass.AP(pSt, 0, [[512, 128], [0, 32], [1, 16]])
            in_d = bass.AP(dec_tm, 32 * p4, [[NPAIR * 32, 128], [1, 32], [0, 16]])
            out_b = bass.AP(bpad[par], 0, [[1024, 128], [32, 32], [1, 16]])
            tk_bpad = s.op("vector", lambda e, out_b=out_b, in_b=in_b,
                           in_d=in_d: e.tensor_tensor(
                out_b, in_b, in_d, ALU.mult), waits=[tk_btr], inc="ve")

            tk_l97 = s.dma("sync", l97[par][2:34, :],
                           acsn[:, toff:toff + 128],
                           waits=[guard_l97[par], tk_acsn_tok], inc="dl97")
            tk_r97a = s.dma("sync", r97[par][0:1, 0:2048],
                            acs[:, toff:toff + 64],
                            waits=[guard_l97[par]], inc="dr97")
            tk_r97b = s.dma("sync", r97[par][1:2, 0:2048],
                            acs[:, toff + 64:toff + 128], inc="dr97")

            tk_xdt = []
            for grp in range(4):
                pTx = pT if grp % 2 == 0 else pT2
                gt = []
                for kk in range(4):
                    kf = 4 * grp + kk
                    gt.append(s.op("tensor", lambda e, kf=kf, kk=kk, toff=toff,
                              pTx=pTx: e.matmul(
                                  pTx[:, 128 * kk:128 * kk + 128],
                                  xbcc[:, kf * TP + toff:kf * TP + toff + 128],
                                  ident[:, :], is_transpose=True,
                                  start=(kk == 0), stop=(kk == 3),
                                  skip_group_check=True),
                              waits=([ech_done[kf], guard_xdt[par],
                                      tk_xdt_g2[grp % 2]] +
                                     ([tk_dpre, tk_bpad] if grp == 0 else []))
                                    if kk == 0 else (),
                              inc="te"))
                in_dt = bass.AP(dt_tm, 32 * p4 + 8 * grp,
                                [[NPAIR * 32, 128], [1, 8], [0, 64]])
                tk_g = s.op("vector", lambda e, grp=grp, par=par,
                            in_dt=in_dt, pTx=pTx: e.tensor_tensor(
                    xdt[par][:, 512 * grp:512 * grp + 512], pTx[:, :],
                    in_dt, ALU.mult), waits=[gt[-1], tk_dttm], inc="ve")
                tk_xdt.append(tk_g)
                tk_xdt_g2[grp % 2] = tk_g
                tk_xdt_g = tk_g
            tk_gmm = []
            for c01 in range(2):
                tk_gmm.append(s.op("tensor", lambda e, c01=c01, toff=toff:
                              e.matmul(
                                  pG[64 * c01:64 * c01 + 64, 0:64],
                                  xbcc[0:16, 16 * TP + toff + 64 * c01:
                                       16 * TP + toff + 64 * c01 + 64],
                                  c0[:, toff + 64 * c01:toff + 64 * c01 + 64],
                                  start=True, stop=True,
                                  skip_group_check=True),
                              waits=[tk_c0, guard_g, guard_pg]
                                    if c01 == 0 else (),
                              inc="te"))

            tk_mq = []
            for q in range(4):
                tk_ds = s.op("tensor", lambda e, q=q, par=par: e.matmul(
                    pA[q % 2][:, 0:512], l97[par][:, :],
                    r97[par][:, 512 * q:512 * q + 512],
                    start=True, stop=True, skip_group_check=True),
                    waits=[tk_l97, tk_r97b, guard_pa[q % 2],
                           guard_m[par]] if q <= 1 else [guard_pa[q % 2]],
                    inc="te")
                tk_ex = s.op("scalar", lambda e, q=q, par=par: e.activation(
                    mbuf[par][:, 512 * q:512 * q + 512], pA[q % 2][:, 0:512],
                    AF.Exp, bias=0.0, scale=1.0), waits=[tk_ds], inc="se")
                guard_pa[q % 2] = tk_ex
                in_g = bass.AP(pG, 0, [[512, 128], [0, 8], [1, 64]])
                tk_m = s.op("vector", lambda e, q=q, par=par: e.tensor_tensor(
                    mbuf[par][:, 512 * q:512 * q + 512],
                    mbuf[par][:, 512 * q:512 * q + 512], in_g, ALU.mult),
                    waits=[tk_ex, tk_gmm[-1]], inc="ve")
                tk_mq.append(tk_m)

            for c01 in range(2):
                g = g0 + c01
                cl = 2 * p4 + c01
                tk_st = None
                for h in range(32):
                    hq, hh = h // 8, h % 8
                    kf, hl = h // 2, h % 2
                    tk_st = s.op("tensor", lambda e, h=h, hq=hq, hh=hh,
                                 kf=kf, hl=hl, c01=c01, par=par: e.matmul(
                        pSt[32 * hq:32 * hq + 32, 64 * hh:64 * hh + 64],
                        bpad[par][64 * c01:64 * c01 + 64, 32 * h:32 * h + 32],
                        xdt[par][64 * c01:64 * c01 + 64,
                                 128 * kf + 64 * hl:128 * kf + 64 * hl + 64],
                        start=(h % 8 == 0), stop=(h % 8 == 7),
                        tile_position=(64 * c01, 32 * hq),
                        skip_group_check=True),
                        waits=[tk_bpad, tk_xdt[3], scan_tok, wave_tok]
                              if h == 0 else (),
                        inc="te")
                lam_ap = bass.AP(rep_all, 8 * TP + 8 * 0 + cl,
                                 [[8 * TP + 64, 128], [8, 8], [0, 64]])
                tk_s1 = s.op("vector", lambda e, g=g, lam_ap=lam_ap:
                             e.tensor_tensor(
                                 sring[(g + 1) % 3][:, :], sring[g % 3][:, :],
                                 lam_ap, ALU.mult),
                             waits=[tk_shuf, wave_tok, scan_tok], inc="ve")
                scan_tok = s.op("vector", lambda e, g=g: e.tensor_tensor(
                    sring[(g + 1) % 3][:, :], sring[(g + 1) % 3][:, :],
                    pSt[:, :], ALU.add), waits=[tk_s1, tk_st], inc="ve")

            for w in range(4):
                tk_wv = None
                first = True
                for c01 in range(2):
                    g = g0 + c01
                    for hh4 in range(4):
                        for hl in range(2):
                            h = 8 * w + 2 * hh4 + hl
                            kf = 4 * w + hh4
                            ypart = (64 * hl, 128 * hh4 + 64 * c01)
                            wts = [guard_py[w % 2], tk_mq[w], tk_ctil,
                                   scan_tok] if first else ()
                            first_mm = (c01 == 0 and hh4 == 0)
                            first = False
                            s.op("tensor", lambda e, w=w, c01=c01, hh4=hh4,
                                 hl=hl, h=h, kf=kf, par=par, ypart=ypart,
                                 first_mm=first_mm: e.matmul(
                                     pY[w % 2][ypart[0]:ypart[0] + 64,
                                               ypart[1]:ypart[1] + 64],
                                     xdt[par][64 * c01:64 * c01 + 64,
                                              128 * kf + 64 * hl:
                                              128 * kf + 64 * hl + 64],
                                     mbuf[par][64 * c01:64 * c01 + 64,
                                               64 * h:64 * h + 64],
                                     start=first_mm, stop=False,
                                     tile_position=(64 * c01, ypart[0]),
                                     skip_group_check=True),
                                 waits=wts, inc=None)
                        for hl in range(2):
                            h = 8 * w + 2 * hh4 + hl
                            hq, hhi = h // 8, h % 8
                            ypart = (64 * hl, 128 * hh4 + 64 * c01)
                            tk_wv = s.op("tensor", lambda e, w=w, c01=c01,
                                         hh4=hh4, h=h, hq=hq, hhi=hhi, p4=p4,
                                         ypart=ypart, g=g: e.matmul(
                                pY[w % 2][ypart[0]:ypart[0] + 64,
                                          ypart[1]:ypart[1] + 64],
                                sring[g % 3][32 * hq:32 * hq + 16,
                                             64 * hhi:64 * hhi + 64],
                                c_til[32 * hq:32 * hq + 16,
                                      512 * hhi + 64 * (2 * p4 + c01):
                                      512 * hhi + 64 * (2 * p4 + c01) + 64],
                                start=False,
                                stop=(c01 == 1 and hh4 == 3),
                                tile_position=(32 * hq, ypart[0]),
                                skip_group_check=True),
                                inc="te")
                wave_tok = tk_wv
                dst = bass.AP(y_buf, (4 * w) * TP + toff,
                              [[16 * TP, 128], [TP, 4], [64, 2], [1, 64]])
                guard_py[w % 2] = s.op("scalar", lambda e, w=w, dst=dst:
                                       e.copy(dst, pY[w % 2].ap().rearrange(
                                           "p (a b c) -> p a b c", a=4, b=2)),
                                       waits=[tk_wv], inc="se")

            guard_bpad[0] = guard_bpad[1] = tk_st
            guard_xdt[0] = guard_xdt[1] = wave_tok
            guard_m[0] = guard_m[1] = wave_tok
            guard_l97[0] = guard_l97[1] = tk_mq[-1]
            guard_g = tk_mq[-1]

        guard_alast = tk_selmm[-1]
        guard_c0 = tk_gmm[-1]
        guard_pt = tk_xdt_g2[0]
        guard_expa = tk_place
        guard_stg = tk_place
        guard_repall = scan_tok
        guard_crep = tk_ctil
        guard_ctil = wave_tok
        guard_wave0 = wave_tok
        guard_scan0 = scan_tok
        tile_ssd = [wave_tok, scan_tok, guard_py[0], guard_py[1]]

        if debug and stages == 2:
            s.dma("sync", dbg["dbg_ybuf"][:, it * 16 * TP:(it + 1) * 16 * TP],
                  y_buf[:, :], waits=tile_ssd, inc="ddbg")
            tile_guard = ("ddbg", s.cnt["ddbg"])
            if it + 1 < NT:
                for dc in range(8):
                    emit_xprep_chunk(it + 1, dc, tile_guard)
            continue

        tk_gate = []
        tk_y2s = []
        tk_yn = []
        tk_ds = []
        tk_sq = None
        def emit_ysq_mm(k):
            return s.op("tensor", lambda e, k=k: e.matmul(
                pG[0:1, 0:TP], ones_col[:, :],
                mbuf[0][:, 512 * (k % 4):512 * (k % 4) + TP],
                start=(k == 0), stop=(k == NECH_Z - 1),
                skip_group_check=True),
                waits=[tk_y2s[k]] + ([guard_pg] if k == 0 else []), inc="te")
        for k in range(NECH_Z):
            ez = NECH_A + k
            wslot = k % 2
            tk_w = s.dma("sync",
                         wbufA[:, wslot * 2048:(wslot + 1) * 2048],
                         w16_in[:, ez * 2048:(ez + 1) * 2048],
                         waits=[prev_wload[wslot],
                                wsplit_in_tok[ez] if it == 0 else None],
                         inc="dw%d" % wslot)
            tk_mm = None
            nmm = 0
            for kc in range(8):
                wh = wbufA[:, wslot * 2048 + kc * 128:
                           wslot * 2048 + (kc + 1) * 128]
                wl = wbufA[:, wslot * 2048 + 1024 + kc * 128:
                           wslot * 2048 + 1024 + (kc + 1) * 128]
                for lhs, rhs in ((wh, x16h), (wh, x16l), (wl, x16h)):
                    tk_mm = s.op(
                        "tensor",
                        lambda e, kc=kc, k=k, lhs=lhs, rhs=rhs, nmm=nmm:
                        e.matmul(
                            pA[k % 2][:, :], lhs,
                            rhs[:, kc * TP:(kc + 1) * TP],
                            start=(nmm == 0), stop=(nmm == 23)),
                        waits=([tk_w, prev_mm_done[k % 2]] +
                               (tile_ssd if k == 0 else []))
                              if nmm == 0 else (),
                        inc="te" if nmm == 23 else None)
                    nmm += 1
            prev_wload[wslot] = tk_mm
            tk_sz = s.op("scalar", lambda e, k=k: e.activation(
                raw[:, 515 * (k % 2):515 * (k % 2) + TP], pA[k % 2][:, :],
                AF.Silu, bias=0.0, scale=1.0),
                waits=[tk_mm] + ([tile_ssd[2], tile_ssd[3]] if k < 2 else []),
                inc="se")
            prev_mm_done[k % 2] = tk_sz
            tk_d = s.op("vector", lambda e, k=k: e.scalar_tensor_tensor(
                y_buf[:, k * TP:(k + 1) * TP],
                xbcc[:, k * TP:(k + 1) * TP],
                dt_sbuf[:, k:k + 1],
                y_buf[:, k * TP:(k + 1) * TP], ALU.mult, ALU.add),
                waits=[tk_sz, tile_guard], inc="ve")
            tk_ds.append(tk_d)
            tk_g2 = s.op("vector", lambda e, k=k: e.tensor_tensor(
                y_buf[:, k * TP:(k + 1) * TP],
                y_buf[:, k * TP:(k + 1) * TP],
                raw[:, 515 * (k % 2):515 * (k % 2) + TP], ALU.mult),
                waits=[tk_d], inc="ve")
            tk_gate.append(tk_g2)
            raw_guard[0] = raw_guard[1] = tk_g2
            tk_y2 = s.op("scalar", lambda e, k=k: e.activation(
                mbuf[0][:, 512 * (k % 4):512 * (k % 4) + TP],
                y_buf[:, k * TP:(k + 1) * TP], AF.Square, bias=0.0, scale=1.0),
                waits=[tk_g2, tk_sq], inc="se")
            tk_y2s.append(tk_y2)
            if k >= 8:
                for k2 in (2 * (k - 8), 2 * (k - 8) + 1):
                    tk_n = s.op("vector", lambda e, k2=k2: e.tensor_scalar(
                        y_buf[:, k2 * TP:(k2 + 1) * TP],
                        y_buf[:, k2 * TP:(k2 + 1) * TP],
                        normw_c[:, k2:k2 + 1], None, ALU.mult),
                        waits=[tk_y2s[k2]], inc="ve")
                    tk_h16 = s.op("vector", lambda e, k2=k2: e.tensor_copy(
                        y16h[:, k2 * TP:(k2 + 1) * TP],
                        y_buf[:, k2 * TP:(k2 + 1) * TP]),
                        waits=[tk_n, tk_ds[4 + k2 // 2], guard_y16,
                               prev_dbg], inc="ve")
                    tk_yn.append(s.op("vector", lambda e, k2=k2:
                                      e.tensor_tensor(
                        y16l[:, k2 * TP:(k2 + 1) * TP],
                        y_buf[:, k2 * TP:(k2 + 1) * TP],
                        y16h[:, k2 * TP:(k2 + 1) * TP], ALU.subtract),
                        waits=[tk_h16, tk_ds[8 + k2 // 2]], inc="ve"))
            if k >= 2:
                tk_sq = emit_ysq_mm(k - 2)
        for k in (NECH_Z - 2, NECH_Z - 1):
            tk_sq = emit_ysq_mm(k)
        tk_rr0 = s.op("scalar", lambda e: e.activation(
            rowA[:, :], pG[0:1, 0:TP], AF.Sqrt,
            bias=cst[:, 0:1], scale=cst[:, 1:2]),
            waits=[tk_sq, guard_rows], inc="se")
        tk_rr = s.op("vector", lambda e: e.reciprocal(rowA[:, :], rowA[:, :]),
                     waits=[tk_rr0], inc="ve")
        tk_rrep = s.op("tensor", lambda e: e.matmul(
            pY[0][:, 0:TP], ones_row[:, :], rowA[:, :],
            start=True, stop=True, skip_group_check=True),
            waits=[tk_rr, wave_tok, guard_py[0]], inc="te")
        tk_rcp = s.op("vector", lambda e: e.tensor_copy(
            xdt[0][:, 0:TP], pY[0][:, 0:TP]), waits=[tk_rrep], inc="ve")
        tk_ev_h1 = []
        tk_q2s = []
        tk_sum = None
        tk_sq2 = None
        def emit_h1_stats(j):
            a = s.op("tensor", lambda e, j=j: e.matmul(
                pG[0:1, 0:TP], ones_col[:, :], c_til[:, 512 * j:512 * j + TP],
                start=(j == 0), stop=(j == 7), skip_group_check=True),
                waits=[tk_ev_h1[j]], inc="te")
            b = s.op("tensor", lambda e, j=j: e.matmul(
                pSt[0:1, 0:TP], ones_col[:, :],
                mbuf[0][:, 512 * (j % 4):512 * (j % 4) + TP],
                start=(j == 0), stop=(j == 7), skip_group_check=True),
                waits=[tk_q2s[j]] +
                      ([scan_tok, guard_pst] if j == 0 else []), inc="te")
            return a, b
        for j in range(8):
            wos = j % 2
            wov = rep_all.ap().bitcast(F16)
            tk_wo = s.dma("sync",
                          wov[:, 4096 * wos:4096 * wos + 4096],
                          w16_out[:, j * 4096:(j + 1) * 4096],
                          waits=[guard_wo[wos],
                                 wsplit_out_tok[j] if it == 0 else None],
                          inc="dwo%d" % wos)
            for k in range(16):
                wh = wov[:, 4096 * wos + k * 128:4096 * wos + (k + 1) * 128]
                wl = wov[:, 4096 * wos + 2048 + k * 128:
                         4096 * wos + 2048 + (k + 1) * 128]
                for li, (lhs, rhs) in enumerate(
                        ((wh, y16h), (wh, y16l), (wl, y16h))):
                    last = (k == 15 and li == 2)
                    tk_mm = s.op("tensor", lambda e, k=k, j=j, lhs=lhs,
                                 rhs=rhs, li=li, last=last: e.matmul(
                        pA[j % 2][:, :], lhs,
                        rhs[:, k * TP:(k + 1) * TP],
                        start=(k == 0 and li == 0), stop=last,
                        skip_group_check=True),
                        waits=[tk_wo, tk_yn[k], prev_mm_done[j % 2]]
                              if k == 0 and li == 0 else
                              ([tk_yn[k]] if j == 0 and li == 0 else ()),
                        inc="te" if last else None)
            guard_wo[wos] = tk_mm
            tk_e1v = s.op("vector", lambda e, j=j: e.tensor_tensor(
                c_til[:, 512 * j:512 * j + TP], pA[j % 2][:, :],
                xdt[0][:, 0:TP], ALU.mult),
                waits=[tk_mm, tk_rcp], inc="ve")
            prev_mm_done[j % 2] = tk_e1v
            tk_e2v = s.op("vector", lambda e, j=j: e.tensor_tensor(
                c_til[:, 512 * j:512 * j + TP],
                c_til[:, 512 * j:512 * j + TP],
                x16h[:, j * TP:(j + 1) * TP], ALU.add),
                waits=[tk_e1v], inc="ve")
            tk_ev = s.op("vector", lambda e, j=j: e.tensor_tensor(
                c_til[:, 512 * j:512 * j + TP],
                c_til[:, 512 * j:512 * j + TP],
                x16l[:, j * TP:(j + 1) * TP], ALU.add),
                waits=[tk_e2v], inc="ve")
            tk_ev_h1.append(tk_ev)
            if it + 1 < NT:
                emit_xprep_chunk(it + 1, j, tk_ev)
            tk_q2 = s.op("scalar", lambda e, j=j: e.activation(
                mbuf[0][:, 512 * (j % 4):512 * (j % 4) + TP],
                c_til[:, 512 * j:512 * j + TP], AF.Square, bias=0.0,
                scale=1.0),
                waits=[tk_ev, tk_sq2 if j > 0 else tk_sq], inc="se")
            tk_q2s.append(tk_q2)
            if j >= 1:
                tk_sum, tk_sq2 = emit_h1_stats(j - 1)
        if debug:
            s.dma("sync", dbg["dbg_h1pre"][:, it * 8 * TP:(it + 1) * 8 * TP],
                  c_til[:, 0:8 * TP], waits=[tk_ev_h1[-1], tk_sq2], inc="ddbg")
        tk_sum, tk_sq2 = emit_h1_stats(7)
        tk_mu = s.op("scalar", lambda e: e.activation(
            rowC[:, :], pG[0:1, 0:TP], AF.Copy, bias=0.0,
            scale=cst[:, 2:3]), waits=[tk_sum, guard_rows], inc="se")
        tk_ex2 = s.op("scalar", lambda e: e.activation(
            rowD[:, :], pSt[0:1, 0:TP], AF.Copy, bias=0.0,
            scale=cst[:, 2:3]), waits=[tk_sq2], inc="se")
        tk_mu2 = s.op("vector", lambda e: e.tensor_tensor(
            rowB[:, :], rowC[:, :], rowC[:, :], ALU.mult),
            waits=[tk_mu], inc="ve")
        tk_var = s.op("vector", lambda e: e.tensor_tensor(
            rowD[:, :], rowD[:, :], rowB[:, :], ALU.subtract),
            waits=[tk_ex2, tk_mu2], inc="ve")
        tk_r10 = s.op("scalar", lambda e: e.activation(
            rowA[:, :], rowD[:, :], AF.Sqrt, bias=cst[:, 0:1], scale=1.0),
            waits=[tk_var, tk_rrep, guard_rows], inc="se")
        tk_r1 = s.op("vector", lambda e: e.reciprocal(rowA[:, :], rowA[:, :]),
                     waits=[tk_r10], inc="ve")
        tk_mr0 = s.op("vector", lambda e: e.tensor_tensor(
            rowB[:, :], rowC[:, :], rowA[:, :], ALU.mult),
            waits=[tk_r1], inc="ve")
        tk_mr = s.op("vector", lambda e: e.tensor_scalar_mul(
            rowB[:, :], rowB[:, :], -1.0), waits=[tk_mr0], inc="ve")
        tk_rep1 = s.op("tensor", lambda e: e.matmul(
            pY[0][:, 0:TP], ones_row[:, :], rowA[:, :],
            start=True, stop=True, skip_group_check=True),
            waits=[tk_r1, tk_rcp, guard_py[0]], inc="te")
        tk_rep2 = s.op("tensor", lambda e: e.matmul(
            pY[1][:, 0:TP], ones_row[:, :], rowB[:, :],
            start=True, stop=True, skip_group_check=True),
            waits=[tk_mr, guard_py[1]], inc="te")
        guard_rows = tk_mr
        tk_am = None
        for j in range(8):
            tk_a1 = s.op("vector", lambda e, j=j: e.tensor_tensor(
                c_til[:, 512 * j:512 * j + TP], c_til[:, 512 * j:512 * j + TP],
                pY[0][:, 0:TP], ALU.mult),
                waits=[tk_rep1, tk_sq2,
                       ("ddbg", s.cnt["ddbg"]) if debug else None], inc="ve")
            tk_a2 = s.op("vector", lambda e, j=j: e.tensor_tensor(
                c_til[:, 512 * j:512 * j + TP], c_til[:, 512 * j:512 * j + TP],
                pY[1][:, 0:TP], ALU.add), waits=[tk_rep2, tk_a1], inc="ve")
            tk_rm = s.op("vector", lambda e, j=j: e.reduce_max(
                red_col[:, :], c_til[:, 512 * j:512 * j + TP], AX.X,
                apply_absolute_value=True), waits=[tk_a2], inc="ve")
            tk_am = s.op("vector", lambda e: e.tensor_max(
                amax_col[:, :], amax_col[:, :], red_col[:, :]),
                waits=[tk_rm], inc="ve")
            tk_ap_last = tk_a2
        tk_spill = s.dma("gpsimd",
                         h1_spill.rearrange("(c p) t -> p c t", p=128)
                         [:, :, t0:t0 + TP],
                         c_til.ap().rearrange("p (c t) -> p c t", t=TP),
                         waits=[tk_am], inc="dh1")
        tk_amax_tile = tk_am
        guard_y16 = guard_wo[1]
        guard_xdt[0] = tk_ev_h1[-1]
        guard_pg = tk_mu
        guard_pst = tk_ex2
        guard_py[0] = guard_py[1] = tk_ap_last
        guard_ctil = tk_spill
        tk_ctil_free = tk_spill
        tile_guard = guard_wo[1]

    if stages >= 4:
        tk_tr = s.op("tensor", lambda e: e.matmul(
            pT[0:1, 0:128], amax_col[:, :], ident[:, :], is_transpose=True,
            start=True, stop=True, skip_group_check=True),
            waits=[tk_amax_tile, tk_xdt[3]], inc="te")
        tk_m2 = s.op("vector", lambda e: e.reduce_max(
            xs_sb[0:1, 0:1], pT[0:1, 0:128], AX.X), waits=[tk_tr], inc="ve")
        tk_xs0 = s.op("scalar", lambda e: e.activation(
            xs_sb[0:1, 1:2], xs_sb[0:1, 0:1], AF.Copy, bias=0.0,
            scale=cst[:, 3:4]), waits=[tk_m2], inc="se")
        tk_xs = s.op("vector", lambda e: e.tensor_scalar_max(
            xs_sb[0:1, 1:2], xs_sb[0:1, 1:2], 1.0e-5), waits=[tk_xs0],
            inc="ve")
        tk_ccin = s.dma("sync", cc_in, xs_sb[0:1, 1:2], waits=[tk_xs],
                        inc="dci")
        tk_cc = s.op("gpsimd", lambda e: e.collective_compute(
            "AllReduce", ALU.max, replica_groups=[list(range(n_cores))],
            ins=[cc_in.opt()], outs=[cc_out.opt()]), waits=[tk_ccin],
            inc="ge")
        tk_ccout = s.dma("sync", xs_sb[0:1, 1:2], cc_out, waits=[tk_cc],
                         inc="dco")
        tk_inv = s.op("vector", lambda e: e.reciprocal(
            xs_sb[0:1, 2:3], xs_sb[0:1, 1:2]), waits=[tk_ccout], inc="ve")
        tk_rx = s.op("tensor", lambda e: e.matmul(
            pG[:, 0:1], ones_row[:, :], xs_sb[0:1, 1:2], start=True,
            stop=False, skip_group_check=True), waits=[tk_inv, tile_guard],
            inc="te")
        tk_ri = s.op("tensor", lambda e: e.matmul(
            pG[:, 1:2], ones_row[:, :], xs_sb[0:1, 2:3], start=False,
            stop=True, skip_group_check=True), inc="te")
        tk_xc = s.op("vector", lambda e: e.tensor_copy(
            xs_col[:, :], pG[:, 0:1]), waits=[tk_ri], inc="ve")
        tk_ic = s.op("vector", lambda e: e.tensor_copy(
            inv_col[:, :], pG[:, 1:2]), inc="ve")
        tk_sws = s.op("vector", lambda e: e.tensor_scalar(
            sws_c[:, :], ws_c[:, :], xs_col[:, 0:1], None, ALU.mult),
            waits=[tk_xc], inc="ve")
        wf8v = xbcc.ap().bitcast(I8)[:, 0:8192]
        wf16v = rep_all.ap().bitcast(F16)[:, 0:8192]
        tk_wf8 = s.dma("sync", wf8v,
                       w_ffnT.rearrange("(c p) n -> p c n", p=128),
                       waits=[tile_guard, prev_dbg], inc="dwf")
        tk_wf16 = s.op("vector", lambda e, wf16v=wf16v, wf8v=wf8v:
                       e.tensor_copy(wf16v, wf8v),
                       waits=[tk_wf8, guard_wo[0], guard_wo[1]], inc="ve")

        prev_rep = tile_guard
        prev_apply = tile_guard
        xbf16 = xbcc.ap().bitcast(F16)
        h1v_s = [y_buf[:, 0:8 * TP], y_buf[:, 8 * TP:16 * TP]]
        q16_s = [xbf16[:, 8 * TP:16 * TP], xbf16[:, 16 * TP:24 * TP]]
        h1_used = [tile_guard, tile_guard]
        ffn_last = [tile_guard, tile_guard]
        out_slice_prev = [None] * 8
        row_guard = guard_rows
        repA_guard = tile_guard
        p2_load = {}
        q_tok = {}

        def emit_p2_load(itx):
            par = itx % 2
            p2_load[itx] = s.dma(
                "sync", h1v_s[par].rearrange("p (c t) -> p c t", t=TP),
                h1_spill.rearrange("(c p) t -> p c t", p=128)
                [:, :, itx * TP:itx * TP + TP],
                waits=[("dh1", s.cnt["dh1"]), h1_used[par]],
                inc="dl1%d" % par)

        def emit_p2_quant_chunk(itx, c):
            par = itx % 2
            tk_u = s.op("scalar", lambda e, par=par, c=c: e.activation(
                q16_s[par][:, c * TP:(c + 1) * TP],
                h1v_s[par][:, c * TP:(c + 1) * TP], AF.Copy,
                bias=QSHIFT, scale=inv_col[:, 0:1]),
                waits=[p2_load[itx], tk_ic, ffn_last[par]], inc="se")
            tk_c2 = s.op("vector", lambda e, par=par, c=c: e.tensor_scalar(
                q16_s[par][:, c * TP:(c + 1) * TP],
                q16_s[par][:, c * TP:(c + 1) * TP],
                -QSHIFT, -128.0, ALU.add, ALU.max), waits=[tk_u], inc="ve")
            q_tok[(itx, c)] = s.op(
                "vector", lambda e, par=par, c=c: e.tensor_scalar_min(
                    q16_s[par][:, c * TP:(c + 1) * TP],
                    q16_s[par][:, c * TP:(c + 1) * TP], 127.0),
                waits=[tk_c2], inc="ve")

        emit_p2_load(0)
        for c in range(8):
            emit_p2_quant_chunk(0, c)
        prev_row_mu = tk_mu
        prev_row_ex2 = tk_ex2
        for it2 in range(NT):
            t0 = it2 * TP
            par = it2 % 2
            if it2 + 1 < NT:
                emit_p2_load(it2 + 1)
            tk_sum2 = None
            tk_sqq2 = None
            tk_h2 = []
            tk_q2bs = []
            stat_a_tok = [None] * 8

            def emit_ffn_stats(j2, it2=it2):
                a = s.op("tensor", lambda e, j2=j2: e.matmul(
                    pG[0:1, 0:TP], ones_col[:, :],
                    c_til[:, 512 * j2:512 * j2 + TP],
                    start=(j2 == 0), stop=(j2 == 7), skip_group_check=True),
                    waits=[tk_h2[j2]] +
                          ([prev_row_mu] if j2 == 0 else []), inc="te")
                b = s.op("tensor", lambda e, j2=j2: e.matmul(
                    pSt[0:1, 0:TP], ones_col[:, :],
                    mbuf[0][:, 512 * (j2 % 4):512 * (j2 % 4) + TP],
                    start=(j2 == 0), stop=(j2 == 7), skip_group_check=True),
                    waits=[tk_q2bs[j2]] +
                          ([prev_row_ex2] if j2 == 0 else []), inc="te")
                stat_a_tok[j2] = a
                return a, b
            for j2 in range(8):
                tk_mm = None
                for kc in range(8):
                    wts = []
                    if kc == 0:
                        wts.append(prev_mm_done[j2 % 2])
                        if it2 == 0 and j2 == 0:
                            wts += [tk_wf16, tk_sws]
                    if j2 == 0:
                        wts.append(q_tok[(it2, kc)])
                    tk_mm = s.op("tensor",
                                 lambda e, kc=kc, j2=j2, par=par: e.matmul(
                        pA[j2 % 2][:, :],
                        wf16v[:, kc * 1024 + j2 * 128:
                              kc * 1024 + j2 * 128 + 128],
                        q16_s[par][:, kc * TP:(kc + 1) * TP],
                        start=(kc == 0), stop=(kc == 7),
                        skip_group_check=True),
                        waits=wts, inc="te" if kc == 7 else None)
                tk_ev2 = s.op("scalar", lambda e, j2=j2: e.activation(
                    c_til[:, 512 * j2:512 * j2 + TP], pA[j2 % 2][:, :],
                    AF.Copy, bias=0.0,
                    scale=sws_c[:, j2:j2 + 1]),
                    waits=[tk_mm, out_slice_prev[j2]], inc="se")
                prev_mm_done[j2 % 2] = tk_ev2
                tk_r2 = s.op("vector", lambda e, j2=j2, par=par:
                             e.scalar_tensor_tensor(
                    c_til[:, 512 * j2:512 * j2 + TP],
                    c_til[:, 512 * j2:512 * j2 + TP],
                    fb_c[:, j2:j2 + 1],
                    h1v_s[par][:, j2 * TP:(j2 + 1) * TP], ALU.add, ALU.add),
                    waits=[tk_ev2], inc="ve")
                tk_h2.append(tk_r2)
                tk_q2b = s.op("scalar", lambda e, j2=j2: e.activation(
                    mbuf[0][:, 512 * (j2 % 4):512 * (j2 % 4) + TP],
                    c_til[:, 512 * j2:512 * j2 + TP], AF.Square, bias=0.0,
                    scale=1.0), waits=[tk_r2, tk_sqq2], inc="se")
                tk_q2bs.append(tk_q2b)
                if j2 >= 1:
                    tk_sum2, tk_sqq2 = emit_ffn_stats(j2 - 1)
                if it2 + 1 < NT and j2 >= 2:
                    emit_p2_quant_chunk(it2 + 1, j2 - 2)
                if j2 == 7:
                    ffn_last[par] = tk_mm
            if it2 + 1 < NT:
                emit_p2_quant_chunk(it2 + 1, 6)
                emit_p2_quant_chunk(it2 + 1, 7)
            tk_sum2, tk_sqq2 = emit_ffn_stats(7)
            h1_used[par] = tk_h2[-1]
            tk_mu = s.op("scalar", lambda e: e.activation(
                rowC[:, :], pG[0:1, 0:TP], AF.Copy, bias=0.0,
                scale=cst[:, 2:3]), waits=[tk_sum2, row_guard], inc="se")
            tk_ex2 = s.op("scalar", lambda e: e.activation(
                rowD[:, :], pSt[0:1, 0:TP], AF.Copy, bias=0.0,
                scale=cst[:, 2:3]), waits=[tk_sqq2], inc="se")
            tk_mu2 = s.op("vector", lambda e: e.tensor_tensor(
                rowB[:, :], rowC[:, :], rowC[:, :], ALU.mult),
                waits=[tk_mu], inc="ve")
            tk_var = s.op("vector", lambda e: e.tensor_tensor(
                rowD[:, :], rowD[:, :], rowB[:, :], ALU.subtract),
                waits=[tk_ex2, tk_mu2], inc="ve")
            tk_r10 = s.op("scalar", lambda e: e.activation(
                rowA[:, :], rowD[:, :], AF.Sqrt, bias=cst[:, 0:1],
                scale=1.0), waits=[tk_var, repA_guard], inc="se")
            tk_r1 = s.op("vector", lambda e: e.reciprocal(
                rowA[:, :], rowA[:, :]), waits=[tk_r10], inc="ve")
            tk_mr0 = s.op("vector", lambda e: e.tensor_tensor(
                rowB[:, :], rowC[:, :], rowA[:, :], ALU.mult),
                waits=[tk_r1], inc="ve")
            tk_mr = s.op("vector", lambda e: e.tensor_scalar_mul(
                rowB[:, :], rowB[:, :], -1.0), waits=[tk_mr0], inc="ve")
            tk_rep1 = s.op("tensor", lambda e: e.matmul(
                pY[0][:, 0:TP], ones_row[:, :], rowA[:, :], start=True,
                stop=True, skip_group_check=True),
                waits=[tk_r1, prev_apply], inc="te")
            tk_rep2 = s.op("tensor", lambda e: e.matmul(
                pY[1][:, 0:TP], ones_row[:, :], rowB[:, :], start=True,
                stop=True, skip_group_check=True), waits=[tk_mr], inc="te")
            prev_row_mu = tk_mu
            prev_row_ex2 = tk_ex2
            row_guard = tk_mr
            repA_guard = tk_rep1
            tk_ap = None
            for j2 in range(8):
                tk_b1 = s.op("vector", lambda e, j2=j2: e.tensor_tensor(
                    c_til[:, 512 * j2:512 * j2 + TP],
                    c_til[:, 512 * j2:512 * j2 + TP], pY[0][:, 0:TP],
                    ALU.mult),
                    waits=[tk_rep1, stat_a_tok[j2], tk_q2bs[j2]], inc="ve")
                tk_ap = s.op("vector", lambda e, j2=j2: e.tensor_tensor(
                    c_til[:, 512 * j2:512 * j2 + TP],
                    c_til[:, 512 * j2:512 * j2 + TP], pY[1][:, 0:TP],
                    ALU.add), waits=[tk_rep2, tk_b1], inc="ve")
                out_slice_prev[j2] = s.dma(
                    "gpsimd",
                    out.rearrange("(c p) t -> p c t", p=128)
                    [:, j2:j2 + 1, t0:t0 + TP],
                    c_til.ap().rearrange("p (c t) -> p c t", t=TP)
                    [:, j2:j2 + 1, :],
                    waits=[tk_ap], inc="dout%d" % j2)
            prev_apply = tk_ap
        s.op("sync", lambda e: e.nop(),
             waits=[("dout%d" % j2, s.cnt["dout%d" % j2]) for j2 in range(8)])

    if debug:
        s.op("sync", lambda e: e.nop(), waits=[("ddbg", s.cnt["ddbg"])])
    if stages <= 1 and not debug:
        tk = s.dma("sync", out[0:128, 0:TP], xbcc[:, 0:TP], waits=ech_done,
                   inc="dout")
        s.op("sync", lambda e: e.nop(), waits=[tk])

    s.emit()
    return nc, ["out"] + dbg_names


def kernel(**inputs):
    shards = host_prepare(inputs)
    nc, outs = build_program(T=8192, TP=512)
    res = run_bass_kernel_spmd(nc, shards, core_ids=list(range(8)))
    return np.stack([r["out"].T for r in res.results], axis=0)
